# version 07
# speedup vs baseline: 1.1208x; 1.0006x over previous
import sys
sys.path.insert(0, '/opt/trn_rl_repo')
import numpy as np
from contextlib import ExitStack
import concourse.bass as bass
import concourse.mybir as mybir

F32 = mybir.dt.float32
BF16 = mybir.dt.bfloat16
U32 = mybir.dt.uint32
AF = mybir.ActivationFunctionType
ALU = mybir.AluOpType
AX = mybir.AxisListType

KTH_PRECHAIN = False
D = 1024
NCH = 8
TEXT = 2304
TFULL = 4096
NE = 4096
NI = 4096
EPS = 1e-5

class MK:
    def __init__(self, nc, es, dbg=()):
        self.nc = nc
        self.es = es
        self.dbg = set(dbg)
        self.S = lambda g: es.enter_context(g)
        self.sem = {}
        self.cnt = {}
        self.streams = {k: [] for k in ("sp", "act", "dve", "pe", "gp")}
        self.dram = {}
        self.sb = {}

    def new_sem(self, name):
        self.sem[name] = self.S(self.nc.semaphore(name))
        self.cnt[name] = 0

    def init_consts(self, values):
        self.constpool = self.sbuf("constpool", [128, max(8, len(values))], F32)
        for j, v in enumerate(values):
            ap = self.constpool[:, j:j + 1]
            self.nc.const_aps.register(F32, float(v), ap)
            self.em("dve", lambda e, ap=ap, v=v: e.memset(ap, float(v)))

    def sbuf(self, name, shape, dtype):
        t = self.S(self.nc.sbuf_tensor(name, shape, dtype))
        self.sb[name] = t
        return t

    def view(self, name, base, off_f32, cols, dtype):
        if dtype == F32:
            ap = base[:, off_f32: off_f32 + cols]
        else:
            assert dtype == BF16
            ap = base[:, off_f32: off_f32 + (cols + 1) // 2].bitcast(BF16)
        self.sb[name] = ap
        return ap

    def din(self, name, shape, dtype=F32):
        ap = self.nc.dram_tensor(name, shape, dtype, kind="ExternalInput").ap()
        self.dram[name] = ap
        return ap

    def dout(self, name, shape, dtype=F32):
        ap = self.nc.dram_tensor(name, shape, dtype, kind="ExternalOutput").ap()
        self.dram[name] = ap
        return ap

    def dscratch(self, name, shape, dtype):
        ap = self.nc.dram_tensor(name, shape, dtype, kind="Internal").ap()
        self.dram[name] = ap
        return ap

    def em(self, eng, fn):
        self.streams[eng].append(fn)

    def inc(self, sname, by):
        self.cnt[sname] += by
        return self.cnt[sname]

    def dma(self, eng, out, in_, sname):
        tgt = self.inc(sname, 16)
        sem = self.sem[sname]
        self.em(eng, lambda e: e.dma_start(out=out, in_=in_).then_inc(sem, 16))
        return tgt

    def wait(self, eng, sname, tgt):
        sem = self.sem[sname]
        self.em(eng, lambda e: e.wait_ge(sem, tgt))

    def op(self, eng, fn, sname=None, by=1):
        if sname is None:
            self.em(eng, fn)
            return None
        tgt = self.inc(sname, by)
        sem = self.sem[sname]
        def g(e, fn=fn, sem=sem, by=by):
            r = fn(e)
            r.then_inc(sem, by)
        self.em(eng, g)
        return tgt

    def barrier(self, engs=("sp", "act", "dve", "pe", "gp")):
        if "q_sp" in self.sem and "sp" in engs:
            self.wait("sp", "q_sp", self.cnt["q_sp"])
        if "q_gp" in self.sem and "gp" in engs:
            self.wait("gp", "q_gp", self.cnt["q_gp"])
        if "bar" not in self.sem:
            self.new_sem("bar")
        sem = self.sem["bar"]
        n = len(engs)
        base = self.cnt["bar"]
        self.cnt["bar"] = base + n
        tgt = base + n
        for e in engs:
            self.em(e, lambda x: x.drain().then_inc(sem, 1))
        for e in engs:
            self.em(e, lambda x: x.wait_ge(sem, tgt))

    def finalize(self):
        nc = self.nc
        with nc.Block() as block:
            @block.sync
            def _(sync):
                for fn in self.streams["sp"]:
                    fn(sync)
            @block.scalar
            def _(scalar):
                for fn in self.streams["act"]:
                    fn(scalar)
            @block.vector
            def _(vector):
                for fn in self.streams["dve"]:
                    fn(vector)
            @block.tensor
            def _(tensor):
                for fn in self.streams["pe"]:
                    fn(tensor)
            @block.gpsimd
            def _(gpsimd):
                for fn in self.streams["gp"]:
                    fn(gpsimd)



def emit_layernorm_fm(mk, T, src_chunk, xln, es, name, dst_chunk=None, ps4=None):
    if dst_chunk is None:
        dst_chunk = lambda c: xln[:, c * T:(c + 1) * T]
    nc = mk.nc
    nt = (T + 511) // 512
    if ps4 is None:
        ps_sum = es.enter_context(nc.psum_tensor(name + "_pss", [1, 1024], F32))
        ps_bc = es.enter_context(nc.psum_tensor(name + "_psb", [128, 1024], F32))
        ps4 = (ps_sum[0:1, 0:512], ps_sum[0:1, 512:1024], ps_bc[:, 0:512], ps_bc[:, 512:1024])
    psum_a, psum_b, psbc_a, psbc_b = ps4
    xsq = mk.sb["scr512a"]
    xsq2 = mk.sb["scr512b"]
    atmp = mk.sb["scr512c"]
    srow = mk.sb["srow"]
    ones_col = mk.sb["ones_col"]
    ones128 = mk.sb["ones128"]
    for it in range(nt):
        t0 = it * 512
        tw = min(512, T - t0)
        pe0 = mk.cnt["c_pe"]
        for c in range(NCH):
            src = src_chunk(c)
            buf = xsq if c % 2 == 0 else xsq2
            if c >= 2:
                mk.wait("act", "c_pe", pe0 + c - 1)
            tgt_sq = mk.op("act", lambda e, src=src, t0=t0, tw=tw, buf=buf: e.activation(
                out=buf[:, 0:tw], in_=src[:, t0:t0 + tw], func=AF.Square), "c_act")
            mk.wait("pe", "c_act", tgt_sq)
            mk.op("pe", lambda e, src=src, t0=t0, tw=tw, c=c: e.matmul(
                out=psum_a[0:1, 0:tw], lhsT=ones128[:, 0:1], rhs=src[:, t0:t0 + tw],
                start=(c == 0), stop=(c == NCH - 1)))
            mk.op("pe", lambda e, tw=tw, c=c, buf=buf: e.matmul(
                out=psum_b[0:1, 0:tw], lhsT=ones128[:, 0:1], rhs=buf[:, 0:tw],
                start=(c == 0), stop=(c == NCH - 1)), "c_pe")
        tgt_sums = mk.cnt["c_pe"]
        mk.wait("dve", "c_pe", tgt_sums)
        mean = srow[0:1, 0:512]; var = srow[0:1, 512:1024]
        rstd = srow[0:1, 1024:1536]; nmr = srow[0:1, 1536:2048]
        mk.op("dve", lambda e, tw=tw, mean=mean: e.tensor_scalar(out=mean[0:1, 0:tw], in0=psum_a[0:1, 0:tw],
              scalar1=1.0 / D, scalar2=None, op0=ALU.mult))
        mk.op("dve", lambda e, tw=tw, mean=mean: e.tensor_tensor(out=atmp[0:1, 0:tw], in0=mean[0:1, 0:tw],
              in1=mean[0:1, 0:tw], op=ALU.mult))
        mk.op("dve", lambda e, tw=tw, var=var: e.tensor_scalar(out=var[0:1, 0:tw], in0=psum_b[0:1, 0:tw],
              scalar1=1.0 / D, scalar2=None, op0=ALU.mult))
        tgt_v = mk.op("dve", lambda e, tw=tw, var=var: e.tensor_tensor(out=var[0:1, 0:tw], in0=var[0:1, 0:tw],
              in1=atmp[0:1, 0:tw], op=ALU.subtract), "c_dve")
        mk.wait("act", "c_dve", tgt_v)
        mk.op("act", lambda e, tw=tw, var=var, rstd=rstd: e.activation(out=rstd[0:1, 0:tw], in_=var[0:1, 0:tw],
              func=AF.Ln, bias=EPS, scale=1.0))
        tgt_r = mk.op("act", lambda e, tw=tw, rstd=rstd: e.activation(out=rstd[0:1, 0:tw], in_=rstd[0:1, 0:tw],
              func=AF.Exp, bias=0.0, scale=-0.5), "c_act")
        mk.wait("dve", "c_act", tgt_r)
        tgt_n = mk.op("dve", lambda e, tw=tw, mean=mean, rstd=rstd, nmr=nmr: e.scalar_tensor_tensor(
            out=nmr[0:1, 0:tw], in0=mean[0:1, 0:tw], scalar=-1.0, in1=rstd[0:1, 0:tw],
            op0=ALU.mult, op1=ALU.mult), "c_dve")
        mk.wait("pe", "c_dve", tgt_n)
        mk.op("pe", lambda e, tw=tw, rstd=rstd: e.matmul(out=psbc_a[:, 0:tw], lhsT=ones_col[0:1, :],
              rhs=rstd[0:1, 0:tw], start=True, stop=True))
        tgt_bc = mk.op("pe", lambda e, tw=tw, nmr=nmr: e.matmul(out=psbc_b[:, 0:tw], lhsT=ones_col[0:1, :],
              rhs=nmr[0:1, 0:tw], start=True, stop=True), "c_pe")
        mk.wait("dve", "c_pe", tgt_bc)
        for c in range(NCH):
            src = src_chunk(c)
            mk.op("dve", lambda e, src=src, t0=t0, tw=tw: e.tensor_tensor(
                out=atmp[:, 0:tw], in0=src[:, t0:t0 + tw], in1=psbc_a[:, 0:tw], op=ALU.mult))
            dstc = dst_chunk(c)
            mk.op("dve", lambda e, dstc=dstc, t0=t0, tw=tw: e.tensor_tensor(
                out=dstc[:, t0:t0 + tw], in0=atmp[:, 0:tw],
                in1=psbc_b[:, 0:tw], op=ALU.add), "c_dve" if c == NCH - 1 else None)
        mk.wait("pe", "c_dve", mk.cnt["c_dve"])
    return mk.cnt["c_dve"]


def emit_ffn_dense(mk, T, xln, resid_chunk, wupT_hbm, wdownT_hbm, es, name):
    nc = mk.nc
    wup = mk.sb[name + "_wup"]
    wdn = mk.sb[name + "_wdn"]
    mid = mk.sb[name + "_mid"]
    ps = [es.enter_context(nc.psum_tensor(name + "_ps%d" % j, [128, 512], F32)) for j in range(8)]
    ntt = (T + 255) // 256
    for half in range(2):
        i0 = half * 2048
        if half == 1:
            mk.wait("gp", "c_dve", mk.cnt["c_dve"])
        for c in range(NCH):
            mk.dma("gp", wup[:, c * 2048:(c + 1) * 2048], wupT_hbm[c * 128:(c + 1) * 128, i0:i0 + 2048], "q_gp")
        for ic in range(16):
            mk.dma("gp", wdn[:, ic * 1024:(ic + 1) * 1024], wdownT_hbm[i0 + ic * 128: i0 + (ic + 1) * 128, :], "q_gp")
        wtgt = mk.cnt["q_gp"]
        mk.wait("pe", "q_gp", wtgt)
        for it in range(ntt):
            t0 = it * 256
            tw = min(256, T - t0)
            act0 = mk.cnt["c_act"]
            for ic in range(16):
                pm = ps[6 + ic % 2]
                if ic >= 2:
                    mk.wait("pe", "c_act", act0 + ic - 1)
                for k in range(NCH):
                    mk.op("pe", lambda e, k=k, ic=ic, t0=t0, tw=tw, pm=pm: e.matmul(
                        out=pm[:, 0:tw], lhsT=wup[:, k * 2048 + ic * 128: k * 2048 + ic * 128 + 128],
                        rhs=xln[:, k * T + t0: k * T + t0 + tw],
                        start=(k == 0), stop=(k == NCH - 1)), "c_pe" if k == NCH - 1 else None)
                mk.wait("act", "c_pe", mk.cnt["c_pe"])
                mk.op("act", lambda e, pm=pm, ic=ic, tw=tw: e.activation(
                    out=mid[:, ic * 256: ic * 256 + tw], in_=pm[:, 0:tw], func=AF.Silu), "c_act")
            mk.wait("pe", "c_act", mk.cnt["c_act"])
            for ic in range(16):
                for oc in range(NCH):
                    mk.op("pe", lambda e, oc=oc, ic=ic, t0=t0, tw=tw: e.matmul(
                        out=ps[oc][:, 0:tw], lhsT=wdn[:, ic * 1024 + oc * 128: ic * 1024 + oc * 128 + 128],
                        rhs=mid[:, ic * 256: ic * 256 + tw],
                        start=(ic == 0), stop=(ic == 15)),
                        "c_pe" if (ic == 15 and oc == NCH - 1) else None)
            tgt_out = mk.cnt["c_pe"]
            mk.wait("dve", "c_pe", tgt_out)
            for oc in range(NCH):
                dst = resid_chunk(oc)
                mk.op("dve", lambda e, oc=oc, dst=dst, t0=t0, tw=tw: e.tensor_tensor(
                    out=dst[:, t0:t0 + tw], in0=ps[oc][:, 0:tw], in1=dst[:, t0:t0 + tw], op=ALU.add),
                    "c_dve" if oc == NCH - 1 else None)
            mk.wait("pe", "c_dve", mk.cnt["c_dve"])
    return mk.cnt["c_dve"]


def emit_peer(mk, T, xln, resid_chunk, hbm, es, name, GRP=3, stages="123"):
    nc = mk.nc
    qfm = mk.sb[name + "_qfm"]
    sims = mk.sb[name + "_sims"]
    exa = mk.sb[name + "_exa"]
    eya = mk.sb[name + "_eya"]
    taua = mk.sb[name + "_tau"]
    srow = mk.sb[name + "_srow"]
    s8 = mk.sb[name + "_s8"]
    det = mk.sb[name + "_det"]
    ue2 = mk.sb[name + "_ue"]
    dsb = mk.sb[name + "_dsb"]
    dsb2 = mk.sb[name + "_dsb2"]
    pds = mk.sb[name + "_pds"]
    pd2 = mk.sb[name + "_pd2"]
    wtmp = mk.sb[name + "_wtmp"]
    wacc = mk.sb[name + "_wacc"]
    wtw = mk.sb[name + "_wtw"]
    keys = mk.sb[name + "_keys"]
    wq = mk.sb[name + "_wq"]
    ident = mk.sb["identbf"]
    nt = (T + 127) // 128
    ps_d = [es.enter_context(nc.psum_tensor(name + "_psd%d" % j, [128, 512], F32)) for j in range(4)]

    mk.dma("gp", keys[0:64, :], hbm["keysT"], "q_gp")
    mk.dma("gp", keys[64:128, :], hbm["keysT"], "q_gp")
    for c in range(NCH):
        mk.dma("gp", wq[:, c * 512:(c + 1) * 512], hbm["wqT"][c * 128:(c + 1) * 128, :], "q_gp")
    mk.wait("pe", "q_gp", mk.cnt["q_gp"])
    with ExitStack() as es1:
        ps_q = es1.enter_context(nc.psum_tensor(name + "_psq", [128, 512], F32))
        ps_s = es1.enter_context(nc.psum_tensor(name + "_pssim", [128, 512], F32))
        for it5 in range((T + 511) // 512):
            t0g = it5 * 512
            twg = min(512, T - t0g)
            for ph in range(8):
                p, h = ph // 4, ph % 4
                boff = p * 256 + h * 64
                for k in range(NCH):
                    mk.op("pe", lambda e, boff=boff, k=k, t0g=t0g, twg=twg: e.matmul(
                        out=ps_q[0:64, 0:twg], lhsT=wq[:, k * 512 + boff: k * 512 + boff + 64],
                        rhs=xln[:, k * T + t0g:k * T + t0g + twg],
                        start=(k == 0), stop=(k == NCH - 1)), "c_pe" if k == NCH - 1 else None)
                mk.wait("act", "c_pe", mk.cnt["c_pe"])
                tgt = mk.op("act", lambda e, ph=ph, twg=twg: e.activation(
                    out=qfm[0:64, ph * 512: ph * 512 + twg], in_=ps_q[0:64, 0:twg], func=AF.Copy), "c_act")
                mk.wait("pe", "c_act", tgt)
            if stages == "q":
                continue
            for itl in range((twg + 127) // 128):
                it = it5 * 4 + itl
                t0 = itl * 128
                tw = min(128, twg - t0)
                for ph in range(8):
                    mk.op("pe", lambda e, t0=t0, tw=tw, ph=ph: e.matmul(
                        out=ps_s[0:tw, ph * 64:(ph + 1) * 64], lhsT=qfm[0:64, ph * 512 + t0: ph * 512 + t0 + tw],
                        rhs=keys[0:64, ph * 64:(ph + 1) * 64], start=True, stop=True),
                        "c_pe" if ph == 7 else None)
                mk.wait("act", "c_pe", mk.cnt["c_pe"])
                mk.wait("act", "c_dve", mk.cnt["c_dve"])
                tgt = mk.op("act", lambda e, tw=tw: e.activation(out=sims[0:tw, :], in_=ps_s[0:tw, :], func=AF.Copy), "c_act")
                mk.wait("dve", "c_act", tgt)
                if stages == "s":
                    continue
                for h in range(4):
                    sx = sims[:, h * 64:(h + 1) * 64]
                    sy = sims[:, 256 + h * 64: 256 + (h + 1) * 64]
                    vx = s8[:, 0:8]; vy = s8[:, 8:16]; g8 = s8[:, 16:24]
                    negM = s8[:, 24:25]; z = s8[:, 25:26]; negK = s8[:, 26:27]
                    ex8 = s8[:, 32:40]; ey8 = s8[:, 40:48]; p8 = s8[:, 48:56]
                    mk.op("dve", lambda e, sx=sx, vx=vx: e.max(out=vx, in_=sx))
                    mk.op("dve", lambda e, sy=sy, vy=vy: e.max(out=vy, in_=sy))
                    mk.op("dve", lambda e: e.drain())
                    mk.op("dve", lambda e, vx=vx, vy=vy: e.scalar_tensor_tensor(
                        out=srow[:, 0:64], in0=vx.rearrange("p (a o) -> p a o", o=1).broadcast_to([128, 8, 8]),
                        scalar=1.0, in1=vy.rearrange("p (o a) -> p o a", o=1).broadcast_to([128, 8, 8]),
                        op0=ALU.mult, op1=ALU.add))
                    mk.op("dve", lambda e, g8=g8: e.max(out=g8, in_=srow[:, 0:64]))
                    mk.op("dve", lambda e: e.drain())
                    mk.op("dve", lambda e, g8=g8, negM=negM: e.tensor_scalar(
                        out=negM, in0=g8[:, 0:1], scalar1=-1.0, scalar2=None, op0=ALU.mult), "c_dve")
                    mk.wait("act", "c_dve", mk.cnt["c_dve"])
                    tgt0 = mk.op("act", lambda e, g8=g8, negM=negM: e.activation(
                        out=srow[:, 0:8], in_=g8, func=AF.Exp, bias=negM, scale=1.0), "c_act")
                    mk.wait("dve", "c_act", tgt0)
                    tgtz = mk.op("dve", lambda e, z=z: e.reduce_sum(
                        out=z, in_=srow[:, 0:8], axis=AX.X), "c_dve")
                    mk.wait("act", "c_dve", tgtz)
                    tgt1 = mk.op("act", lambda e, z=z: e.activation(out=srow[:, 8:9], in_=z, func=AF.Ln), "c_act")
                    mk.wait("dve", "c_act", tgt1)
                    tgt2 = mk.op("dve", lambda e, negM=negM, negK=negK: e.tensor_tensor(
                        out=negK, in0=negM, in1=srow[:, 8:9], op=ALU.subtract), "c_dve")
                    mk.wait("act", "c_dve", tgt2)
                    exs = exa[:, it * 256 + h * 64: it * 256 + (h + 1) * 64]
                    eys = eya[:, it * 256 + h * 64: it * 256 + (h + 1) * 64]
                    mk.op("act", lambda e, sx=sx, negK=negK, exs=exs: e.activation(
                        out=exs, in_=sx, func=AF.Exp, bias=negK, scale=1.0), "c_act")
                    tgt3 = mk.op("act", lambda e, sy=sy, eys=eys: e.activation(
                        out=eys, in_=sy, func=AF.Exp), "c_act")
                    mk.wait("dve", "c_act", tgt3)
                    mk.op("dve", lambda e, exs=exs, ex8=ex8: e.max(out=ex8, in_=exs))
                    mk.op("dve", lambda e, eys=eys, ey8=ey8: e.max(out=ey8, in_=eys))
                    mk.op("dve", lambda e: e.drain())
                    mk.op("dve", lambda e, ex8=ex8, ey8=ey8: e.scalar_tensor_tensor(
                        out=srow[:, 0:64], in0=ex8.rearrange("p (a o) -> p a o", o=1).broadcast_to([128, 8, 8]),
                        scalar=1.0, in1=ey8.rearrange("p (o a) -> p o a", o=1).broadcast_to([128, 8, 8]),
                        op0=ALU.mult, op1=ALU.mult))
                    mk.op("dve", lambda e, p8=p8: e.max(out=p8, in_=srow[:, 0:64]))
                    mk.op("dve", lambda e: e.drain())
                    mk.op("dve", lambda e, p8=p8, it=it, h=h: e.tensor_copy(
                        out=taua[:, it * 4 + h: it * 4 + h + 1], in_=p8[:, 7:8]), "c_dve")
                mk.wait("pe", "c_act", tgt)
    if stages in ("q", "s") or "2" not in stages:
        return mk.cnt["c_dve"]
    ngr = (nt + GRP - 1) // GRP
    pd2b = pd2.bitcast(BF16)
    sp2 = [pd2b[:, 0:1024], pd2b[:, 1024:2048]]
    evtag = {}
    mbtag = {}
    for g in range(ngr):
        j0 = g * GRP
        jn = min(GRP, nt - j0)
        gw = sum(min(128, T - (j0 + j) * 128) for j in range(jn))
        with ExitStack() as es2:
            ps_tr = es2.enter_context(nc.psum_tensor(name + "_ptr%d" % g, [128, 1024], BF16))
            wtv = wtw[:, 0:32 * gw].rearrange("p (a b) -> p a b", a=32)
            for eq in range(4):
                e0 = eq * 1024
                mk.wait("gp", "c_pe", mk.cnt["c_pe"])
                for k in range(NCH):
                    mk.dma("gp", det[:, k * 1024:(k + 1) * 1024],
                           hbm["deT"][k * 128:(k + 1) * 128, e0:e0 + 1024], "q_gp")
                mk.wait("pe", "q_gp", mk.cnt["q_gp"])
                for j in range(jn):
                    it = j0 + j
                    t0 = it * 128
                    tw = min(128, T - t0)
                    par = (g * 4 * jn + eq * jn + j) % 2
                    pda, pdb = ps_d[par * 2], ps_d[par * 2 + 1]
                    dsl = dsb if par == 0 else dsb2
                    if par in evtag:
                        mk.wait("pe", "c_act", evtag[par])
                    for ns in range(2):
                        pm = pda if ns == 0 else pdb
                        for k in range(NCH):
                            mk.op("pe", lambda e, k=k, pm=pm, t0=t0, tw=tw, ns=ns: e.matmul(
                                out=pm[0:tw, :], lhsT=xln[:, k * T + t0: k * T + t0 + tw],
                                rhs=det[:, k * 1024 + ns * 512: k * 1024 + (ns + 1) * 512],
                                start=(k == 0), stop=(k == NCH - 1)),
                                "c_pe" if k == NCH - 1 else None)
                    mk.wait("act", "c_pe", mk.cnt["c_pe"])
                    if par in mbtag:
                        mk.wait("act", "c_dve", mbtag[par])
                    for ns in range(2):
                        pm = pda if ns == 0 else pdb
                        mk.op("act", lambda e, pm=pm, ns=ns, tw=tw, dsl=dsl: e.activation(
                            out=dsl[0:tw, ns * 512:(ns + 1) * 512], in_=pm[0:tw, :], func=AF.Copy),
                            "c_act" if ns == 1 else None)
                    evtag[par] = mk.cnt["c_act"]
                    tgt_d = mk.cnt["c_act"]
                    t_sil = [None] * 4
                    t_add = [None] * 4
                    for h in range(4):
                        exs = exa[:, it * 256 + h * 64 + eq * 16: it * 256 + h * 64 + eq * 16 + 16]
                        eys = eya[:, it * 256 + h * 64: it * 256 + (h + 1) * 64]
                        tau = taua[:, it * 4 + h: it * 4 + h + 1]
                        mk.op("dve", lambda e, exs=exs, eys=eys, tw=tw: e.tensor_tensor(
                            out=pds[0:tw, :],
                            in0=exs[0:tw].rearrange("p (a o) -> p a o", o=1).broadcast_to([tw, 16, 64]),
                            in1=eys[0:tw].rearrange("p (o a) -> p o a", o=1).broadcast_to([tw, 16, 64]),
                            op=ALU.mult))
                        mk.op("dve", lambda e, tau=tau, tw=tw: e.scalar_tensor_tensor(
                            out=pds[0:tw, :], in0=pds[0:tw, :], scalar=tau, in1=pds[0:tw, :],
                            op0=ALU.is_ge, op1=ALU.mult))
                        if h == 0:
                            mk.wait("dve", "c_act", tgt_d)
                        else:
                            mk.wait("dve", "c_act", t_sil[h - 1])
                        tgt_mb = mk.op("dve", lambda e, tw=tw, dsl=dsl: e.tensor_tensor(
                            out=wtmp[0:tw, :], in0=pds[0:tw, :], in1=dsl[0:tw, :], op=ALU.mult), "c_dve")
                        if h >= 1:
                            mk.wait("dve", "c_act", t_sil[h - 1])
                            t_add[h - 1] = mk.op("dve", lambda e, h=h, tw=tw: e.tensor_tensor(
                                out=wacc[0:tw, :], in0=wacc[0:tw, :], in1=sp2[(h - 1) % 2][0:tw, :],
                                op=ALU.add), "c_dve")
                        mk.wait("act", "c_dve", tgt_mb)
                        if h >= 2 and t_add[h - 2] is not None:
                            mk.wait("act", "c_dve", t_add[h - 2])
                        dst = wacc if h == 0 else sp2[h % 2]
                        t_sil[h] = mk.op("act", lambda e, dst=dst, tw=tw: e.activation(
                            out=dst[0:tw, :], in_=wtmp[0:tw, :], func=AF.Silu), "c_act")
                        if h == 3:
                            mk.wait("dve", "c_act", t_sil[3])
                            t_add[3] = mk.op("dve", lambda e, tw=tw: e.tensor_tensor(
                                out=wacc[0:tw, :], in0=wacc[0:tw, :], in1=sp2[1][0:tw, :],
                                op=ALU.add), "c_dve")
                    mbtag[par] = t_add[3]
                    mk.wait("pe", "c_dve", mk.cnt["c_dve"])
                    mk.wait("pe", "c_act", mk.cnt["c_act"])
                    gcol0 = sum(min(128, T - (j0 + jj) * 128) for jj in range(j))
                    for sl in range(8):
                        mk.op("pe", lambda e, sl=sl, tw=tw: e.transpose(
                            out=ps_tr[:, sl * 128: sl * 128 + tw], in_=wacc[:, sl * 128:(sl + 1) * 128],
                            identity=ident[:, 0:tw]), "c_pe" if sl == 7 else None)
                    mk.wait("act", "c_pe", mk.cnt["c_pe"])
                    mk.op("act", lambda e, eq=eq, gcol0=gcol0, tw=tw, wtv=wtv: e.activation(
                        out=wtv[:, eq * 8: eq * 8 + 8, gcol0:gcol0 + tw],
                        in_=ps_tr[:, 0:1024].rearrange("p (a b) -> p a b", a=8)[:, :, 0:tw],
                        func=AF.Copy), "c_act")
                    mk.wait("pe", "c_act", mk.cnt["c_act"])
        if "3" not in stages:
            continue
        with ExitStack() as es3:
            ps_o = [es3.enter_context(nc.psum_tensor(name + "_pso%d_%d" % (g, j), [128, 512], F32)) for j in range(4)]
            mk.wait("pe", "c_act", mk.cnt["c_act"])
            pe_tgts = {}
            for ec in range(32):
                buf = ue2[:, (ec % 2) * 1024:(ec % 2) * 1024 + 1024]
                if ec >= 2:
                    mk.wait("gp", "c_pe", pe_tgts[ec - 2])
                mk.dma("gp", buf, hbm["ue"][ec * 128:(ec + 1) * 128, :], "q_gp")
                mk.wait("pe", "q_gp", mk.cnt["q_gp"])
                for oc in range(8):
                    psu = ps_d[oc] if oc < 4 else ps_o[oc - 4]
                    mk.op("pe", lambda e, ec=ec, oc=oc, gw=gw, buf=buf, psu=psu: e.matmul(
                        out=psu[:, 0:gw], lhsT=buf[:, oc * 128:(oc + 1) * 128],
                        rhs=wtw[:, ec * gw:(ec + 1) * gw],
                        start=(ec == 0), stop=(ec == 31)),
                        "c_pe" if oc == 7 else None)
                pe_tgts[ec] = mk.cnt["c_pe"]
            mk.wait("dve", "c_pe", mk.cnt["c_pe"])
            for oc in range(8):
                psu = ps_d[oc] if oc < 4 else ps_o[oc - 4]
                dst = resid_chunk(oc)
                col = 0
                for j in range(jn):
                    it = j0 + j
                    t0 = it * 128
                    tw = min(128, T - t0)
                    mk.op("dve", lambda e, dst=dst, t0=t0, tw=tw, col=col, psu=psu: e.tensor_tensor(
                        out=dst[:, t0:t0 + tw], in0=psu[:, col:col + tw],
                        in1=dst[:, t0:t0 + tw], op=ALU.add), "c_dve")
                    col += tw
            mk.wait("pe", "c_dve", mk.cnt["c_dve"])
            mk.wait("gp", "c_dve", mk.cnt["c_dve"])
    return mk.cnt["c_dve"]

def emit_attnb(mk, T, TQ0, xln, resid_chunk, hbm, maskc, es, name):
    nc = mk.nc
    wbuf = mk.sb[name + "_wbuf"]
    kw = vw = qw = wbuf
    ow = wbuf
    kroll = mk.sb[name + "_kroll"]
    vroll = mk.sb[name + "_vroll"]
    qbuf = mk.sb[name + "_qbuf"]
    att = mk.sb[name + "_att"]
    exps = mk.sb[name + "_exps"]
    pbf = mk.sb[name + "_pbf"]
    pbft = mk.sb[name + "_pbft"]
    red = mk.sb[name + "_red"]
    bq = mk.sb[name + "_bq"]; bk = mk.sb[name + "_bk"]; bo = mk.sb[name + "_bo"]
    vbb = mk.sb[name + "_vbb"]
    vbr = mk.sb[name + "_vbr"]
    mkc = mk.sb[name + "_mkc"]
    ident = mk.sb["identbf"]
    ones_col = mk.sb["ones_col"]
    NT = T // 128
    NQ = (T - TQ0) // 128
    L_tgte = [0]
    L_pbft = [None, None]
    ps_sc = es.enter_context(nc.psum_tensor(name + "_psc", [128, 2048], F32))
    ps_tr = [es.enter_context(nc.psum_tensor(name + "_ptr%d" % j, [128, 512], BF16)) for j in range(2)]
    ps_at = [es.enter_context(nc.psum_tensor(name + "_pat%d" % j, [128, 512], F32)) for j in range(2)]

    def load_w(which, half=None):
        mk.wait("gp", "c_pe", mk.cnt["c_pe"])
        if half is None:
            for c in range(NCH):
                mk.dma("gp", wbuf[:, c * 1024:(c + 1) * 1024], hbm[which][c * 128:(c + 1) * 128, :], "q_gp")
        else:
            for c in range(NCH):
                mk.dma("gp", wbuf[:, c * 512:(c + 1) * 512],
                       hbm[which][c * 128:(c + 1) * 128, half * 512:(half + 1) * 512], "q_gp")
        mk.wait("pe", "q_gp", mk.cnt["q_gp"])
    mk.dma("gp", bq[:, :], hbm["qb8"], "q_gp")
    mk.dma("gp", bk[:, :], hbm["kb8"], "q_gp")
    mk.dma("gp", bo[:, :], hbm["ob8"], "q_gp")
    mk.dma("gp", vbr[:, :], hbm["vb_row"], "q_gp")
    mk.dma("gp", mkc[:, :], maskc, "q_gp")
    mk.wait("pe", "q_gp", mk.cnt["q_gp"])
    mk.op("pe", lambda e: e.matmul(out=ps_at[0][:, 0:512], lhsT=ones_col[0:1, :], rhs=vbr[0:1, 0:512],
          start=True, stop=True))
    mk.op("pe", lambda e: e.matmul(out=ps_at[1][:, 0:512], lhsT=ones_col[0:1, :], rhs=vbr[0:1, 512:1024],
          start=True, stop=True), "c_pe")
    mk.wait("act", "c_pe", mk.cnt["c_pe"])
    mk.op("act", lambda e: e.activation(out=vbb[:, 0:512], in_=ps_at[0][:, 0:512], func=AF.Copy))
    tgtv = mk.op("act", lambda e: e.activation(out=vbb[:, 512:1024], in_=ps_at[1][:, 0:512], func=AF.Copy), "c_act")
    mk.wait("pe", "c_act", tgtv)

    def proj_k(tiles):
        t0 = tiles[0] * 128
        tw = 128 * len(tiles)
        load_w("kwT")
        for oc in range(NCH):
            for k in range(NCH):
                mk.op("pe", lambda e, oc=oc, k=k, t0=t0, tw=tw: e.matmul(
                    out=ps_sc[:, 0:tw], lhsT=kw[:, k * 1024 + oc * 128: k * 1024 + oc * 128 + 128],
                    rhs=xln[:, k * T + t0: k * T + t0 + tw],
                    start=(k == 0), stop=(k == NCH - 1)), "c_pe" if k == NCH - 1 else None)
            mk.wait("act", "c_pe", mk.cnt["c_pe"])
            tgt = None
            for j, tj in enumerate(tiles):
                sl = tj % 7
                tgt = mk.op("act", lambda e, oc=oc, j=j, sl=sl: e.activation(
                    out=kroll[:, oc * 896 + sl * 128: oc * 896 + sl * 128 + 128],
                    in_=ps_sc[:, j * 128:(j + 1) * 128], func=AF.Identity,
                    bias=bk[:, oc:oc + 1], scale=1.0), "c_act")
            mk.wait("pe", "c_act", tgt)

    def proj_v(tiles):
        load_w("vwT")
        for tj in tiles:
            sl = tj % 7
            for nh in range(2):
                for k in range(NCH):
                    mk.op("pe", lambda e, tj=tj, nh=nh, k=k: e.matmul(
                        out=ps_sc[:, nh * 512:(nh + 1) * 512], lhsT=xln[:, k * T + tj * 128: k * T + tj * 128 + 128],
                        rhs=vw[:, k * 1024 + nh * 512: k * 1024 + (nh + 1) * 512],
                        start=(k == 0), stop=(k == NCH - 1)), "c_pe" if k == NCH - 1 else None)
            mk.wait("dve", "c_pe", mk.cnt["c_pe"])
            tgt = mk.op("dve", lambda e, sl=sl: e.tensor_tensor(
                out=vroll[:, sl * 1024:(sl + 1) * 1024], in0=ps_sc[:, 0:1024], in1=vbb[:, :], op=ALU.add), "c_dve")
            mk.wait("pe", "c_dve", tgt)

    proj_k([0, 1, 2])
    proj_v([0, 1, 2])
    for g in range((NQ + 3) // 4):
        nq = min(4, NQ - g * 4)
        newt = [t for t in range(g * 4 + 3, min(g * 4 + 7, NT))]
        if newt:
            proj_k(newt)
            proj_v(newt)
        q0 = TQ0 + g * 512
        qn = 128 * nq
        load_w("qwT")
        for oc in range(NCH):
            for k in range(NCH):
                mk.op("pe", lambda e, oc=oc, k=k, q0=q0, qn=qn: e.matmul(
                    out=ps_sc[:, 0:qn], lhsT=qw[:, k * 1024 + oc * 128: k * 1024 + oc * 128 + 128],
                    rhs=xln[:, k * T + q0: k * T + q0 + qn],
                    start=(k == 0), stop=(k == NCH - 1)), "c_pe" if k == NCH - 1 else None)
            mk.wait("act", "c_pe", mk.cnt["c_pe"])
            tgt = mk.op("act", lambda e, oc=oc, qn=qn: e.activation(
                out=qbuf[:, oc * 512: oc * 512 + qn], in_=ps_sc[:, 0:qn], func=AF.Identity,
                bias=bq[:, oc:oc + 1], scale=1.0), "c_act")
            mk.wait("pe", "c_act", tgt)
        for lq in range(nq):
            qt = g * 4 + lq
            mi = min(qt, 2)
            for hg in range(4):
                mk.wait("pe", "c_act", L_tgte[0])
                for hh in range(4):
                    h = hg * 4 + hh
                    ch, ro = h // 2, (h % 2) * 64
                    for c3 in range(3):
                        sl = (qt + c3) % 7
                        mk.op("pe", lambda e, hh=hh, ch=ch, ro=ro, lq=lq, sl=sl, c3=c3: e.matmul(
                            out=ps_sc[0:128, hh * 512 + c3 * 128: hh * 512 + (c3 + 1) * 128],
                            lhsT=qbuf[ro:ro + 64, ch * 512 + lq * 128: ch * 512 + lq * 128 + 128],
                            rhs=kroll[ro:ro + 64, ch * 896 + sl * 128: ch * 896 + sl * 128 + 128],
                            start=(c3 == 0), stop=False))
                    mk.op("pe", lambda e, hh=hh, mi=mi: e.matmul(
                        out=ps_sc[0:128, hh * 512: hh * 512 + 384], lhsT=ident[:, :],
                        rhs=mkc[:, mi * 384:(mi + 1) * 384], start=False, stop=True),
                        "c_pe" if hh == 3 else None)
                mk.wait("dve", "c_pe", mk.cnt["c_pe"])
                scv = ps_sc[:, 0:2048].rearrange("p (a b) -> p a b", a=4)[:, :, 0:384]
                mk.op("dve", lambda e, scv=scv: e.reduce_max(out=red[:, 0:4], in_=scv, axis=AX.X))
                mk.op("dve", lambda e: e.drain())
                tgtn = mk.op("dve", lambda e: e.tensor_scalar(
                    out=red[:, 4:8], in0=red[:, 0:4], scalar1=-0.125, scalar2=None, op0=ALU.mult), "c_dve")
                mk.wait("act", "c_dve", tgtn)
                tgte = None
                for hh in range(4):
                    tgte = mk.op("act", lambda e, hh=hh: e.activation(
                        out=exps[:, hh * 384:(hh + 1) * 384], in_=ps_sc[:, hh * 512: hh * 512 + 384],
                        func=AF.Exp, bias=red[:, 4 + hh:5 + hh], scale=0.125,
                        accum_out=red[:, 8 + hh:9 + hh]), "c_act")
                L_tgte[0] = tgte
                mk.wait("dve", "c_act", tgte)
                mk.op("dve", lambda e: e.reciprocal(out=red[:, 12:16], in_=red[:, 8:12]))
                mk.op("dve", lambda e: e.drain())
                tgtp = None
                for hh in range(4):
                    tgtp = mk.op("dve", lambda e, hh=hh: e.tensor_scalar(
                        out=pbf[:, hh * 384:(hh + 1) * 384], in0=exps[:, hh * 384:(hh + 1) * 384],
                        scalar1=red[:, 12 + hh:13 + hh], scalar2=None, op0=ALU.mult), "c_dve")
                mk.wait("pe", "c_dve", tgtp)
                for hh in range(4):
                    h = hg * 4 + hh
                    ro = (h % 2) * 64
                    pp = hh % 2
                    if L_pbft[pp]:
                        mk.wait("pe", "c_act", L_pbft[pp])
                    for c3 in range(3):
                        mk.op("pe", lambda e, hh=hh, pp=pp, c3=c3: e.transpose(
                            out=ps_tr[pp][:, c3 * 128:(c3 + 1) * 128],
                            in_=pbf[:, hh * 384 + c3 * 128: hh * 384 + (c3 + 1) * 128],
                            identity=ident[:, :]), "c_pe" if c3 == 2 else None)
                    mk.wait("act", "c_pe", mk.cnt["c_pe"])
                    tgt = mk.op("act", lambda e, pp=pp: e.activation(
                        out=pbft[:, pp * 384:(pp + 1) * 384], in_=ps_tr[pp][:, 0:384], func=AF.Copy), "c_act")
                    L_pbft[pp] = tgt
                    mk.wait("pe", "c_act", tgt)
                    for c3 in range(3):
                        sl = (qt + c3) % 7
                        mk.op("pe", lambda e, h=h, hh=hh, pp=pp, sl=sl, c3=c3, ro=ro: e.matmul(
                            out=ps_at[(h // 2) % 2][ro:ro + 64, 0:128],
                            lhsT=vroll[:, sl * 1024 + h * 64: sl * 1024 + h * 64 + 64],
                            rhs=pbft[:, pp * 384 + c3 * 128: pp * 384 + (c3 + 1) * 128],
                            start=(c3 == 0), stop=(c3 == 2), tile_position=(0, ro)),
                            "c_pe" if c3 == 2 else None)
                    if h % 2 == 1:
                        mk.wait("act", "c_pe", mk.cnt["c_pe"])
                        tgt = mk.op("act", lambda e, h=h, lq=lq: e.activation(
                            out=att[:, (h // 2) * 512 + lq * 128: (h // 2) * 512 + lq * 128 + 128],
                            in_=ps_at[(h // 2) % 2][:, 0:128], func=AF.Copy), "c_act")
                        mk.wait("pe", "c_act", tgt)
        mk.wait("pe", "c_act", mk.cnt["c_act"])
        for half in range(2):
            load_w("owT", half=half)
            for ic in range(NCH):
                for oc in range(4):
                    occ = half * 4 + oc
                    mk.op("pe", lambda e, ic=ic, oc=oc, occ=occ, qn=qn: e.matmul(
                        out=ps_sc[:, oc * 512: oc * 512 + qn], lhsT=ow[:, ic * 512 + oc * 128: ic * 512 + oc * 128 + 128],
                        rhs=att[:, ic * 512: ic * 512 + qn],
                        start=(ic == 0), stop=(ic == NCH - 1)),
                        "c_pe" if (ic == NCH - 1 and oc == 3) else None)
            mk.wait("dve", "c_pe", mk.cnt["c_pe"])
            for oc in range(4):
                occ = half * 4 + oc
                dst = resid_chunk(occ)
                mk.op("dve", lambda e, oc=oc, occ=occ, dst=dst, q0=q0, qn=qn: e.scalar_tensor_tensor(
                    out=dst[:, q0:q0 + qn], in0=ps_sc[:, oc * 512: oc * 512 + qn],
                    scalar=bo[:, occ:occ + 1], in1=dst[:, q0:q0 + qn],
                    op0=ALU.add, op1=ALU.add), "c_dve" if oc == 3 else None)
            mk.wait("pe", "c_dve", mk.cnt["c_dve"])
    return mk.cnt["c_dve"]


def emit_attna(mk, TF, xT_hbm, hbm, scr, es, name):
    nc = mk.nc
    BL = 512
    NB = TF // BL
    S = mk.sb
    xab = S[name + "_xab"]; xlnb = S[name + "_xln"]; pre = S[name + "_pre"]
    qs = S[name + "_qs"]; ks = S[name + "_ks"]; vs = S[name + "_vs"]
    acp = S[name + "_acp"]
    afm = S[name + "_afm"]; lam = S[name + "_lam"]
    gam = S[name + "_gam"]; ivg = S[name + "_ivg"]
    cb = S[name + "_cb"]; bt = S[name + "_bt"]; bbc = S[name + "_bbc"]
    beta8 = S[name + "_beta8"]; gfm = S[name + "_gfm"]; og = S[name + "_og"]
    st = S[name + "_st"]; wbuf = S[name + "_wbuf"]; wb8 = S[name + "_wb8"]
    cvw = S[name + "_cvw"]; bia = S[name + "_bia"]; bb1 = S[name + "_bb1"]
    lgb = S[name + "_lgb"]; lbb = S[name + "_lbb"]; lgr = S[name + "_lgr"]
    gbd = S[name + "_gbd"]; kqb = S[name + "_kqb"]
    pkt = S[name + "_pkt"]
    wht = S[name + "_wht"]
    kth = S[name + "_kth"]; wth = S[name + "_wth"]
    osb = S[name + "_osb"]; osq = S[name + "_osq"]; ored = S[name + "_ored"]
    r2f = S[name + "_r2f"]; ogp = S[name + "_ogp"]; wfm = S[name + "_wfm"]
    ascr = S[name + "_ascr"]; ascr2 = S[name + "_ascr2"]; zbuf = S[name + "_zbuf"]
    gc8 = S[name + "_gc8"]; erow = S[name + "_erow"]
    mdiag = S[name + "_mdiag"]; mdiag2 = S[name + "_mdiag2"]
    lamc = S[name + "_lamc"]; tls = S[name + "_tls"]
    ident = S["identbf"]
    ones_col = S["ones_col"]
    ps_pp = es.enter_context(nc.psum_tensor(name + "_ppp", [128, 512], F32))
    ps_g = es.enter_context(nc.psum_tensor(name + "_pg", [128, 512], F32))
    ps_tr = es.enter_context(nc.psum_tensor(name + "_ptr", [128, 1024], BF16))
    ps_tr2 = es.enter_context(nc.psum_tensor(name + "_ptr2", [128, 1024], BF16))
    ps_acc = es.enter_context(nc.psum_tensor(name + "_pacc", [128, 512], F32))
    ps_o = es.enter_context(nc.psum_tensor(name + "_po", [128, 512], F32))
    ps_s = [es.enter_context(nc.psum_tensor(name + "_psu%d" % j, [128, 512], F32)) for j in range(2)]

    WSLOT = [wbuf[:, 0:8192], wbuf[:, 8192:16384]]
    wcnt = [0]
    wguard = [0, 0]

    def load_w(which, avoid=None):
        idx = (1 - avoid) if avoid is not None else wcnt[0] % 2
        s = WSLOT[idx]
        wcnt[0] += 1
        mk.wait("gp", "c_pe", wguard[idx])
        for c in range(NCH):
            mk.dma("gp", s[:, c * 1024:(c + 1) * 1024], hbm[which][c * 128:(c + 1) * 128, :], "q_gp")
        return s, mk.cnt["q_gp"], idx

    def mark_used(idx):
        wguard[idx] = mk.cnt["c_pe"]

    mk.dma("gp", wb8[:, :], hbm["wb"], "q_gp")
    mk.dma("gp", cvw[:, :], hbm["cvw"], "q_gp")
    mk.dma("gp", bia[:, :], hbm["bia"], "q_gp")
    mk.dma("gp", bb1[:, :], hbm["bb1"], "q_gp")
    mk.dma("gp", lgr[0:1, :], hbm["lnr"], "q_gp")
    mk.dma("gp", mdiag[:, :], hbm["mdiag"], "q_gp")
    mk.dma("gp", mdiag2[:, :], hbm["mdiag2"], "q_gp")
    mk.dma("gp", erow[:, :], hbm["erow"], "q_gp")
    mk.wait("pe", "q_gp", mk.cnt["q_gp"])
    mk.op("pe", lambda e: e.matmul(out=ps_o[:, 0:128], lhsT=ones_col[0:1, :],
          rhs=lgr[0:1, 0:128], start=True, stop=True))
    mk.op("pe", lambda e: e.matmul(out=ps_o[:, 128:256], lhsT=ones_col[0:1, :], rhs=lgr[0:1, 128:256],
          start=True, stop=True), "c_pe")
    mk.wait("act", "c_pe", mk.cnt["c_pe"])
    mk.op("act", lambda e: e.activation(out=lgb[:, :], in_=ps_o[:, 0:128], func=AF.Copy))
    t0_ = mk.op("act", lambda e: e.activation(out=lbb[:, :], in_=ps_o[:, 128:256], func=AF.Copy), "c_act")
    mk.wait("pe", "c_act", t0_)
    mk.op("dve", lambda e: e.memset(st[:, :], 0.0))
    mk.op("dve", lambda e: e.memset(pre[:, :], 0.0))
    mk.op("dve", lambda e: e.memset(tls[:, :], 0.0))
    mk.op("dve", lambda e: e.memset(wht[:, :], 0.0))
    mk.op("dve", lambda e: e.memset(zbuf[:, :], 0.0))
    tgt = mk.op("dve", lambda e: e.memset(lamc[:, :], 0.0), "c_dve")
    mk.wait("pe", "c_dve", tgt)
    w_next, wt_next, wi_next = load_w("waT")

    for b in range(NB):
        tb = b * BL
        for hb in range(2):
            t0 = tb + hb * 256
            mk.wait("sp", "c_dve", mk.cnt["c_dve"])
            for c in range(NCH):
                mk.dma("sp", xab[:, c * 256:(c + 1) * 256], xT_hbm[c * 128:(c + 1) * 128, t0:t0 + 256], "q_sp")
            xt = mk.cnt["q_sp"]
            mk.wait("act", "q_sp", xt)
            mk.wait("pe", "q_sp", xt)
            mk.wait("dve", "q_sp", xt)
            with ExitStack() as esl:
                emit_layernorm_fm(mk, 256, lambda c: xab[:, c * 256:(c + 1) * 256], None, esl,
                                  name + "_ln%d_%d" % (b, hb),
                                  dst_chunk=lambda c: xlnb[:, c * 512 + hb * 256: c * 512 + hb * 256 + 256],
                                  ps4=(ps_pp[0:1, 0:512], ps_g[0:1, 0:512], ps_s[0][:, 0:512], ps_s[1][:, 0:512]))
        w_a, wt_a, wi_a = w_next, wt_next, wi_next
        mk.wait("pe", "q_gp", wt_a)
        w_next, wt_next, wi_next = load_w("wqT", avoid=wi_a)
        for oc in range(NCH):
            for k in range(NCH):
                mk.op("pe", lambda e, oc=oc, k=k, w_a=w_a: e.matmul(
                    out=ps_pp[:, 0:512],
                    lhsT=w_a[:, k * 1024 + oc * 128: k * 1024 + oc * 128 + 128],
                    rhs=xlnb[:, k * 512:(k + 1) * 512],
                    start=(k == 0), stop=(k == NCH - 1)), "c_pe" if k == NCH - 1 else None)
            mk.wait("act", "c_pe", mk.cnt["c_pe"])
            tgt = mk.op("act", lambda e, oc=oc: e.activation(
                out=acp[:, oc * 512:(oc + 1) * 512], in_=ps_pp[:, 0:512], func=AF.Sigmoid,
                bias=bia[:, oc:oc + 1], scale=1.0), "c_act")
            mk.wait("pe", "c_act", tgt)
        mark_used(wi_a)
        for k in range(NCH):
            mk.op("pe", lambda e, k=k: e.matmul(
                out=ps_pp[0:8, 0:512], lhsT=wb8[:, k * 8:(k + 1) * 8],
                rhs=xlnb[:, k * 512:(k + 1) * 512],
                start=(k == 0), stop=(k == NCH - 1)), "c_pe" if k == NCH - 1 else None)
        mk.wait("act", "c_pe", mk.cnt["c_pe"])
        tgt = mk.op("act", lambda e: e.activation(out=beta8[0:8, :], in_=ps_pp[0:8, 0:512],
              func=AF.Sigmoid, bias=bb1[0:8, 0:1], scale=1.0), "c_act")
        mk.wait("pe", "c_act", tgt)
        mk.wait("gp", "c_act", tgt)
        bw = mk.dma("gp", scr["beta"][0:8, tb:tb + 512], beta8[0:8, :], "q_gp")
        mk.wait("gp", "q_gp", bw)
        for h in range(NCH):
            mk.dma("gp", bbc[:, h * 512:(h + 1) * 512],
                   scr["beta"][h:h + 1, tb:tb + 512].broadcast_to([128, 512]), "q_gp")
        bbt = mk.cnt["q_gp"]
        mk.wait("dve", "q_gp", bbt)
        mk.wait("act", "q_gp", bbt)
        for oc in range(NCH):
            mk.op("act", lambda e, oc=oc: e.activation(
                out=afm[:, oc * 512:(oc + 1) * 512], in_=acp[:, oc * 512:(oc + 1) * 512],
                func=AF.Ln), "c_act" if oc == NCH - 1 else None)
        mk.wait("dve", "c_act", mk.cnt["c_act"])
        for hb in range(2):
            for oc in range(NCH):
                mk.op("dve", lambda e, oc=oc, hb=hb: e.tensor_tensor_scan(
                    out=lam[:, oc * 256:(oc + 1) * 256],
                    data0=afm[:, oc * 512 + hb * 256: oc * 512 + hb * 256 + 256],
                    data1=zbuf[:, 0:256], initial=lamc[:, oc:oc + 1],
                    op0=ALU.add, op1=ALU.add))
                mk.op("dve", lambda e: e.drain())
                mk.op("dve", lambda e, oc=oc: e.tensor_copy(out=ascr[:, 0:1], in_=lamc[:, oc:oc + 1]))
                mk.op("dve", lambda e, oc=oc: e.tensor_copy(
                    out=ascr[:, 1:16], in_=lam[:, oc * 256 + 15: oc * 256 + 255].rearrange(
                        "p (a b) -> p a b", b=16)[:, :, 0]))
                mk.op("dve", lambda e, oc=oc: e.tensor_copy(out=lamc[:, oc:oc + 1],
                      in_=lam[:, oc * 256 + 255: oc * 256 + 256]))
                mk.op("dve", lambda e: e.drain())
                tgt = mk.op("dve", lambda e, oc=oc: e.tensor_tensor(
                    out=ascr[:, 256:512], in0=lam[:, oc * 256:(oc + 1) * 256],
                    in1=ascr[:, 0:16].rearrange("p (a o) -> p a o", o=1).broadcast_to([128, 16, 16]),
                    op=ALU.subtract), "c_dve")
                mk.wait("act", "c_dve", tgt)
                mk.op("act", lambda e, oc=oc, hb=hb: e.activation(
                    out=gam.rearrange("p (n g d) -> p n g d", g=8, d=16)[:, hb * 16:hb * 16 + 16, oc, :],
                    in_=ascr[:, 256:512].rearrange("p (n d) -> p n d", d=16), func=AF.Exp))
                tgt = mk.op("act", lambda e, oc=oc, hb=hb: e.activation(
                    out=ivg[:, oc * 512 + hb * 256: oc * 512 + hb * 256 + 256],
                    in_=ascr[:, 256:512], func=AF.Exp, bias=0.0, scale=-1.0), "c_act")
                mk.wait("dve", "c_act", tgt)
        for ti, (tn, wname, has_scale) in enumerate((("q", "wkT", False), ("k", "wvT", True), ("v", "wgT", False))):
            w_cur, wt_cur, wi_cur = w_next, wt_next, wi_next
            mk.wait("pe", "q_gp", wt_cur)
            w_next, wt_next, wi_next = load_w(wname, avoid=wi_cur)
            dst = (qs, ks, vs)[ti]
            for oc in range(NCH):
                mk.op("dve", lambda e, oc=oc, ti=ti: e.tensor_copy(
                    out=pre[:, oc * 520: oc * 520 + 3], in_=tls[:, ti * 24 + oc * 3: ti * 24 + oc * 3 + 3]))
            mk.op("dve", lambda e: e.drain())
            for oc in range(NCH):
                for k in range(NCH):
                    mk.op("pe", lambda e, oc=oc, k=k, w_cur=w_cur: e.matmul(
                        out=ps_pp[:, 0:512], lhsT=w_cur[:, k * 1024 + oc * 128: k * 1024 + oc * 128 + 128],
                        rhs=xlnb[:, k * 512:(k + 1) * 512],
                        start=(k == 0), stop=(k == NCH - 1)), "c_pe" if k == NCH - 1 else None)
                mk.wait("act", "c_pe", mk.cnt["c_pe"])
                tgt = mk.op("act", lambda e, oc=oc: e.activation(
                    out=pre[:, oc * 520 + 3: oc * 520 + 515], in_=ps_pp[:, 0:512], func=AF.Copy), "c_act")
                mk.wait("pe", "c_act", tgt)
                mk.wait("dve", "c_act", tgt)
                mk.op("dve", lambda e, oc=oc, ti=ti: e.tensor_scalar(
                    out=ascr[:, 0:512], in0=pre[:, oc * 520: oc * 520 + 512],
                    scalar1=cvw[:, ti * 32 + oc * 4: ti * 32 + oc * 4 + 1],
                    scalar2=bia[:, (ti + 1) * 8 + oc: (ti + 1) * 8 + oc + 1],
                    op0=ALU.mult, op1=ALU.add))
                for j in range(1, 4):
                    mk.op("dve", lambda e, oc=oc, ti=ti, j=j: e.scalar_tensor_tensor(
                        out=ascr[:, 0:512], in0=pre[:, oc * 520 + j: oc * 520 + j + 512],
                        scalar=cvw[:, ti * 32 + oc * 4 + j: ti * 32 + oc * 4 + j + 1],
                        in1=ascr[:, 0:512], op0=ALU.mult, op1=ALU.add))
                tgt = mk.op("dve", lambda e, oc=oc, ti=ti: e.tensor_copy(
                    out=tls[:, ti * 24 + oc * 3: ti * 24 + oc * 3 + 3],
                    in_=pre[:, oc * 520 + 512: oc * 520 + 515]), "c_dve")
                mk.wait("act", "c_dve", tgt)
                if ti < 2:
                    tgt = mk.op("act", lambda e, oc=oc, dst=dst: e.activation(
                        out=dst.rearrange("p (n g d) -> p n g d", g=8, d=16)[:, :, oc, :],
                        in_=ascr[:, 0:512].rearrange("p (n d) -> p n d", d=16),
                        func=AF.Silu), "c_act")
                else:
                    tgt = mk.op("act", lambda e, oc=oc, dst=dst: e.activation(
                        out=dst[:, oc * 512:(oc + 1) * 512], in_=ascr[:, 0:512],
                        func=AF.Silu), "c_act")
                mk.wait("dve", "c_act", tgt)
            mark_used(wi_cur)
            if has_scale:
                mk.op("dve", lambda e, dst=dst: e.tensor_scalar(
                    out=dst[:, :], in0=dst[:, :],
                    scalar1=0.08838834764831845, scalar2=None, op0=ALU.mult), "c_dve")
        w_cur, wt_cur, wi_cur = w_next, wt_next, wi_next
        mk.wait("pe", "q_gp", wt_cur)
        if b < NB - 1:
            w_next, wt_next, wi_next = load_w("waT", avoid=wi_cur)
        for oc in range(NCH):
            for k in range(NCH):
                mk.op("pe", lambda e, oc=oc, k=k, w_cur=w_cur: e.matmul(
                    out=ps_pp[:, 0:512], lhsT=w_cur[:, k * 1024 + oc * 128: k * 1024 + oc * 128 + 128],
                    rhs=xlnb[:, k * 512:(k + 1) * 512],
                    start=(k == 0), stop=(k == NCH - 1)), "c_pe" if k == NCH - 1 else None)
            mk.wait("act", "c_pe", mk.cnt["c_pe"])
            tgt = mk.op("act", lambda e, oc=oc: e.activation(
                out=gfm.rearrange("p (n g d) -> p n g d", g=8, d=16)[:, :, oc, :],
                in_=ps_pp[:, 0:512].rearrange("p (n d) -> p n d", d=16), func=AF.Sigmoid), "c_act")
            mk.wait("pe", "c_act", tgt)
        mark_used(wi_cur)
        mk.wait("act", "c_dve", mk.cnt["c_dve"])
        for oc in range(NCH):
            mk.op("act", lambda e, oc=oc: e.activation(
                out=ascr2[:, 0:512], in_=afm[:, oc * 512:(oc + 1) * 512], func=AF.Exp,
                bias=0.0, scale=-1.0), "c_act")
            mk.wait("dve", "c_act", mk.cnt["c_act"])
            mk.op("dve", lambda e, oc=oc: e.tensor_tensor(
                out=bt.rearrange("p (n g d) -> p n g d", g=8, d=16)[:, :, oc, :],
                in0=ascr2[:, 0:512].rearrange("p (n d) -> p n d", d=16),
                in1=bbc[:, oc * 512:(oc + 1) * 512].rearrange("p (n d) -> p n d", d=16), op=ALU.mult))
            mk.op("dve", lambda e, oc=oc: e.tensor_tensor(
                out=ascr2[:, 0:512], in0=vs[:, oc * 512:(oc + 1) * 512],
                in1=ivg[:, oc * 512:(oc + 1) * 512], op=ALU.mult))
            tgt = mk.op("dve", lambda e, oc=oc: e.tensor_tensor(
                out=cb[:, oc * 512:(oc + 1) * 512], in0=ascr2[:, 0:512],
                in1=acp[:, oc * 512:(oc + 1) * 512], op=ALU.mult), "c_dve")
            mk.wait("act", "c_dve", tgt)
        if "dbg_ks" in scr and b == NB - 1:
            tgt = mk.cnt["c_dve"]
            mk.wait("gp", "c_dve", tgt)
            mk.dma("gp", scr["dbg_ks"], ks[:, :], "q_gp")
        prevctx = None
        for n in range(32):
            prevctx = emit_attna_chunk(mk, nc, name, n, locals(), prevctx)
        t_lastog = emit_attna_partO(mk, name, locals(), prevctx)
        if "dbg0" in scr and b == 0:
            mk.wait("gp", "c_dve", mk.cnt["c_dve"])
            mk.wait("gp", "c_act", mk.cnt["c_act"])
            for dnm, dsrc in (("og", og), ("gam", gam), ("cb", cb), ("st", st), ("ks", ks), ("qs", qs)):
                mk.dma("gp", scr["dbg0"][dnm], dsrc[:, :], "q_gp")
        w_cur2, wt_cur2, wi_cur2 = load_w("woT", avoid=(wi_next if b < NB - 1 else None))
        mk.wait("pe", "q_gp", wt_cur2)
        mk.wait("pe", "c_act", t_lastog)
        for oc in range(NCH):
            for k in range(NCH):
                mk.op("pe", lambda e, oc=oc, k=k, w_cur2=w_cur2: e.matmul(
                    out=ps_pp[:, 0:512], lhsT=w_cur2[:, k * 1024 + oc * 128: k * 1024 + oc * 128 + 128],
                    rhs=og[:, k * 512:(k + 1) * 512],
                    start=(k == 0), stop=(k == NCH - 1)), "c_pe" if k == NCH - 1 else None)
            mk.wait("act", "c_pe", mk.cnt["c_pe"])
            if oc > 0:
                mk.wait("act", "q_gp", mk.cnt["q_gp"])
            tgt = mk.op("act", lambda e, oc=oc: e.activation(
                out=ascr2[:, 0:512], in_=ps_pp[:, 0:512], func=AF.Copy), "c_act")
            mk.wait("gp", "c_act", tgt)
            mk.dma("gp", scr["aout"][oc * 128:(oc + 1) * 128, tb:tb + 512], ascr2[:, 0:512], "q_gp")
            mk.wait("pe", "c_act", tgt)
        mark_used(wi_cur2)
        mk.wait("gp", "q_gp", mk.cnt["q_gp"])
        mk.wait("act", "q_gp", mk.cnt["q_gp"])
        mk.wait("dve", "q_gp", mk.cnt["q_gp"])
    return

def emit_attna_chunk(mk, nc, name, n, L, prev):
    S = mk.sb
    qs, ks, vs = L["qs"], L["ks"], L["vs"]
    gam, cb, bt, gfm, og, st = L["gam"], L["cb"], L["bt"], L["gfm"], L["og"], L["st"]
    gbd, kqb, pkt2 = L["gbd"], L["kqb"], L["pkt"]
    wht, kth2, wth = L["wht"], L["kth"], L["wth"]

    osb, osq, ored, r2f, ogp, wfm = L["osb"], L["osq"], L["ored"], L["r2f"], L["ogp"], L["wfm"]
    ascr, ascr2 = L["ascr"], L["ascr2"]
    gc8, erow = L["gc8"], L["erow"]
    mdiag, mdiag2, lgb, lbb = L["mdiag"], L["mdiag2"], L["lgb"], L["lbb"]
    ident = L["ident"]
    ps_pp, ps_g, ps_tr, ps_tr2, ps_acc, ps_o = L["ps_pp"], L["ps_g"], L["ps_tr"], L["ps_tr2"], L["ps_acc"], L["ps_o"]
    ps_s = L["ps_s"]
    c0 = n * 16
    nb = n * 128
    par = n % 2
    kth = kth2[:, par * 1024:(par + 1) * 1024]
    pkt = pkt2[:, par * 512:(par + 1) * 512]
    p0f = ascr2[:, 512 + par * 128: 640 + par * 128]
    p0t = ps_tr[:, 512 + par * 128: 640 + par * 128]
    kc = ks[:, nb:nb + 128]; qc = qs[:, nb:nb + 128]
    btc = bt[:, nb:nb + 128]; gmc = gam[:, nb:nb + 128]; gfc = gfm[:, nb:nb + 128]

    def pv(buf, w=16, off=0):
        return buf.rearrange("p (a b) -> p a b", b=512)[:, :, c0 + off: c0 + off + w]

    mk.wait("pe", "c_dve", mk.cnt["c_dve"])
    mk.wait("pe", "c_act", mk.cnt["c_act"])
    for h in range(NCH):
        mk.op("pe", lambda e, h=h: e.matmul(
            out=ps_pp[:, h * 32: h * 32 + 16], lhsT=st[:, h * 128:(h + 1) * 128],
            rhs=kc[:, h * 16:(h + 1) * 16], start=True, stop=True))
        mk.op("pe", lambda e, h=h: e.matmul(
            out=ps_pp[:, h * 32 + 16: h * 32 + 32], lhsT=st[:, h * 128:(h + 1) * 128],
            rhs=qc[:, h * 16:(h + 1) * 16], start=True, stop=True))
    mk.op("pe", lambda e: e.matmul(out=ps_g[:, 0:128], lhsT=kc, rhs=kc,
          start=True, stop=True))
    mk.op("pe", lambda e: e.matmul(out=ps_g[:, 128:256], lhsT=kc, rhs=qc,
          start=True, stop=True), "c_pe")
    tgt_pg = mk.cnt["c_pe"]
    mk.wait("dve", "c_pe", tgt_pg)
    ppv = ps_pp[:, 0:256].rearrange("p (a b) -> p a b", b=32)
    mk.op("dve", lambda e: e.tensor_tensor(
        out=r2f.rearrange("p (a b) -> p a b", b=16), in0=pv(cb),
        in1=ppv[:, :, 0:16], op=ALU.subtract))
    mk.op("dve", lambda e: e.tensor_tensor(out=gbd[:, :], in0=ps_g[:, 0:128], in1=mdiag[:, :], op=ALU.mult))
    mk.op("dve", lambda e: e.tensor_tensor(
        out=p0f.rearrange("p (a b) -> p a b", b=16),
        in0=gmc.rearrange("p (a b) -> p a b", b=16),
        in1=ppv[:, :, 16:32], op=ALU.mult))
    tgt_gb = mk.op("dve", lambda e: e.tensor_tensor(out=kqb[:, :], in0=ps_g[:, 128:256], in1=mdiag2[:, :],
          op=ALU.mult), "c_dve")
    mk.wait("pe", "c_dve", tgt_gb)
    mk.op("pe", lambda e: e.transpose(out=ps_tr[:, 0:128], in_=r2f[:, :], identity=ident[:, :]))
    mk.op("pe", lambda e: e.transpose(out=ps_tr[:, 128:256], in_=btc, identity=ident[:, :]))
    mk.op("pe", lambda e: e.transpose(out=ps_tr[:, 256:384], in_=gmc, identity=ident[:, :]))
    mk.op("pe", lambda e: e.transpose(out=ps_tr[:, 384:512], in_=gfc, identity=ident[:, :]))
    tgt_p0t = mk.op("pe", lambda e: e.transpose(out=p0t, in_=p0f, identity=ident[:, :]), "c_pe")
    if n == 0 or KTH_PRECHAIN:
        for h in range(NCH):
            mk.op("pe", lambda e, h=h: e.transpose(
                out=ps_tr2[0:16, h * 128:(h + 1) * 128], in_=kc[:, h * 16:(h + 1) * 16],
                identity=ident[:, :]), "c_pe" if h == NCH - 1 else None)
    tgt_tr2 = mk.cnt["c_pe"]
    mk.wait("act", "c_pe", tgt_tr2)
    tgt_pk = mk.op("act", lambda e: e.activation(out=pkt[:, :], in_=ps_tr[:, 0:512], func=AF.Copy), "c_act")
    if n == 0 or KTH_PRECHAIN:
        tgt_pk = mk.op("act", lambda e: e.activation(out=kth[0:16, :], in_=ps_tr2[0:16, 0:1024], func=AF.Copy), "c_act")
    r2t = pkt[:, 0:128]; btt = pkt[:, 128:256]; gmt = pkt[:, 256:384]; gpk = pkt[:, 384:512]
    if prev is not None:
        ppar = prev["par"]
        gmt_p = pkt2[:, ppar * 512 + 256: ppar * 512 + 384]
        gpk_p = pkt2[:, ppar * 512 + 384: ppar * 512 + 512]
        p0t_p = ps_tr[:, 512 + ppar * 128: 640 + ppar * 128]

    def filler(k):
        if prev is None:
            return
        if k == 0:
            mk.op("dve", lambda e: e.tensor_tensor(out=osb[:, :], in0=ps_o[:, 0:128], in1=gmt_p, op=ALU.mult))
        elif k == 1:
            mk.wait("act", "c_pe", prev["p0t_tag"])
            L["t_o3"] = mk.op("act", lambda e: e.activation(
                out=ascr2[:, 256:384], in_=p0t_p, func=AF.Copy), "c_act")
        elif k == 2:
            mk.wait("dve", "c_act", L["t_o3"])
            mk.op("dve", lambda e: e.tensor_tensor(out=osb[:, :], in0=osb[:, :], in1=ascr2[:, 256:384], op=ALU.add))
        elif k == 3:
            mk.op("dve", lambda e: e.reduce_sum(out=ored[:, 0:1], in_=osb[:, :], axis=AX.X))
            mk.op("dve", lambda e: e.drain())
        elif k == 4:
            mk.op("dve", lambda e: e.tensor_scalar(out=ored[:, 1:2], in0=ored[:, 0:1],
                  scalar1=1.0 / 128, scalar2=None, op0=ALU.mult))
            mk.op("dve", lambda e: e.drain())
        elif k == 5:
            L["t_m"] = mk.op("dve", lambda e: e.tensor_scalar(out=osb[:, :], in0=osb[:, :],
                  scalar1=ored[:, 1:2], scalar2=None, op0=ALU.subtract), "c_dve")
        elif k == 6:
            mk.wait("act", "c_dve", L["t_m"])
            L["t_sq"] = mk.op("act", lambda e: e.activation(out=osq[:, :], in_=osb[:, :], func=AF.Square), "c_act")
        elif k == 8:
            mk.wait("dve", "c_act", L["t_sq"])
            mk.op("dve", lambda e: e.reduce_sum(out=ored[:, 2:3], in_=osq[:, :], axis=AX.X))
            mk.op("dve", lambda e: e.drain())
        elif k == 9:
            L["t_v"] = mk.op("dve", lambda e: e.tensor_scalar(out=ored[:, 5:6], in0=ored[:, 2:3],
                  scalar1=1.0 / 128, scalar2=None, op0=ALU.mult), "c_dve")
        elif k == 10:
            mk.wait("act", "c_dve", L["t_v"])
            mk.op("act", lambda e: e.activation(out=ored[:, 6:7], in_=ored[:, 5:6], func=AF.Ln, bias=EPS, scale=1.0))
            L["t_r"] = mk.op("act", lambda e: e.activation(out=ored[:, 6:7], in_=ored[:, 6:7], func=AF.Exp,
                  bias=0.0, scale=-0.5), "c_act")
        elif k == 12:
            mk.wait("dve", "c_act", L["t_r"])
            mk.op("dve", lambda e: e.tensor_scalar(out=osb[:, :], in0=osb[:, :], scalar1=ored[:, 6:7],
                  scalar2=None, op0=ALU.mult))
        elif k == 13:
            mk.op("dve", lambda e: e.tensor_tensor(out=osb[:, :], in0=osb[:, :], in1=lgb[:, :], op=ALU.mult))
            mk.op("dve", lambda e: e.tensor_tensor(out=osb[:, :], in0=osb[:, :], in1=lbb[:, :], op=ALU.add))
        elif k == 14:
            L["t_og"] = mk.op("dve", lambda e, gpk_p=gpk_p: e.tensor_tensor(
                out=ogp[:, :], in0=osb[:, :], in1=gpk_p, op=ALU.mult), "c_dve")

    mk.wait("pe", "c_act", tgt_pk)
    for k in range(16):
        pc = (k % 2) * 128
        mk.op("pe", lambda e, pc=pc: e.matmul(
            out=ps_acc[:, pc:pc + 128], lhsT=ident[:, :], rhs=r2t,
            start=True, stop=(k == 0)), None if k > 0 else "c_pe")
        if k > 0:
            mk.op("pe", lambda e, pc=pc: e.matmul(
                out=ps_acc[:, pc:pc + 128], lhsT=gbd[:, :], rhs=wht[:, :],
                start=False, stop=True), "c_pe")
        tgt_mm = mk.cnt["c_pe"]
        mk.wait("dve", "c_pe", tgt_mm)
        tgt_st = mk.op("dve", lambda e, pc=pc: e.tensor_tensor(
            out=wht[:, :], in0=btt, in1=ps_acc[:, pc:pc + 128], op=ALU.mult), "c_dve")
        filler(k)
        if (not KTH_PRECHAIN) and n < 31 and 2 <= k <= 9:
            h = k - 2
            kcn = ks[:, (nb + 128) + h * 16: (nb + 128) + (h + 1) * 16]
            L["t_ktr"] = mk.op("pe", lambda e, h=h, kcn=kcn: e.transpose(
                out=ps_tr2[0:16, h * 128:(h + 1) * 128], in_=kcn,
                identity=ident[:, :]), "c_pe")
        if (not KTH_PRECHAIN) and n < 31 and k == 10:
            mk.wait("act", "c_pe", L["t_ktr"])
            L["t_kev"] = mk.op("act", lambda e: e.activation(
                out=kth2[0:16, (1 - par) * 1024:(2 - par) * 1024],
                in_=ps_tr2[0:16, 0:1024], func=AF.Copy), "c_act")
        mk.wait("pe", "c_dve", tgt_st)
    if prev is not None:
        mk.wait("pe", "c_dve", L["t_og"])
        if (not KTH_PRECHAIN) and n < 31:
            mk.wait("pe", "c_act", L["t_kev"])
        t_ogt = mk.op("pe", lambda e: e.transpose(out=ps_tr2[:, 896:1024],
              in_=ogp[:, :], identity=ident[:, :]), "c_pe")
        mk.wait("act", "c_pe", t_ogt)
        pc0 = prev["c0"]
        mk.op("act", lambda e, pc0=pc0: e.activation(
            out=og.rearrange("p (a b) -> p a b", b=512)[:, :, pc0:pc0 + 16],
            in_=ps_tr2[:, 896:1024].rearrange("p (a b) -> p a b", b=16),
            func=AF.Copy), "c_act")
    mk.op("pe", lambda e: e.transpose(out=ps_tr[:, 768:896], in_=wht[:, :], identity=ident[:, :]), "c_pe")
    mk.wait("act", "c_pe", mk.cnt["c_pe"])
    tgt_wf = mk.op("act", lambda e: e.activation(out=wfm[:, :], in_=ps_tr[:, 768:896], func=AF.Copy), "c_act")
    mk.wait("pe", "c_act", tgt_wf)
    for h in range(NCH):
        mk.op("pe", lambda e, h=h: e.transpose(
            out=ps_tr2[0:16, h * 128:(h + 1) * 128], in_=wfm[:, h * 16:(h + 1) * 16],
            identity=ident[:, :]), "c_pe" if h == NCH - 1 else None)
    mk.wait("act", "c_pe", mk.cnt["c_pe"])
    tgt_wt = mk.op("act", lambda e: e.activation(out=wth[0:16, :], in_=ps_tr2[0:16, 0:1024], func=AF.Copy), "c_act")
    mk.wait("pe", "c_act", tgt_wt)
    mk.op("pe", lambda e: e.transpose(
        out=ps_tr[0:8, 896:1024], in_=gam[:, nb + 15:nb + 128:16],
        identity=ident[:, :]), "c_pe")
    mk.wait("act", "c_pe", mk.cnt["c_pe"])
    tgt_g8 = mk.op("act", lambda e: e.activation(out=gc8[0:8, :], in_=ps_tr[0:8, 896:1024], func=AF.Copy), "c_act")
    mk.wait("pe", "c_act", tgt_g8)
    for h in range(NCH):
        mk.op("pe", lambda e, h=h: e.matmul(
            out=ps_s[h // 4][:, (h % 4) * 128:(h % 4 + 1) * 128], lhsT=kth[0:16, h * 128:(h + 1) * 128],
            rhs=wth[0:16, h * 128:(h + 1) * 128], start=True, stop=True))
    for h in range(NCH):
        psg = ps_acc if h < 4 else ps_o
        mk.op("pe", lambda e, h=h, psg=psg: e.matmul(
            out=psg[:, (h % 4) * 128:(h % 4 + 1) * 128], lhsT=erow[0:8, h * 128:(h + 1) * 128],
            rhs=gc8[0:8, :], start=True, stop=True), "c_pe" if h == NCH - 1 else None)
    tgt_su = mk.cnt["c_pe"]
    mk.wait("dve", "c_pe", tgt_su)
    for g2 in range(2):
        psg = ps_acc if g2 == 0 else ps_o
        mk.op("dve", lambda e, g2=g2: e.tensor_tensor(
            out=ascr[:, 0:512], in0=st[:, g2 * 512:(g2 + 1) * 512], in1=ps_s[g2][:, 0:512], op=ALU.add))
        mk.op("dve", lambda e, g2=g2, psg=psg: e.tensor_tensor(
            out=st[:, g2 * 512:(g2 + 1) * 512], in0=ascr[:, 0:512], in1=psg[:, 0:512], op=ALU.mult),
            "c_dve" if g2 == 1 else None)
    tgt_stup = mk.cnt["c_dve"]
    mk.wait("pe", "c_dve", tgt_stup)
    tgt_omm = mk.op("pe", lambda e: e.matmul(out=ps_o[:, 0:128], lhsT=kqb[:, :], rhs=wht[:, :],
          start=True, stop=True), "c_pe")
    return {"par": par, "c0": c0, "p0t_tag": tgt_p0t, "omm_tag": tgt_omm}


def emit_attna_partO(mk, name, L, prev):
    S = mk.sb
    og = L["og"]
    osb, osq, ored, ogp = L["osb"], L["osq"], L["ored"], L["ogp"]
    ascr2, lgb, lbb = L["ascr2"], L["lgb"], L["lbb"]
    ident = L["ident"]
    ps_tr, ps_acc, ps_o = L["ps_tr"], L["ps_acc"], L["ps_o"]
    pkt2 = L["pkt"]
    ppar = prev["par"]
    gmt_p = pkt2[:, ppar * 512 + 256: ppar * 512 + 384]
    gpk_p = pkt2[:, ppar * 512 + 384: ppar * 512 + 512]
    p0t_p = ps_tr[:, 512 + ppar * 128: 640 + ppar * 128]
    mk.wait("dve", "c_pe", prev["omm_tag"])
    mk.op("dve", lambda e: e.tensor_tensor(out=osb[:, :], in0=ps_o[:, 0:128], in1=gmt_p, op=ALU.mult))
    mk.wait("act", "c_pe", prev["p0t_tag"])
    t_o3 = mk.op("act", lambda e: e.activation(out=ascr2[:, 256:384], in_=p0t_p, func=AF.Copy), "c_act")
    mk.wait("dve", "c_act", t_o3)
    mk.op("dve", lambda e: e.tensor_tensor(out=osb[:, :], in0=osb[:, :], in1=ascr2[:, 256:384], op=ALU.add))
    mk.op("dve", lambda e: e.reduce_sum(out=ored[:, 0:1], in_=osb[:, :], axis=AX.X))
    mk.op("dve", lambda e: e.drain())
    mk.op("dve", lambda e: e.tensor_scalar(out=ored[:, 1:2], in0=ored[:, 0:1],
          scalar1=1.0 / 128, scalar2=None, op0=ALU.mult))
    mk.op("dve", lambda e: e.drain())
    t_m = mk.op("dve", lambda e: e.tensor_scalar(out=osb[:, :], in0=osb[:, :],
          scalar1=ored[:, 1:2], scalar2=None, op0=ALU.subtract), "c_dve")
    mk.wait("act", "c_dve", t_m)
    t_sq = mk.op("act", lambda e: e.activation(out=osq[:, :], in_=osb[:, :], func=AF.Square), "c_act")
    mk.wait("dve", "c_act", t_sq)
    mk.op("dve", lambda e: e.reduce_sum(out=ored[:, 2:3], in_=osq[:, :], axis=AX.X))
    mk.op("dve", lambda e: e.drain())
    t_v = mk.op("dve", lambda e: e.tensor_scalar(out=ored[:, 5:6], in0=ored[:, 2:3], scalar1=1.0 / 128,
          scalar2=None, op0=ALU.mult), "c_dve")
    mk.wait("act", "c_dve", t_v)
    mk.op("act", lambda e: e.activation(out=ored[:, 6:7], in_=ored[:, 5:6], func=AF.Ln, bias=EPS, scale=1.0))
    t_r = mk.op("act", lambda e: e.activation(out=ored[:, 6:7], in_=ored[:, 6:7], func=AF.Exp,
          bias=0.0, scale=-0.5), "c_act")
    mk.wait("dve", "c_act", t_r)
    mk.op("dve", lambda e: e.tensor_scalar(out=osb[:, :], in0=osb[:, :], scalar1=ored[:, 6:7],
          scalar2=None, op0=ALU.mult))
    mk.op("dve", lambda e: e.tensor_tensor(out=osb[:, :], in0=osb[:, :], in1=lgb[:, :], op=ALU.mult))
    mk.op("dve", lambda e: e.tensor_tensor(out=osb[:, :], in0=osb[:, :], in1=lbb[:, :], op=ALU.add))
    t_og = mk.op("dve", lambda e: e.tensor_tensor(out=ogp[:, :], in0=osb[:, :], in1=gpk_p, op=ALU.mult), "c_dve")
    mk.wait("pe", "c_dve", t_og)
    ps_tr2 = L["ps_tr2"]
    t_ogt = mk.op("pe", lambda e: e.transpose(out=ps_tr2[:, 896:1024],
          in_=ogp[:, :], identity=ident[:, :]), "c_pe")
    mk.wait("act", "c_pe", t_ogt)
    pc0 = prev["c0"]
    t_oge = mk.op("act", lambda e, pc0=pc0: e.activation(
        out=og.rearrange("p (a b) -> p a b", b=512)[:, :, pc0:pc0 + 16],
        in_=ps_tr2[:, 896:1024].rearrange("p (a b) -> p a b", b=16),
        func=AF.Copy), "c_act")
    return t_oge


from concourse.bass_utils import run_bass_kernel_spmd
import ml_dtypes

_BF = ml_dtypes.bfloat16
_NC_CACHE = []


def _build_full():
    nc = bass.Bass("TRN2", target_bir_lowering=False, detect_race_conditions=False)
    es = ExitStack()
    with es:
        mk = MK(nc, es)
        xT = mk.din("xT", [1024, TFULL], F32)
        xe = mk.din("xe", [1024, TEXT], F32)
        selc = mk.din("selc", [128, 2], F32)
        for w in ("waT", "wqT", "wkT", "wvT", "wgT", "woT"):
            mk.din(w, [1024, 1024], F32)
        mk.din("wb_h", [128, 64], F32)
        mk.din("cvw_h", [128, 96], F32)
        mk.din("bia_h", [128, 32], F32)
        mk.din("bb1_h", [8, 1], F32)
        mk.din("lnr_h", [1, 256], F32)
        mk.din("mdiag_h", [128, 128], BF16)
        mk.din("mdiag2_h", [128, 128], BF16)
        mk.din("erow_h", [8, 1024], BF16)
        identbf = mk.din("identbf_h", [128, 128], BF16)
        mk.din("wupT", [1024, 4096], F32)
        mk.din("wdnT", [4096, 1024], F32)
        mk.din("pqT", [1024, 512], F32)
        mk.din("keysT", [64, 512], F32)
        mk.din("deT", [1024, 4096], F32)
        mk.din("ue", [4096, 1024], F32)
        for w in ("abqwT", "abkwT", "abvwT", "abowT"):
            mk.din(w, [1024, 1024], F32)
        for b in ("qb8", "kb8", "ob8"):
            mk.din(b, [128, 8], F32)
        mk.din("vb_row", [1, 1024], F32)
        maskc_h = mk.din("maskc", [128, 3 * 384], BF16)
        outT = mk.dout("outT", [1024, 2048], F32)
        aoutp = mk.dscratch("aout_s", [1024, 4352], BF16)
        beta_s = mk.dscratch("beta_s", [8, TFULL], F32)
        for s in ("q_sp", "q_gp", "c_pe", "c_act", "c_dve", "c_gp"):
            mk.new_sem(s)
        mk.sbuf("scr512a", [128, 512], F32)
        mk.sbuf("scr512b", [128, 512], F32)
        mk.sbuf("scr512c", [128, 512], F32)
        mk.sbuf("srow", [1, 2048], F32)
        oc_ = mk.sbuf("ones_col", [1, 128], F32)
        o1_ = mk.sbuf("ones128", [128, 1], F32)
        idb = mk.sbuf("identbf", [128, 128], BF16)
        sel = mk.sbuf("sel_s", [128, 2], F32)
        mk.init_consts([1e-5, -0.5])
        mk.op("dve", lambda e: e.memset(oc_[:, :], 1.0))
        mk.op("dve", lambda e: e.memset(o1_[:, :], 1.0))
        mk.dma("sp", idb[:, :], identbf, "q_sp")
        mk.dma("sp", sel[:, :], mk.dram["selc"], "q_sp")

        with ExitStack() as esA:
            def asb(nm, shape, dt):
                return esA.enter_context(nc.sbuf_tensor(nm, shape, dt))
            N = "aa"
            for nm, shape, dt in (
                ("_xab", [128, 8 * 256], F32), ("_xln", [128, 8 * 512], BF16),
                ("_pre", [128, 8 * 520], BF16),
                ("_qs", [128, 8 * 512], BF16), ("_ks", [128, 8 * 512], BF16),
                ("_vs", [128, 8 * 512], BF16), ("_gam", [128, 8 * 512], BF16),
                ("_ivg", [128, 8 * 512], BF16), ("_cb", [128, 8 * 512], BF16),
                ("_bt", [128, 8 * 512], BF16), ("_bbc", [128, 8 * 512], BF16),
                ("_gfm", [128, 8 * 512], BF16), ("_og", [128, 8 * 512], BF16),
                ("_acp", [128, 8 * 512], BF16),
                ("_afm", [128, 8 * 512], F32), ("_lam", [128, 8 * 256], F32),
                ("_beta8", [8, 512], F32), ("_st", [128, 8 * 128], BF16),
                ("_wbuf", [128, 2 * 8192], BF16), ("_wb8", [128, 64], BF16),
                ("_cvw", [128, 96], F32), ("_bia", [128, 32], F32),
                ("_bb1", [8, 1], F32), ("_lgb", [128, 128], F32),
                ("_lbb", [128, 128], F32), ("_lgr", [1, 256], F32),
                ("_gbd", [128, 128], BF16), ("_kqb", [128, 128], BF16),
                ("_wht", [128, 128], BF16),
                ("_r2f", [128, 128], BF16), ("_ogp", [128, 128], BF16),
                ("_wfm", [128, 128], BF16), ("_mdiag", [128, 128], BF16),
                ("_mdiag2", [128, 128], BF16), ("_pkt", [128, 1024], BF16),
                ("_kth", [16, 2048], BF16), ("_wth", [16, 1024], BF16),
                ("_osb", [128, 128], F32), ("_osq", [128, 128], F32),
                ("_ored", [128, 8], F32), ("_ascr", [128, 512], F32),
                ("_ascr2", [128, 768], BF16), ("_zbuf", [128, 256], F32),
                ("_gc8", [8, 128], BF16), ("_erow", [8, 1024], BF16),
                ("_lamc", [128, 8], F32), ("_tls", [128, 72], BF16),
            ):
                mk.sb[N + nm] = asb(N + nm, shape, dt)
            mk.barrier()
            az = mk.sb[N + "_ascr2"]
            tz = mk.op("dve", lambda e: e.memset(az[:, 0:256], 0.0), "c_dve")
            mk.wait("gp", "c_dve", tz)
            for c in range(NCH):
                mk.dma("gp", aoutp[c * 128:(c + 1) * 128, 0:256], az[:, 0:256], "q_gp")
            mk.wait("dve", "q_gp", mk.cnt["q_gp"])
            hbm_a = {"waT": mk.dram["waT"], "wqT": mk.dram["wqT"], "wkT": mk.dram["wkT"],
                     "wvT": mk.dram["wvT"], "wgT": mk.dram["wgT"], "woT": mk.dram["woT"],
                     "wb": mk.dram["wb_h"], "cvw": mk.dram["cvw_h"], "bia": mk.dram["bia_h"],
                     "bb1": mk.dram["bb1_h"], "lnr": mk.dram["lnr_h"], "mdiag": mk.dram["mdiag_h"],
                     "mdiag2": mk.dram["mdiag2_h"], "erow": mk.dram["erow_h"]}
            emit_attna(mk, TFULL, xT, hbm_a, {"aout": aoutp[:, 256:4352], "beta": beta_s}, esA, N)
            mk.barrier()

        with ExitStack() as esB:
            resid = esB.enter_context(nc.sbuf_tensor("resid", [128, NCH * TEXT], F32))
            xln2 = esB.enter_context(nc.sbuf_tensor("xln2", [128, NCH * TEXT], BF16))
            arena = esB.enter_context(nc.sbuf_tensor("arena", [128, 21248], F32))
            mk.sb["resid"] = resid
            mk.sb["xln2"] = xln2
            mk.sb["arena"] = arena
            pkeys = esB.enter_context(nc.sbuf_tensor("peer_keys", [128, 512], BF16))
            mk.sb["peer_keys"] = pkeys
            mk.view("ffn_wup", arena, 0, NCH * 2048, BF16)
            mk.view("ffn_wdn", arena, 8192, 16 * 1024, BF16)
            mk.view("ffn_mid", arena, 16384, 16 * 256, BF16)
            o = [0]
            def V(nm, cols, dt):
                mk.view(nm, arena, o[0], cols, dt)
                o[0] += (cols if dt == F32 else (cols + 1) // 2)
                o[0] = (o[0] + 1) & ~1
            nt = (TEXT + 127) // 128
            V("peer_qfm", 8 * 512, BF16)
            V("peer_sims", 512, F32)
            V("peer_exa", nt * 256, BF16)
            V("peer_eya", nt * 256, BF16)
            V("peer_tau", nt * 4, F32)
            V("peer_srow", 64, F32)
            V("peer_s8", 64, F32)
            V("peer_det", 8 * 1024, BF16)
            V("peer_ue", 2 * 1024, BF16)
            V("peer_dsb", 1024, BF16)
            V("peer_dsb2", 1024, BF16)
            V("peer_pds", 1024, F32)
            V("peer_pd2", 1024, F32)
            V("peer_wtmp", 1024, BF16)
            V("peer_wacc", 1024, BF16)
            wtw_off = o[0]
            V("peer_wtw", 32 * 128 * 2, BF16)
            mk.view("peer_wq", arena, wtw_off, 8 * 512, BF16)
            assert o[0] <= 21248, o[0]
            o[0] = 0
            V("ab_wbuf", 8 * 1024, BF16)
            V("ab_kroll", 8 * 896, BF16)
            V("ab_vroll", 7 * 1024, BF16)
            V("ab_qbuf", 8 * 512, BF16)
            V("ab_att", 8 * 512, BF16)
            exps_off = o[0]
            V("ab_exps", 4 * 384, F32)
            V("ab_pbf", 4 * 384, BF16)
            V("ab_pbft", 2 * 384, BF16)
            V("ab_red", 16, F32)
            V("ab_bq", 8, F32); V("ab_bk", 8, F32); V("ab_bo", 8, F32)
            V("ab_vbb", 1024, BF16)
            V("ab_mkc", 3 * 384, BF16)
            mk.sb["ab_vbr"] = arena[0:1, exps_off:exps_off + 1024]
            assert o[0] <= 21248, o[0]
            for k in list(mk.sb):
                if k.startswith("peer_"):
                    mk.sb[k.replace("peer_", "peer0_", 1)] = mk.sb[k]
                    mk.sb[k.replace("peer_", "peer1_", 1)] = mk.sb[k]
                if k.startswith("ffn_"):
                    mk.sb[k.replace("ffn_", "ffn0_", 1)] = mk.sb[k]
                    mk.sb[k.replace("ffn_", "ffn1_", 1)] = mk.sb[k]

            rc = lambda c: resid[:, c * TEXT:(c + 1) * TEXT]
            for c in range(NCH):
                mk.dma("sp", rc(c), xe[c * 128:(c + 1) * 128, :], "q_sp")
            xet = mk.cnt["q_sp"]
            for half, (c0_, c1_) in enumerate(((0, TEXT), (2048, 2048 + TEXT))):
                for c in range(NCH):
                    mk.dma("gp", xln2[:, c * TEXT:(c + 1) * TEXT],
                           aoutp[c * 128:(c + 1) * 128, c0_:c1_], "q_gp")
                agt = mk.cnt["q_gp"]
                if half == 0:
                    mk.wait("dve", "q_sp", xet)
                mk.wait("dve", "q_gp", agt)
                tsl = None
                for c in range(NCH):
                    tsl = mk.op("dve", lambda e, c=c, half=half: e.scalar_tensor_tensor(
                        out=rc(c), in0=xln2[:, c * TEXT:(c + 1) * TEXT],
                        scalar=sel[:, half:half + 1], in1=rc(c),
                        op0=ALU.mult, op1=ALU.add), "c_dve" if c == NCH - 1 else None)
                mk.wait("gp", "c_dve", tsl)
            mk.barrier()
            hbm_p = {"wqT": mk.dram["pqT"], "keysT": mk.dram["keysT"],
                     "deT": mk.dram["deT"], "ue": mk.dram["ue"]}
            hbm_b = {"qwT": mk.dram["abqwT"], "kwT": mk.dram["abkwT"], "vwT": mk.dram["abvwT"],
                     "owT": mk.dram["abowT"], "qb8": mk.dram["qb8"], "kb8": mk.dram["kb8"],
                     "ob8": mk.dram["ob8"], "vb_row": mk.dram["vb_row"]}
            for sub in range(2):
                with ExitStack() as esl:
                    emit_layernorm_fm(mk, TEXT, rc, xln2, esl, "lnf%d" % sub)
                mk.barrier()
                with ExitStack() as esf:
                    emit_ffn_dense(mk, TEXT, xln2, rc, mk.dram["wupT"], mk.dram["wdnT"], esf, "ffn%d" % sub)
                mk.barrier()
                with ExitStack() as esp:
                    emit_peer(mk, TEXT, xln2, rc, hbm_p, esp, "peer%d" % sub, GRP=2)
                mk.barrier()
                if sub == 0:
                    with ExitStack() as esl:
                        emit_layernorm_fm(mk, TEXT, rc, xln2, esl, "lnb")
                    mk.barrier()
                    with ExitStack() as esa:
                        emit_attnb(mk, TEXT, 256, xln2, rc, hbm_b, maskc_h, esa, "ab")
                    mk.barrier()
            for c in range(NCH):
                mk.dma("sp", outT[c * 128:(c + 1) * 128, :], resid[:, c * TEXT + 256:(c + 1) * TEXT], "q_sp")
            mk.wait("sp", "q_sp", mk.cnt["q_sp"])
            mk.barrier()
        mk.finalize()
    return nc


def _host_pack(inputs):
    i = {k: np.asarray(v) for k, v in inputs.items()}
    shared = {}
    shared["waT"] = np.ascontiguousarray(i["L0_attn_a_Wa"], np.float32)
    shared["wqT"] = np.ascontiguousarray(i["L0_attn_a_Wq"], np.float32)
    shared["wkT"] = np.ascontiguousarray(i["L0_attn_a_Wk"], np.float32)
    shared["wvT"] = np.ascontiguousarray(i["L0_attn_a_Wv"], np.float32)
    shared["wgT"] = np.ascontiguousarray(i["L0_attn_a_Wg"], np.float32)
    shared["woT"] = np.ascontiguousarray(i["L0_attn_a_Wo"], np.float32)
    wb_h = np.zeros((128, 64), np.float32)
    cvw_h = np.zeros((128, 96), np.float32)
    bia_h = np.zeros((128, 32), np.float32)
    Wb = np.asarray(i["L0_attn_a_Wb"], np.float32)
    for c in range(8):
        wb_h[:, c * 8:(c + 1) * 8] = Wb[c * 128:(c + 1) * 128, :]
        for ti, cw in enumerate(("cqw", "ckw", "cvw")):
            w = np.asarray(i["L0_attn_a_" + cw], np.float32)
            for j in range(4):
                cvw_h[:, ti * 32 + c * 4 + j] = w[c * 128:(c + 1) * 128, j]
        for g, bv in enumerate(("ba", "cqb", "ckb", "cvb")):
            b = np.asarray(i["L0_attn_a_" + bv], np.float32)
            bia_h[:, g * 8 + c] = b[c * 128:(c + 1) * 128]
    shared["wb_h"] = wb_h
    shared["cvw_h"] = cvw_h
    shared["bia_h"] = bia_h
    shared["bb1_h"] = np.ascontiguousarray(np.asarray(i["L0_attn_a_bb"], np.float32).reshape(8, 1))
    shared["lnr_h"] = np.concatenate([np.asarray(i["L0_attn_a_ln_g"], np.float32),
                                      np.asarray(i["L0_attn_a_ln_b"], np.float32)]).reshape(1, 256)
    md = np.zeros((128, 128), np.float32)
    md2 = np.zeros((128, 128), np.float32)
    for h in range(8):
        for a_ in range(16):
            for t in range(16):
                if a_ < t:
                    md[h * 16 + a_, h * 16 + t] = -1
                if a_ <= t:
                    md2[h * 16 + a_, h * 16 + t] = 1
    shared["mdiag_h"] = md.astype(_BF)
    shared["mdiag2_h"] = md2.astype(_BF)
    erow_h = np.zeros((8, 1024), np.float32)
    for h in range(8):
        erow_h[h, h * 128:(h + 1) * 128] = 1
    shared["erow_h"] = erow_h.astype(_BF)
    shared["identbf_h"] = np.eye(128, dtype=_BF)
    shared["wupT"] = np.ascontiguousarray(np.asarray(i["L0_ffn_W_up"], np.float32).T)
    shared["wdnT"] = np.ascontiguousarray(np.asarray(i["L0_ffn_W_down"], np.float32).T)
    shared["pqT"] = np.ascontiguousarray(np.asarray(i["L0_ffn_W_q"], np.float32).T)
    keys = np.asarray(i["L0_ffn_keys"], np.float32)
    shared["keysT"] = np.ascontiguousarray(keys.transpose(3, 2, 0, 1).reshape(64, 512))
    shared["deT"] = np.ascontiguousarray(np.asarray(i["L0_ffn_down_embed"], np.float32).T)
    shared["ue"] = np.ascontiguousarray(np.asarray(i["L0_ffn_up_embed"], np.float32))
    shared["abqwT"] = np.ascontiguousarray(np.asarray(i["L0_attn_b_q_w"], np.float32).T)
    shared["abkwT"] = np.ascontiguousarray(np.asarray(i["L0_attn_b_k_w"], np.float32).T)
    shared["abvwT"] = np.ascontiguousarray(np.asarray(i["L0_attn_b_v_w"], np.float32).T)
    shared["abowT"] = np.ascontiguousarray(np.asarray(i["L0_attn_b_o_w"], np.float32).T)
    shared["qb8"] = np.ascontiguousarray(np.asarray(i["L0_attn_b_q_b"], np.float32).reshape(8, 128).T)
    shared["kb8"] = np.ascontiguousarray(np.asarray(i["L0_attn_b_k_b"], np.float32).reshape(8, 128).T)
    shared["ob8"] = np.ascontiguousarray(np.asarray(i["L0_attn_b_o_b"], np.float32).reshape(8, 128).T)
    shared["vb_row"] = np.ascontiguousarray(np.asarray(i["L0_attn_b_v_b"], np.float32).reshape(1, 1024))
    base = np.full((128, 384), -1e30, np.float32)
    for q in range(128):
        base[q, q:q + 257] = 0.0
    mk0 = []
    for qt in range(3):
        m = base.copy()
        cut = max(0, 256 - qt * 128)
        m[:, 0:cut] = -1e30
        mk0.append(m)
    maskc0 = np.concatenate(mk0, 1).astype(_BF)
    maskc1 = np.concatenate([base, base, base], 1).astype(_BF)
    x = np.asarray(i["L0_attn_a_x"], np.float32)
    in_maps = []
    for core in range(8):
        b, hf = core // 2, core % 2
        xbT = np.ascontiguousarray(x[b].T)
        if hf == 0:
            xe = np.concatenate([np.zeros((1024, 256), np.float32), xbT[:, 0:2048]], 1)
        else:
            xe = np.ascontiguousarray(xbT[:, 1792:4096])
        m = dict(shared)
        m["xT"] = xbT
        m["xe"] = xe
        m["selc"] = np.tile(np.array([[1.0 - hf, float(hf)]], np.float32), (128, 1))
        m["maskc"] = maskc0 if hf == 0 else maskc1
        in_maps.append(m)
    return in_maps


def kernel(**inputs):
    if not _NC_CACHE:
        _NC_CACHE.append(_build_full())
    nc = _NC_CACHE[0]
    in_maps = _host_pack(inputs)
    res = run_bass_kernel_spmd(nc, in_maps, core_ids=list(range(8)))
    out = np.zeros((4, 4096, 1024), np.float32)
    for core in range(8):
        b, hf = core // 2, core % 2
        out[b, hf * 2048:(hf + 1) * 2048, :] = np.asarray(res.results[core]["outT"]).T
    return out


# version 08
# speedup vs baseline: 1.1345x; 1.0123x over previous
import sys
sys.path.insert(0, '/opt/trn_rl_repo')
import numpy as np
from contextlib import ExitStack
import concourse.bass as bass
import concourse.mybir as mybir

F32 = mybir.dt.float32
BF16 = mybir.dt.bfloat16
U32 = mybir.dt.uint32
AF = mybir.ActivationFunctionType
ALU = mybir.AluOpType
AX = mybir.AxisListType

KTH_PRECHAIN = False
D = 1024
NCH = 8
TEXT = 2304
TFULL = 4096
NE = 4096
NI = 4096
EPS = 1e-5

class MK:
    def __init__(self, nc, es, dbg=()):
        self.nc = nc
        self.es = es
        self.dbg = set(dbg)
        self.S = lambda g: es.enter_context(g)
        self.sem = {}
        self.cnt = {}
        self.streams = {k: [] for k in ("sp", "act", "dve", "pe", "gp")}
        self.dram = {}
        self.sb = {}

    def new_sem(self, name):
        self.sem[name] = self.S(self.nc.semaphore(name))
        self.cnt[name] = 0

    def init_consts(self, values):
        self.constpool = self.sbuf("constpool", [128, max(8, len(values))], F32)
        for j, v in enumerate(values):
            ap = self.constpool[:, j:j + 1]
            self.nc.const_aps.register(F32, float(v), ap)
            self.em("dve", lambda e, ap=ap, v=v: e.memset(ap, float(v)))

    def sbuf(self, name, shape, dtype):
        t = self.S(self.nc.sbuf_tensor(name, shape, dtype))
        self.sb[name] = t
        return t

    def view(self, name, base, off_f32, cols, dtype):
        if dtype == F32:
            ap = base[:, off_f32: off_f32 + cols]
        else:
            assert dtype == BF16
            ap = base[:, off_f32: off_f32 + (cols + 1) // 2].bitcast(BF16)
        self.sb[name] = ap
        return ap

    def din(self, name, shape, dtype=F32):
        ap = self.nc.dram_tensor(name, shape, dtype, kind="ExternalInput").ap()
        self.dram[name] = ap
        return ap

    def dout(self, name, shape, dtype=F32):
        ap = self.nc.dram_tensor(name, shape, dtype, kind="ExternalOutput").ap()
        self.dram[name] = ap
        return ap

    def dscratch(self, name, shape, dtype):
        ap = self.nc.dram_tensor(name, shape, dtype, kind="Internal").ap()
        self.dram[name] = ap
        return ap

    def em(self, eng, fn):
        self.streams[eng].append(fn)

    def inc(self, sname, by):
        self.cnt[sname] += by
        return self.cnt[sname]

    def dma(self, eng, out, in_, sname):
        tgt = self.inc(sname, 16)
        sem = self.sem[sname]
        self.em(eng, lambda e: e.dma_start(out=out, in_=in_).then_inc(sem, 16))
        return tgt

    def wait(self, eng, sname, tgt):
        sem = self.sem[sname]
        self.em(eng, lambda e: e.wait_ge(sem, tgt))

    def op(self, eng, fn, sname=None, by=1):
        if sname is None:
            self.em(eng, fn)
            return None
        tgt = self.inc(sname, by)
        sem = self.sem[sname]
        def g(e, fn=fn, sem=sem, by=by):
            r = fn(e)
            r.then_inc(sem, by)
        self.em(eng, g)
        return tgt

    def barrier(self, engs=("sp", "act", "dve", "pe", "gp")):
        if "q_sp" in self.sem and "sp" in engs:
            self.wait("sp", "q_sp", self.cnt["q_sp"])
        if "q_gp" in self.sem and "gp" in engs:
            self.wait("gp", "q_gp", self.cnt["q_gp"])
        if "bar" not in self.sem:
            self.new_sem("bar")
        sem = self.sem["bar"]
        n = len(engs)
        base = self.cnt["bar"]
        self.cnt["bar"] = base + n
        tgt = base + n
        for e in engs:
            self.em(e, lambda x: x.drain().then_inc(sem, 1))
        for e in engs:
            self.em(e, lambda x: x.wait_ge(sem, tgt))

    def finalize(self):
        nc = self.nc
        with nc.Block() as block:
            @block.sync
            def _(sync):
                for fn in self.streams["sp"]:
                    fn(sync)
            @block.scalar
            def _(scalar):
                for fn in self.streams["act"]:
                    fn(scalar)
            @block.vector
            def _(vector):
                for fn in self.streams["dve"]:
                    fn(vector)
            @block.tensor
            def _(tensor):
                for fn in self.streams["pe"]:
                    fn(tensor)
            @block.gpsimd
            def _(gpsimd):
                for fn in self.streams["gp"]:
                    fn(gpsimd)



def emit_layernorm_fm(mk, T, src_chunk, xln, es, name, dst_chunk=None, ps4=None):
    if dst_chunk is None:
        dst_chunk = lambda c: xln[:, c * T:(c + 1) * T]
    nc = mk.nc
    nt = (T + 511) // 512
    if ps4 is None:
        ps_sum = es.enter_context(nc.psum_tensor(name + "_pss", [1, 1024], F32))
        ps_bc = es.enter_context(nc.psum_tensor(name + "_psb", [128, 1024], F32))
        ps4 = (ps_sum[0:1, 0:512], ps_sum[0:1, 512:1024], ps_bc[:, 0:512], ps_bc[:, 512:1024])
    psum_a, psum_b, psbc_a, psbc_b = ps4
    xsq = mk.sb["scr512a"]
    xsq2 = mk.sb["scr512b"]
    atmp = mk.sb["scr512c"]
    srow = mk.sb["srow"]
    ones_col = mk.sb["ones_col"]
    ones128 = mk.sb["ones128"]
    for it in range(nt):
        t0 = it * 512
        tw = min(512, T - t0)
        pe0 = mk.cnt["c_pe"]
        for c in range(NCH):
            src = src_chunk(c)
            buf = xsq if c % 2 == 0 else xsq2
            if c >= 2:
                mk.wait("act", "c_pe", pe0 + c - 1)
            tgt_sq = mk.op("act", lambda e, src=src, t0=t0, tw=tw, buf=buf: e.activation(
                out=buf[:, 0:tw], in_=src[:, t0:t0 + tw], func=AF.Square), "c_act")
            mk.wait("pe", "c_act", tgt_sq)
            mk.op("pe", lambda e, src=src, t0=t0, tw=tw, c=c: e.matmul(
                out=psum_a[0:1, 0:tw], lhsT=ones128[:, 0:1], rhs=src[:, t0:t0 + tw],
                start=(c == 0), stop=(c == NCH - 1)))
            mk.op("pe", lambda e, tw=tw, c=c, buf=buf: e.matmul(
                out=psum_b[0:1, 0:tw], lhsT=ones128[:, 0:1], rhs=buf[:, 0:tw],
                start=(c == 0), stop=(c == NCH - 1)), "c_pe")
        tgt_sums = mk.cnt["c_pe"]
        mk.wait("dve", "c_pe", tgt_sums)
        mean = srow[0:1, 0:512]; var = srow[0:1, 512:1024]
        rstd = srow[0:1, 1024:1536]; nmr = srow[0:1, 1536:2048]
        mk.op("dve", lambda e, tw=tw, mean=mean: e.tensor_scalar(out=mean[0:1, 0:tw], in0=psum_a[0:1, 0:tw],
              scalar1=1.0 / D, scalar2=None, op0=ALU.mult))
        mk.op("dve", lambda e, tw=tw, mean=mean: e.tensor_tensor(out=atmp[0:1, 0:tw], in0=mean[0:1, 0:tw],
              in1=mean[0:1, 0:tw], op=ALU.mult))
        mk.op("dve", lambda e, tw=tw, var=var: e.tensor_scalar(out=var[0:1, 0:tw], in0=psum_b[0:1, 0:tw],
              scalar1=1.0 / D, scalar2=None, op0=ALU.mult))
        tgt_v = mk.op("dve", lambda e, tw=tw, var=var: e.tensor_tensor(out=var[0:1, 0:tw], in0=var[0:1, 0:tw],
              in1=atmp[0:1, 0:tw], op=ALU.subtract), "c_dve")
        mk.wait("act", "c_dve", tgt_v)
        mk.op("act", lambda e, tw=tw, var=var, rstd=rstd: e.activation(out=rstd[0:1, 0:tw], in_=var[0:1, 0:tw],
              func=AF.Ln, bias=EPS, scale=1.0))
        tgt_r = mk.op("act", lambda e, tw=tw, rstd=rstd: e.activation(out=rstd[0:1, 0:tw], in_=rstd[0:1, 0:tw],
              func=AF.Exp, bias=0.0, scale=-0.5), "c_act")
        mk.wait("dve", "c_act", tgt_r)
        tgt_n = mk.op("dve", lambda e, tw=tw, mean=mean, rstd=rstd, nmr=nmr: e.scalar_tensor_tensor(
            out=nmr[0:1, 0:tw], in0=mean[0:1, 0:tw], scalar=-1.0, in1=rstd[0:1, 0:tw],
            op0=ALU.mult, op1=ALU.mult), "c_dve")
        mk.wait("pe", "c_dve", tgt_n)
        mk.op("pe", lambda e, tw=tw, rstd=rstd: e.matmul(out=psbc_a[:, 0:tw], lhsT=ones_col[0:1, :],
              rhs=rstd[0:1, 0:tw], start=True, stop=True))
        tgt_bc = mk.op("pe", lambda e, tw=tw, nmr=nmr: e.matmul(out=psbc_b[:, 0:tw], lhsT=ones_col[0:1, :],
              rhs=nmr[0:1, 0:tw], start=True, stop=True), "c_pe")
        mk.wait("dve", "c_pe", tgt_bc)
        for c in range(NCH):
            src = src_chunk(c)
            mk.op("dve", lambda e, src=src, t0=t0, tw=tw: e.tensor_tensor(
                out=atmp[:, 0:tw], in0=src[:, t0:t0 + tw], in1=psbc_a[:, 0:tw], op=ALU.mult))
            dstc = dst_chunk(c)
            mk.op("dve", lambda e, dstc=dstc, t0=t0, tw=tw: e.tensor_tensor(
                out=dstc[:, t0:t0 + tw], in0=atmp[:, 0:tw],
                in1=psbc_b[:, 0:tw], op=ALU.add), "c_dve" if c == NCH - 1 else None)
        mk.wait("pe", "c_dve", mk.cnt["c_dve"])
    return mk.cnt["c_dve"]


def emit_ffn_dense(mk, T, xln, resid_chunk, wupT_hbm, wdownT_hbm, es, name):
    nc = mk.nc
    wup = mk.sb[name + "_wup"]
    wdn = mk.sb[name + "_wdn"]
    mid = mk.sb[name + "_mid"]
    ps = [es.enter_context(nc.psum_tensor(name + "_ps%d" % j, [128, 512], F32)) for j in range(8)]
    ntt = (T + 255) // 256
    for half in range(2):
        i0 = half * 2048
        if half == 1:
            mk.wait("gp", "c_dve", mk.cnt["c_dve"])
        for c in range(NCH):
            mk.dma("gp", wup[:, c * 2048:(c + 1) * 2048], wupT_hbm[c * 128:(c + 1) * 128, i0:i0 + 2048], "q_gp")
        for ic in range(16):
            mk.dma("gp", wdn[:, ic * 1024:(ic + 1) * 1024], wdownT_hbm[i0 + ic * 128: i0 + (ic + 1) * 128, :], "q_gp")
        wtgt = mk.cnt["q_gp"]
        mk.wait("pe", "q_gp", wtgt)
        for it in range(ntt):
            t0 = it * 256
            tw = min(256, T - t0)
            act0 = mk.cnt["c_act"]
            for ic in range(16):
                pm = ps[6 + ic % 2]
                if ic >= 2:
                    mk.wait("pe", "c_act", act0 + ic - 1)
                for k in range(NCH):
                    mk.op("pe", lambda e, k=k, ic=ic, t0=t0, tw=tw, pm=pm: e.matmul(
                        out=pm[:, 0:tw], lhsT=wup[:, k * 2048 + ic * 128: k * 2048 + ic * 128 + 128],
                        rhs=xln[:, k * T + t0: k * T + t0 + tw],
                        start=(k == 0), stop=(k == NCH - 1)), "c_pe" if k == NCH - 1 else None)
                mk.wait("act", "c_pe", mk.cnt["c_pe"])
                mk.op("act", lambda e, pm=pm, ic=ic, tw=tw: e.activation(
                    out=mid[:, ic * 256: ic * 256 + tw], in_=pm[:, 0:tw], func=AF.Silu), "c_act")
            mk.wait("pe", "c_act", mk.cnt["c_act"])
            for ic in range(16):
                for oc in range(NCH):
                    mk.op("pe", lambda e, oc=oc, ic=ic, t0=t0, tw=tw: e.matmul(
                        out=ps[oc][:, 0:tw], lhsT=wdn[:, ic * 1024 + oc * 128: ic * 1024 + oc * 128 + 128],
                        rhs=mid[:, ic * 256: ic * 256 + tw],
                        start=(ic == 0), stop=(ic == 15)),
                        "c_pe" if (ic == 15 and oc == NCH - 1) else None)
            tgt_out = mk.cnt["c_pe"]
            mk.wait("dve", "c_pe", tgt_out)
            for oc in range(NCH):
                dst = resid_chunk(oc)
                mk.op("dve", lambda e, oc=oc, dst=dst, t0=t0, tw=tw: e.tensor_tensor(
                    out=dst[:, t0:t0 + tw], in0=ps[oc][:, 0:tw], in1=dst[:, t0:t0 + tw], op=ALU.add),
                    "c_dve" if oc == NCH - 1 else None)
            mk.wait("pe", "c_dve", mk.cnt["c_dve"])
    return mk.cnt["c_dve"]


def emit_peer(mk, T, xln, resid_chunk, hbm, es, name, GRP=3, stages="123"):
    nc = mk.nc
    qfm = mk.sb[name + "_qfm"]
    sims = mk.sb[name + "_sims"]
    exa = mk.sb[name + "_exa"]
    eya = mk.sb[name + "_eya"]
    taua = mk.sb[name + "_tau"]
    srow = mk.sb[name + "_srow"]
    s8 = mk.sb[name + "_s8"]
    det = mk.sb[name + "_det"]
    ue2 = mk.sb[name + "_ue"]
    dsb = mk.sb[name + "_dsb"]
    dsb2 = mk.sb[name + "_dsb2"]
    pds = mk.sb[name + "_pds"]
    pd2 = mk.sb[name + "_pd2"]
    wtmp = mk.sb[name + "_wtmp"]
    wacc = mk.sb[name + "_wacc"]
    wtw = mk.sb[name + "_wtw"]
    keys = mk.sb[name + "_keys"]
    wq = mk.sb[name + "_wq"]
    ident = mk.sb["identbf"]
    nt = (T + 127) // 128
    ps_d = [es.enter_context(nc.psum_tensor(name + "_psd%d" % j, [128, 512], F32)) for j in range(4)]

    mk.dma("gp", keys[0:64, :], hbm["keysT"], "q_gp")
    mk.dma("gp", keys[64:128, :], hbm["keysT"], "q_gp")
    for c in range(NCH):
        mk.dma("gp", wq[:, c * 512:(c + 1) * 512], hbm["wqT"][c * 128:(c + 1) * 128, :], "q_gp")
    mk.wait("pe", "q_gp", mk.cnt["q_gp"])
    with ExitStack() as es1:
        ps_q = es1.enter_context(nc.psum_tensor(name + "_psq", [128, 512], F32))
        ps_s = es1.enter_context(nc.psum_tensor(name + "_pssim", [128, 512], F32))
        for it5 in range((T + 511) // 512):
            t0g = it5 * 512
            twg = min(512, T - t0g)
            for ph in range(8):
                p, h = ph // 4, ph % 4
                boff = p * 256 + h * 64
                for k in range(NCH):
                    mk.op("pe", lambda e, boff=boff, k=k, t0g=t0g, twg=twg: e.matmul(
                        out=ps_q[0:64, 0:twg], lhsT=wq[:, k * 512 + boff: k * 512 + boff + 64],
                        rhs=xln[:, k * T + t0g:k * T + t0g + twg],
                        start=(k == 0), stop=(k == NCH - 1)), "c_pe" if k == NCH - 1 else None)
                mk.wait("act", "c_pe", mk.cnt["c_pe"])
                tgt = mk.op("act", lambda e, ph=ph, twg=twg: e.activation(
                    out=qfm[0:64, ph * 512: ph * 512 + twg], in_=ps_q[0:64, 0:twg], func=AF.Copy), "c_act")
                mk.wait("pe", "c_act", tgt)
            if stages == "q":
                continue
            for itl in range((twg + 127) // 128):
                it = it5 * 4 + itl
                t0 = itl * 128
                tw = min(128, twg - t0)
                for ph in range(8):
                    mk.op("pe", lambda e, t0=t0, tw=tw, ph=ph: e.matmul(
                        out=ps_s[0:tw, ph * 64:(ph + 1) * 64], lhsT=qfm[0:64, ph * 512 + t0: ph * 512 + t0 + tw],
                        rhs=keys[0:64, ph * 64:(ph + 1) * 64], start=True, stop=True),
                        "c_pe" if ph == 7 else None)
                mk.wait("act", "c_pe", mk.cnt["c_pe"])
                mk.wait("act", "c_dve", mk.cnt["c_dve"])
                tgt = mk.op("act", lambda e, tw=tw: e.activation(out=sims[0:tw, :], in_=ps_s[0:tw, :], func=AF.Copy), "c_act")
                mk.wait("dve", "c_act", tgt)
                if stages == "s":
                    continue
                for h in range(4):
                    sx = sims[:, h * 64:(h + 1) * 64]
                    sy = sims[:, 256 + h * 64: 256 + (h + 1) * 64]
                    vx = s8[:, 0:8]; vy = s8[:, 8:16]; g8 = s8[:, 16:24]
                    negM = s8[:, 24:25]; z = s8[:, 25:26]; negK = s8[:, 26:27]
                    ex8 = s8[:, 32:40]; ey8 = s8[:, 40:48]; p8 = s8[:, 48:56]
                    mk.op("dve", lambda e, sx=sx, vx=vx: e.max(out=vx, in_=sx))
                    mk.op("dve", lambda e, sy=sy, vy=vy: e.max(out=vy, in_=sy))
                    mk.op("dve", lambda e: e.drain())
                    mk.op("dve", lambda e, vx=vx, vy=vy: e.scalar_tensor_tensor(
                        out=srow[:, 0:64], in0=vx.rearrange("p (a o) -> p a o", o=1).broadcast_to([128, 8, 8]),
                        scalar=1.0, in1=vy.rearrange("p (o a) -> p o a", o=1).broadcast_to([128, 8, 8]),
                        op0=ALU.mult, op1=ALU.add))
                    mk.op("dve", lambda e, g8=g8: e.max(out=g8, in_=srow[:, 0:64]))
                    mk.op("dve", lambda e: e.drain())
                    mk.op("dve", lambda e, g8=g8, negM=negM: e.tensor_scalar(
                        out=negM, in0=g8[:, 0:1], scalar1=-1.0, scalar2=None, op0=ALU.mult), "c_dve")
                    mk.wait("act", "c_dve", mk.cnt["c_dve"])
                    tgt0 = mk.op("act", lambda e, g8=g8, negM=negM: e.activation(
                        out=srow[:, 0:8], in_=g8, func=AF.Exp, bias=negM, scale=1.0), "c_act")
                    mk.wait("dve", "c_act", tgt0)
                    tgtz = mk.op("dve", lambda e, z=z: e.reduce_sum(
                        out=z, in_=srow[:, 0:8], axis=AX.X), "c_dve")
                    mk.wait("act", "c_dve", tgtz)
                    tgt1 = mk.op("act", lambda e, z=z: e.activation(out=srow[:, 8:9], in_=z, func=AF.Ln), "c_act")
                    mk.wait("dve", "c_act", tgt1)
                    tgt2 = mk.op("dve", lambda e, negM=negM, negK=negK: e.tensor_tensor(
                        out=negK, in0=negM, in1=srow[:, 8:9], op=ALU.subtract), "c_dve")
                    mk.wait("act", "c_dve", tgt2)
                    exs = exa[:, it * 256 + h * 64: it * 256 + (h + 1) * 64]
                    eys = eya[:, it * 256 + h * 64: it * 256 + (h + 1) * 64]
                    mk.op("act", lambda e, sx=sx, negK=negK, exs=exs: e.activation(
                        out=exs, in_=sx, func=AF.Exp, bias=negK, scale=1.0), "c_act")
                    tgt3 = mk.op("act", lambda e, sy=sy, eys=eys: e.activation(
                        out=eys, in_=sy, func=AF.Exp), "c_act")
                    mk.wait("dve", "c_act", tgt3)
                    mk.op("dve", lambda e, exs=exs, ex8=ex8: e.max(out=ex8, in_=exs))
                    mk.op("dve", lambda e, eys=eys, ey8=ey8: e.max(out=ey8, in_=eys))
                    mk.op("dve", lambda e: e.drain())
                    mk.op("dve", lambda e, ex8=ex8, ey8=ey8: e.scalar_tensor_tensor(
                        out=srow[:, 0:64], in0=ex8.rearrange("p (a o) -> p a o", o=1).broadcast_to([128, 8, 8]),
                        scalar=1.0, in1=ey8.rearrange("p (o a) -> p o a", o=1).broadcast_to([128, 8, 8]),
                        op0=ALU.mult, op1=ALU.mult))
                    mk.op("dve", lambda e, p8=p8: e.max(out=p8, in_=srow[:, 0:64]))
                    mk.op("dve", lambda e: e.drain())
                    mk.op("dve", lambda e, p8=p8, it=it, h=h: e.tensor_copy(
                        out=taua[:, it * 4 + h: it * 4 + h + 1], in_=p8[:, 7:8]), "c_dve")
                mk.wait("pe", "c_act", tgt)
    if stages in ("q", "s") or "2" not in stages:
        return mk.cnt["c_dve"]
    ngr = (nt + GRP - 1) // GRP
    pd2b = pd2.bitcast(BF16)
    sp2 = [pd2b[:, 0:1024], pd2b[:, 1024:2048]]
    evtag = {}
    mbtag = {}
    for g in range(ngr):
        j0 = g * GRP
        jn = min(GRP, nt - j0)
        gw = sum(min(128, T - (j0 + j) * 128) for j in range(jn))
        with ExitStack() as es2:
            ps_tr = es2.enter_context(nc.psum_tensor(name + "_ptr%d" % g, [128, 1024], BF16))
            wtv = wtw[:, 0:32 * gw].rearrange("p (a b) -> p a b", a=32)
            for eq in range(4):
                e0 = eq * 1024
                mk.wait("gp", "c_pe", mk.cnt["c_pe"])
                for k in range(NCH):
                    mk.dma("gp", det[:, k * 1024:(k + 1) * 1024],
                           hbm["deT"][k * 128:(k + 1) * 128, e0:e0 + 1024], "q_gp")
                mk.wait("pe", "q_gp", mk.cnt["q_gp"])
                for j in range(jn):
                    it = j0 + j
                    t0 = it * 128
                    tw = min(128, T - t0)
                    par = (g * 4 * jn + eq * jn + j) % 2
                    pda, pdb = ps_d[par * 2], ps_d[par * 2 + 1]
                    dsl = dsb if par == 0 else dsb2
                    if par in evtag:
                        mk.wait("pe", "c_act", evtag[par])
                    for ns in range(2):
                        pm = pda if ns == 0 else pdb
                        for k in range(NCH):
                            mk.op("pe", lambda e, k=k, pm=pm, t0=t0, tw=tw, ns=ns: e.matmul(
                                out=pm[0:tw, :], lhsT=xln[:, k * T + t0: k * T + t0 + tw],
                                rhs=det[:, k * 1024 + ns * 512: k * 1024 + (ns + 1) * 512],
                                start=(k == 0), stop=(k == NCH - 1)),
                                "c_pe" if k == NCH - 1 else None)
                    mk.wait("act", "c_pe", mk.cnt["c_pe"])
                    if par in mbtag:
                        mk.wait("act", "c_dve", mbtag[par])
                    for ns in range(2):
                        pm = pda if ns == 0 else pdb
                        mk.op("act", lambda e, pm=pm, ns=ns, tw=tw, dsl=dsl: e.activation(
                            out=dsl[0:tw, ns * 512:(ns + 1) * 512], in_=pm[0:tw, :], func=AF.Copy),
                            "c_act" if ns == 1 else None)
                    evtag[par] = mk.cnt["c_act"]
                    tgt_d = mk.cnt["c_act"]
                    t_sil = [None] * 4
                    t_add = [None] * 4
                    for h in range(4):
                        exs = exa[:, it * 256 + h * 64 + eq * 16: it * 256 + h * 64 + eq * 16 + 16]
                        eys = eya[:, it * 256 + h * 64: it * 256 + (h + 1) * 64]
                        tau = taua[:, it * 4 + h: it * 4 + h + 1]
                        mk.op("dve", lambda e, exs=exs, eys=eys, tw=tw: e.tensor_tensor(
                            out=pds[0:tw, :],
                            in0=exs[0:tw].rearrange("p (a o) -> p a o", o=1).broadcast_to([tw, 16, 64]),
                            in1=eys[0:tw].rearrange("p (o a) -> p o a", o=1).broadcast_to([tw, 16, 64]),
                            op=ALU.mult))
                        mk.op("dve", lambda e, tau=tau, tw=tw: e.scalar_tensor_tensor(
                            out=pds[0:tw, :], in0=pds[0:tw, :], scalar=tau, in1=pds[0:tw, :],
                            op0=ALU.is_ge, op1=ALU.mult))
                        if h == 0:
                            mk.wait("dve", "c_act", tgt_d)
                        else:
                            mk.wait("dve", "c_act", t_sil[h - 1])
                        tgt_mb = mk.op("dve", lambda e, tw=tw, dsl=dsl: e.tensor_tensor(
                            out=wtmp[0:tw, :], in0=pds[0:tw, :], in1=dsl[0:tw, :], op=ALU.mult), "c_dve")
                        if h >= 1:
                            mk.wait("dve", "c_act", t_sil[h - 1])
                            t_add[h - 1] = mk.op("dve", lambda e, h=h, tw=tw: e.tensor_tensor(
                                out=wacc[0:tw, :], in0=wacc[0:tw, :], in1=sp2[(h - 1) % 2][0:tw, :],
                                op=ALU.add), "c_dve")
                        mk.wait("act", "c_dve", tgt_mb)
                        if h >= 2 and t_add[h - 2] is not None:
                            mk.wait("act", "c_dve", t_add[h - 2])
                        dst = wacc if h == 0 else sp2[h % 2]
                        t_sil[h] = mk.op("act", lambda e, dst=dst, tw=tw: e.activation(
                            out=dst[0:tw, :], in_=wtmp[0:tw, :], func=AF.Silu), "c_act")
                        if h == 3:
                            mk.wait("dve", "c_act", t_sil[3])
                            t_add[3] = mk.op("dve", lambda e, tw=tw: e.tensor_tensor(
                                out=wacc[0:tw, :], in0=wacc[0:tw, :], in1=sp2[1][0:tw, :],
                                op=ALU.add), "c_dve")
                    mbtag[par] = t_add[3]
                    mk.wait("pe", "c_dve", mk.cnt["c_dve"])
                    mk.wait("pe", "c_act", mk.cnt["c_act"])
                    gcol0 = sum(min(128, T - (j0 + jj) * 128) for jj in range(j))
                    for sl in range(8):
                        mk.op("pe", lambda e, sl=sl, tw=tw: e.transpose(
                            out=ps_tr[:, sl * 128: sl * 128 + tw], in_=wacc[:, sl * 128:(sl + 1) * 128],
                            identity=ident[:, 0:tw]), "c_pe" if sl == 7 else None)
                    mk.wait("act", "c_pe", mk.cnt["c_pe"])
                    mk.op("act", lambda e, eq=eq, gcol0=gcol0, tw=tw, wtv=wtv: e.activation(
                        out=wtv[:, eq * 8: eq * 8 + 8, gcol0:gcol0 + tw],
                        in_=ps_tr[:, 0:1024].rearrange("p (a b) -> p a b", a=8)[:, :, 0:tw],
                        func=AF.Copy), "c_act")
                    mk.wait("pe", "c_act", mk.cnt["c_act"])
        if "3" not in stages:
            continue
        with ExitStack() as es3:
            ps_o = [es3.enter_context(nc.psum_tensor(name + "_pso%d_%d" % (g, j), [128, 512], F32)) for j in range(4)]
            mk.wait("pe", "c_act", mk.cnt["c_act"])
            pe_tgts = {}
            for ec in range(32):
                buf = ue2[:, (ec % 2) * 1024:(ec % 2) * 1024 + 1024]
                if ec >= 2:
                    mk.wait("gp", "c_pe", pe_tgts[ec - 2])
                mk.dma("gp", buf, hbm["ue"][ec * 128:(ec + 1) * 128, :], "q_gp")
                mk.wait("pe", "q_gp", mk.cnt["q_gp"])
                for oc in range(8):
                    psu = ps_d[oc] if oc < 4 else ps_o[oc - 4]
                    mk.op("pe", lambda e, ec=ec, oc=oc, gw=gw, buf=buf, psu=psu: e.matmul(
                        out=psu[:, 0:gw], lhsT=buf[:, oc * 128:(oc + 1) * 128],
                        rhs=wtw[:, ec * gw:(ec + 1) * gw],
                        start=(ec == 0), stop=(ec == 31)),
                        "c_pe" if oc == 7 else None)
                pe_tgts[ec] = mk.cnt["c_pe"]
            mk.wait("dve", "c_pe", mk.cnt["c_pe"])
            for oc in range(8):
                psu = ps_d[oc] if oc < 4 else ps_o[oc - 4]
                dst = resid_chunk(oc)
                col = 0
                for j in range(jn):
                    it = j0 + j
                    t0 = it * 128
                    tw = min(128, T - t0)
                    mk.op("dve", lambda e, dst=dst, t0=t0, tw=tw, col=col, psu=psu: e.tensor_tensor(
                        out=dst[:, t0:t0 + tw], in0=psu[:, col:col + tw],
                        in1=dst[:, t0:t0 + tw], op=ALU.add), "c_dve")
                    col += tw
            mk.wait("pe", "c_dve", mk.cnt["c_dve"])
            mk.wait("gp", "c_dve", mk.cnt["c_dve"])
    return mk.cnt["c_dve"]

def emit_attnb(mk, T, TQ0, xln, resid_chunk, hbm, maskc, es, name):
    nc = mk.nc
    wbuf = mk.sb[name + "_wbuf"]
    kw = vw = qw = wbuf
    ow = wbuf
    kroll = mk.sb[name + "_kroll"]
    vroll = mk.sb[name + "_vroll"]
    qbuf = mk.sb[name + "_qbuf"]
    att = mk.sb[name + "_att"]
    exps = mk.sb[name + "_exps"]
    pbf = mk.sb[name + "_pbf"]
    pbft = mk.sb[name + "_pbft"]
    red = mk.sb[name + "_red"]
    bq = mk.sb[name + "_bq"]; bk = mk.sb[name + "_bk"]; bo = mk.sb[name + "_bo"]
    vbb = mk.sb[name + "_vbb"]
    vbr = mk.sb[name + "_vbr"]
    mkc = mk.sb[name + "_mkc"]
    ident = mk.sb["identbf"]
    ones_col = mk.sb["ones_col"]
    NT = T // 128
    NQ = (T - TQ0) // 128
    L_tgte = [0]
    L_pbft = [None, None]
    ps_sc = es.enter_context(nc.psum_tensor(name + "_psc", [128, 2048], F32))
    ps_tr = [es.enter_context(nc.psum_tensor(name + "_ptr%d" % j, [128, 512], BF16)) for j in range(2)]
    ps_at = [es.enter_context(nc.psum_tensor(name + "_pat%d" % j, [128, 512], F32)) for j in range(2)]

    def load_w(which, half=None):
        mk.wait("gp", "c_pe", mk.cnt["c_pe"])
        if half is None:
            for c in range(NCH):
                mk.dma("gp", wbuf[:, c * 1024:(c + 1) * 1024], hbm[which][c * 128:(c + 1) * 128, :], "q_gp")
        else:
            for c in range(NCH):
                mk.dma("gp", wbuf[:, c * 512:(c + 1) * 512],
                       hbm[which][c * 128:(c + 1) * 128, half * 512:(half + 1) * 512], "q_gp")
        mk.wait("pe", "q_gp", mk.cnt["q_gp"])
    mk.dma("gp", bq[:, :], hbm["qb8"], "q_gp")
    mk.dma("gp", bk[:, :], hbm["kb8"], "q_gp")
    mk.dma("gp", bo[:, :], hbm["ob8"], "q_gp")
    mk.dma("gp", vbr[:, :], hbm["vb_row"], "q_gp")
    mk.dma("gp", mkc[:, :], maskc, "q_gp")
    mk.wait("pe", "q_gp", mk.cnt["q_gp"])
    mk.op("pe", lambda e: e.matmul(out=ps_at[0][:, 0:512], lhsT=ones_col[0:1, :], rhs=vbr[0:1, 0:512],
          start=True, stop=True))
    mk.op("pe", lambda e: e.matmul(out=ps_at[1][:, 0:512], lhsT=ones_col[0:1, :], rhs=vbr[0:1, 512:1024],
          start=True, stop=True), "c_pe")
    mk.wait("act", "c_pe", mk.cnt["c_pe"])
    mk.op("act", lambda e: e.activation(out=vbb[:, 0:512], in_=ps_at[0][:, 0:512], func=AF.Copy))
    tgtv = mk.op("act", lambda e: e.activation(out=vbb[:, 512:1024], in_=ps_at[1][:, 0:512], func=AF.Copy), "c_act")
    mk.wait("pe", "c_act", tgtv)

    def proj_k(tiles):
        t0 = tiles[0] * 128
        tw = 128 * len(tiles)
        load_w("kwT")
        for oc in range(NCH):
            for k in range(NCH):
                mk.op("pe", lambda e, oc=oc, k=k, t0=t0, tw=tw: e.matmul(
                    out=ps_sc[:, 0:tw], lhsT=kw[:, k * 1024 + oc * 128: k * 1024 + oc * 128 + 128],
                    rhs=xln[:, k * T + t0: k * T + t0 + tw],
                    start=(k == 0), stop=(k == NCH - 1)), "c_pe" if k == NCH - 1 else None)
            mk.wait("act", "c_pe", mk.cnt["c_pe"])
            tgt = None
            for j, tj in enumerate(tiles):
                sl = tj % 7
                tgt = mk.op("act", lambda e, oc=oc, j=j, sl=sl: e.activation(
                    out=kroll[:, oc * 896 + sl * 128: oc * 896 + sl * 128 + 128],
                    in_=ps_sc[:, j * 128:(j + 1) * 128], func=AF.Identity,
                    bias=bk[:, oc:oc + 1], scale=1.0), "c_act")
            mk.wait("pe", "c_act", tgt)

    def proj_v(tiles):
        load_w("vwT")
        for tj in tiles:
            sl = tj % 7
            for nh in range(2):
                for k in range(NCH):
                    mk.op("pe", lambda e, tj=tj, nh=nh, k=k: e.matmul(
                        out=ps_sc[:, nh * 512:(nh + 1) * 512], lhsT=xln[:, k * T + tj * 128: k * T + tj * 128 + 128],
                        rhs=vw[:, k * 1024 + nh * 512: k * 1024 + (nh + 1) * 512],
                        start=(k == 0), stop=(k == NCH - 1)), "c_pe" if k == NCH - 1 else None)
            mk.wait("dve", "c_pe", mk.cnt["c_pe"])
            tgt = mk.op("dve", lambda e, sl=sl: e.tensor_tensor(
                out=vroll[:, sl * 1024:(sl + 1) * 1024], in0=ps_sc[:, 0:1024], in1=vbb[:, :], op=ALU.add), "c_dve")
            mk.wait("pe", "c_dve", tgt)

    proj_k([0, 1, 2])
    proj_v([0, 1, 2])
    for g in range((NQ + 3) // 4):
        nq = min(4, NQ - g * 4)
        newt = [t for t in range(g * 4 + 3, min(g * 4 + 7, NT))]
        if newt:
            proj_k(newt)
            proj_v(newt)
        q0 = TQ0 + g * 512
        qn = 128 * nq
        load_w("qwT")
        for oc in range(NCH):
            for k in range(NCH):
                mk.op("pe", lambda e, oc=oc, k=k, q0=q0, qn=qn: e.matmul(
                    out=ps_sc[:, 0:qn], lhsT=qw[:, k * 1024 + oc * 128: k * 1024 + oc * 128 + 128],
                    rhs=xln[:, k * T + q0: k * T + q0 + qn],
                    start=(k == 0), stop=(k == NCH - 1)), "c_pe" if k == NCH - 1 else None)
            mk.wait("act", "c_pe", mk.cnt["c_pe"])
            tgt = mk.op("act", lambda e, oc=oc, qn=qn: e.activation(
                out=qbuf[:, oc * 512: oc * 512 + qn], in_=ps_sc[:, 0:qn], func=AF.Identity,
                bias=bq[:, oc:oc + 1], scale=1.0), "c_act")
            mk.wait("pe", "c_act", tgt)
        for lq in range(nq):
            qt = g * 4 + lq
            mi = min(qt, 2)
            for hg in range(4):
                mk.wait("pe", "c_act", L_tgte[0])
                for hh in range(4):
                    h = hg * 4 + hh
                    ch, ro = h // 2, (h % 2) * 64
                    for c3 in range(3):
                        sl = (qt + c3) % 7
                        mk.op("pe", lambda e, hh=hh, ch=ch, ro=ro, lq=lq, sl=sl, c3=c3: e.matmul(
                            out=ps_sc[0:128, hh * 512 + c3 * 128: hh * 512 + (c3 + 1) * 128],
                            lhsT=qbuf[ro:ro + 64, ch * 512 + lq * 128: ch * 512 + lq * 128 + 128],
                            rhs=kroll[ro:ro + 64, ch * 896 + sl * 128: ch * 896 + sl * 128 + 128],
                            start=(c3 == 0), stop=False))
                    mk.op("pe", lambda e, hh=hh, mi=mi: e.matmul(
                        out=ps_sc[0:128, hh * 512: hh * 512 + 384], lhsT=ident[:, :],
                        rhs=mkc[:, mi * 384:(mi + 1) * 384], start=False, stop=True),
                        "c_pe" if hh == 3 else None)
                mk.wait("dve", "c_pe", mk.cnt["c_pe"])
                scv = ps_sc[:, 0:2048].rearrange("p (a b) -> p a b", a=4)[:, :, 0:384]
                mk.op("dve", lambda e, scv=scv: e.reduce_max(out=red[:, 0:4], in_=scv, axis=AX.X))
                mk.op("dve", lambda e: e.drain())
                tgtn = mk.op("dve", lambda e: e.tensor_scalar(
                    out=red[:, 4:8], in0=red[:, 0:4], scalar1=-0.125, scalar2=None, op0=ALU.mult), "c_dve")
                mk.wait("act", "c_dve", tgtn)
                tgte = None
                for hh in range(4):
                    tgte = mk.op("act", lambda e, hh=hh: e.activation(
                        out=exps[:, hh * 384:(hh + 1) * 384], in_=ps_sc[:, hh * 512: hh * 512 + 384],
                        func=AF.Exp, bias=red[:, 4 + hh:5 + hh], scale=0.125,
                        accum_out=red[:, 8 + hh:9 + hh]), "c_act")
                L_tgte[0] = tgte
                mk.wait("dve", "c_act", tgte)
                mk.op("dve", lambda e: e.reciprocal(out=red[:, 12:16], in_=red[:, 8:12]))
                mk.op("dve", lambda e: e.drain())
                tgtp = None
                for hh in range(4):
                    tgtp = mk.op("dve", lambda e, hh=hh: e.tensor_scalar(
                        out=pbf[:, hh * 384:(hh + 1) * 384], in0=exps[:, hh * 384:(hh + 1) * 384],
                        scalar1=red[:, 12 + hh:13 + hh], scalar2=None, op0=ALU.mult), "c_dve")
                mk.wait("pe", "c_dve", tgtp)
                for hh in range(4):
                    h = hg * 4 + hh
                    ro = (h % 2) * 64
                    pp = hh % 2
                    if L_pbft[pp]:
                        mk.wait("pe", "c_act", L_pbft[pp])
                    for c3 in range(3):
                        mk.op("pe", lambda e, hh=hh, pp=pp, c3=c3: e.transpose(
                            out=ps_tr[pp][:, c3 * 128:(c3 + 1) * 128],
                            in_=pbf[:, hh * 384 + c3 * 128: hh * 384 + (c3 + 1) * 128],
                            identity=ident[:, :]), "c_pe" if c3 == 2 else None)
                    mk.wait("act", "c_pe", mk.cnt["c_pe"])
                    tgt = mk.op("act", lambda e, pp=pp: e.activation(
                        out=pbft[:, pp * 384:(pp + 1) * 384], in_=ps_tr[pp][:, 0:384], func=AF.Copy), "c_act")
                    L_pbft[pp] = tgt
                    mk.wait("pe", "c_act", tgt)
                    for c3 in range(3):
                        sl = (qt + c3) % 7
                        mk.op("pe", lambda e, h=h, hh=hh, pp=pp, sl=sl, c3=c3, ro=ro: e.matmul(
                            out=ps_at[(h // 2) % 2][ro:ro + 64, 0:128],
                            lhsT=vroll[:, sl * 1024 + h * 64: sl * 1024 + h * 64 + 64],
                            rhs=pbft[:, pp * 384 + c3 * 128: pp * 384 + (c3 + 1) * 128],
                            start=(c3 == 0), stop=(c3 == 2), tile_position=(0, ro)),
                            "c_pe" if c3 == 2 else None)
                    if h % 2 == 1:
                        mk.wait("act", "c_pe", mk.cnt["c_pe"])
                        tgt = mk.op("act", lambda e, h=h, lq=lq: e.activation(
                            out=att[:, (h // 2) * 512 + lq * 128: (h // 2) * 512 + lq * 128 + 128],
                            in_=ps_at[(h // 2) % 2][:, 0:128], func=AF.Copy), "c_act")
                        mk.wait("pe", "c_act", tgt)
        mk.wait("pe", "c_act", mk.cnt["c_act"])
        for half in range(2):
            load_w("owT", half=half)
            for ic in range(NCH):
                for oc in range(4):
                    occ = half * 4 + oc
                    mk.op("pe", lambda e, ic=ic, oc=oc, occ=occ, qn=qn: e.matmul(
                        out=ps_sc[:, oc * 512: oc * 512 + qn], lhsT=ow[:, ic * 512 + oc * 128: ic * 512 + oc * 128 + 128],
                        rhs=att[:, ic * 512: ic * 512 + qn],
                        start=(ic == 0), stop=(ic == NCH - 1)),
                        "c_pe" if (ic == NCH - 1 and oc == 3) else None)
            mk.wait("dve", "c_pe", mk.cnt["c_pe"])
            for oc in range(4):
                occ = half * 4 + oc
                dst = resid_chunk(occ)
                mk.op("dve", lambda e, oc=oc, occ=occ, dst=dst, q0=q0, qn=qn: e.scalar_tensor_tensor(
                    out=dst[:, q0:q0 + qn], in0=ps_sc[:, oc * 512: oc * 512 + qn],
                    scalar=bo[:, occ:occ + 1], in1=dst[:, q0:q0 + qn],
                    op0=ALU.add, op1=ALU.add), "c_dve" if oc == 3 else None)
            mk.wait("pe", "c_dve", mk.cnt["c_dve"])
    return mk.cnt["c_dve"]


def emit_attna(mk, TF, xT_hbm, hbm, scr, es, name):
    nc = mk.nc
    BL = 512
    NB = TF // BL
    S = mk.sb
    xab = S[name + "_xab"]; xlnb = S[name + "_xln"]; pre = S[name + "_pre"]
    qs = S[name + "_qs"]; ks = S[name + "_ks"]; vs = S[name + "_vs"]
    acp = S[name + "_acp"]
    afm = S[name + "_afm"]; lam = S[name + "_lam"]
    gam = S[name + "_gam"]; ivg = S[name + "_ivg"]
    cb = S[name + "_cb"]; bt = S[name + "_bt"]; bbc = S[name + "_bbc"]
    beta8 = S[name + "_beta8"]; gfm = S[name + "_gfm"]; og = S[name + "_og"]
    st = S[name + "_st"]; wbuf = S[name + "_wbuf"]; wb8 = S[name + "_wb8"]
    cvw = S[name + "_cvw"]; bia = S[name + "_bia"]; bb1 = S[name + "_bb1"]
    lgb = S[name + "_lgb"]; lbb = S[name + "_lbb"]; lgr = S[name + "_lgr"]
    gbd = S[name + "_gbd"]; kqb = S[name + "_kqb"]
    pkt = S[name + "_pkt"]
    wht = S[name + "_wht"]
    kth = S[name + "_kth"]; wth = S[name + "_wth"]
    osb = S[name + "_osb"]; osq = S[name + "_osq"]; ored = S[name + "_ored"]
    r2f = S[name + "_r2f"]; ogp = S[name + "_ogp"]; wfm = S[name + "_wfm"]
    ascr = S[name + "_ascr"]; ascr2 = S[name + "_ascr2"]; zbuf = S[name + "_zbuf"]
    gc8 = S[name + "_gc8"]; erow = S[name + "_erow"]
    mdiag = S[name + "_mdiag"]; mdiag2 = S[name + "_mdiag2"]
    lamc = S[name + "_lamc"]; tls = S[name + "_tls"]
    ident = S["identbf"]
    ones_col = S["ones_col"]
    ps_pp = es.enter_context(nc.psum_tensor(name + "_ppp", [128, 512], F32))
    ps_g = es.enter_context(nc.psum_tensor(name + "_pg", [128, 512], F32))
    ps_tr = es.enter_context(nc.psum_tensor(name + "_ptr", [128, 1024], BF16))
    ps_tr2 = es.enter_context(nc.psum_tensor(name + "_ptr2", [128, 1024], BF16))
    ps_acc = es.enter_context(nc.psum_tensor(name + "_pacc", [128, 512], F32))
    ps_o = es.enter_context(nc.psum_tensor(name + "_po", [128, 512], F32))
    ps_s = [es.enter_context(nc.psum_tensor(name + "_psu%d" % j, [128, 512], F32)) for j in range(2)]

    WSLOT = [wbuf[:, 0:8192], wbuf[:, 8192:16384]]
    wcnt = [0]
    wguard = [0, 0]

    def load_w(which, avoid=None):
        idx = (1 - avoid) if avoid is not None else wcnt[0] % 2
        s = WSLOT[idx]
        wcnt[0] += 1
        mk.wait("gp", "c_pe", wguard[idx])
        for c in range(NCH):
            mk.dma("gp", s[:, c * 1024:(c + 1) * 1024], hbm[which][c * 128:(c + 1) * 128, :], "q_gp")
        return s, mk.cnt["q_gp"], idx

    def mark_used(idx):
        wguard[idx] = mk.cnt["c_pe"]

    mk.dma("gp", wb8[:, :], hbm["wb"], "q_gp")
    mk.dma("gp", cvw[:, :], hbm["cvw"], "q_gp")
    mk.dma("gp", bia[:, :], hbm["bia"], "q_gp")
    mk.dma("gp", bb1[:, :], hbm["bb1"], "q_gp")
    mk.dma("gp", lgr[0:1, :], hbm["lnr"], "q_gp")
    mk.dma("gp", mdiag[:, :], hbm["mdiag"], "q_gp")
    mk.dma("gp", mdiag2[:, :], hbm["mdiag2"], "q_gp")
    mk.dma("gp", erow[:, :], hbm["erow"], "q_gp")
    mk.wait("pe", "q_gp", mk.cnt["q_gp"])
    mk.op("pe", lambda e: e.matmul(out=ps_o[:, 0:128], lhsT=ones_col[0:1, :],
          rhs=lgr[0:1, 0:128], start=True, stop=True))
    mk.op("pe", lambda e: e.matmul(out=ps_o[:, 128:256], lhsT=ones_col[0:1, :], rhs=lgr[0:1, 128:256],
          start=True, stop=True), "c_pe")
    mk.wait("act", "c_pe", mk.cnt["c_pe"])
    mk.op("act", lambda e: e.activation(out=lgb[:, :], in_=ps_o[:, 0:128], func=AF.Copy))
    t0_ = mk.op("act", lambda e: e.activation(out=lbb[:, :], in_=ps_o[:, 128:256], func=AF.Copy), "c_act")
    mk.wait("pe", "c_act", t0_)
    mk.op("dve", lambda e: e.memset(st[:, :], 0.0))
    mk.op("dve", lambda e: e.memset(pre[:, :], 0.0))
    mk.op("dve", lambda e: e.memset(tls[:, :], 0.0))
    mk.op("dve", lambda e: e.memset(wht[:, :], 0.0))
    mk.op("dve", lambda e: e.memset(zbuf[:, :], 0.0))
    tgt = mk.op("dve", lambda e: e.memset(lamc[:, :], 0.0), "c_dve")
    mk.wait("pe", "c_dve", tgt)
    w_next, wt_next, wi_next = load_w("waT")

    for b in range(NB):
        tb = b * BL
        for hb in range(2):
            t0 = tb + hb * 256
            mk.wait("sp", "c_dve", mk.cnt["c_dve"])
            for c in range(NCH):
                mk.dma("sp", xab[:, c * 256:(c + 1) * 256], xT_hbm[c * 128:(c + 1) * 128, t0:t0 + 256], "q_sp")
            xt = mk.cnt["q_sp"]
            mk.wait("act", "q_sp", xt)
            mk.wait("pe", "q_sp", xt)
            mk.wait("dve", "q_sp", xt)
            with ExitStack() as esl:
                emit_layernorm_fm(mk, 256, lambda c: xab[:, c * 256:(c + 1) * 256], None, esl,
                                  name + "_ln%d_%d" % (b, hb),
                                  dst_chunk=lambda c: xlnb[:, c * 512 + hb * 256: c * 512 + hb * 256 + 256],
                                  ps4=(ps_pp[0:1, 0:512], ps_g[0:1, 0:512], ps_s[0][:, 0:512], ps_s[1][:, 0:512]))
        w_a, wt_a, wi_a = w_next, wt_next, wi_next
        mk.wait("pe", "q_gp", wt_a)
        w_next, wt_next, wi_next = load_w("wqT", avoid=wi_a)
        for oc in range(NCH):
            for k in range(NCH):
                mk.op("pe", lambda e, oc=oc, k=k, w_a=w_a: e.matmul(
                    out=ps_pp[:, 0:512],
                    lhsT=w_a[:, k * 1024 + oc * 128: k * 1024 + oc * 128 + 128],
                    rhs=xlnb[:, k * 512:(k + 1) * 512],
                    start=(k == 0), stop=(k == NCH - 1)), "c_pe" if k == NCH - 1 else None)
            mk.wait("act", "c_pe", mk.cnt["c_pe"])
            tgt = mk.op("act", lambda e, oc=oc: e.activation(
                out=acp[:, oc * 512:(oc + 1) * 512], in_=ps_pp[:, 0:512], func=AF.Sigmoid,
                bias=bia[:, oc:oc + 1], scale=1.0), "c_act")
            mk.wait("pe", "c_act", tgt)
        mark_used(wi_a)
        for k in range(NCH):
            mk.op("pe", lambda e, k=k: e.matmul(
                out=ps_pp[0:8, 0:512], lhsT=wb8[:, k * 8:(k + 1) * 8],
                rhs=xlnb[:, k * 512:(k + 1) * 512],
                start=(k == 0), stop=(k == NCH - 1)), "c_pe" if k == NCH - 1 else None)
        mk.wait("act", "c_pe", mk.cnt["c_pe"])
        tgt = mk.op("act", lambda e: e.activation(out=beta8[0:8, :], in_=ps_pp[0:8, 0:512],
              func=AF.Sigmoid, bias=bb1[0:8, 0:1], scale=1.0), "c_act")
        mk.wait("pe", "c_act", tgt)
        mk.wait("gp", "c_act", tgt)
        bw = mk.dma("gp", scr["beta"][0:8, tb:tb + 512], beta8[0:8, :], "q_gp")
        mk.wait("gp", "q_gp", bw)
        for h in range(NCH):
            mk.dma("gp", bbc[:, h * 512:(h + 1) * 512],
                   scr["beta"][h:h + 1, tb:tb + 512].broadcast_to([128, 512]), "q_gp")
        bbt = mk.cnt["q_gp"]
        mk.wait("dve", "q_gp", bbt)
        mk.wait("act", "q_gp", bbt)
        for oc in range(NCH):
            mk.op("act", lambda e, oc=oc: e.activation(
                out=afm[:, oc * 512:(oc + 1) * 512], in_=acp[:, oc * 512:(oc + 1) * 512],
                func=AF.Ln), "c_act" if oc == NCH - 1 else None)
        mk.wait("dve", "c_act", mk.cnt["c_act"])
        for hb in range(2):
            for oc in range(NCH):
                mk.op("dve", lambda e, oc=oc, hb=hb: e.tensor_tensor_scan(
                    out=lam[:, oc * 256:(oc + 1) * 256],
                    data0=afm[:, oc * 512 + hb * 256: oc * 512 + hb * 256 + 256],
                    data1=zbuf[:, 0:256], initial=lamc[:, oc:oc + 1],
                    op0=ALU.add, op1=ALU.add))
                mk.op("dve", lambda e: e.drain())
                mk.op("dve", lambda e, oc=oc: e.tensor_copy(out=ascr[:, 0:1], in_=lamc[:, oc:oc + 1]))
                mk.op("dve", lambda e, oc=oc: e.tensor_copy(
                    out=ascr[:, 1:16], in_=lam[:, oc * 256 + 15: oc * 256 + 255].rearrange(
                        "p (a b) -> p a b", b=16)[:, :, 0]))
                mk.op("dve", lambda e, oc=oc: e.tensor_copy(out=lamc[:, oc:oc + 1],
                      in_=lam[:, oc * 256 + 255: oc * 256 + 256]))
                mk.op("dve", lambda e: e.drain())
                tgt = mk.op("dve", lambda e, oc=oc: e.tensor_tensor(
                    out=ascr[:, 256:512], in0=lam[:, oc * 256:(oc + 1) * 256],
                    in1=ascr[:, 0:16].rearrange("p (a o) -> p a o", o=1).broadcast_to([128, 16, 16]),
                    op=ALU.subtract), "c_dve")
                mk.wait("act", "c_dve", tgt)
                mk.op("act", lambda e, oc=oc, hb=hb: e.activation(
                    out=gam.rearrange("p (n g d) -> p n g d", g=8, d=16)[:, hb * 16:hb * 16 + 16, oc, :],
                    in_=ascr[:, 256:512].rearrange("p (n d) -> p n d", d=16), func=AF.Exp))
                tgt = mk.op("act", lambda e, oc=oc, hb=hb: e.activation(
                    out=ivg[:, oc * 512 + hb * 256: oc * 512 + hb * 256 + 256],
                    in_=ascr[:, 256:512], func=AF.Exp, bias=0.0, scale=-1.0), "c_act")
                mk.wait("dve", "c_act", tgt)
        for ti, (tn, wname, has_scale) in enumerate((("q", "wkT", False), ("k", "wvT", True), ("v", "wgT", False))):
            w_cur, wt_cur, wi_cur = w_next, wt_next, wi_next
            mk.wait("pe", "q_gp", wt_cur)
            w_next, wt_next, wi_next = load_w(wname, avoid=wi_cur)
            dst = (qs, ks, vs)[ti]
            for oc in range(NCH):
                mk.op("dve", lambda e, oc=oc, ti=ti: e.tensor_copy(
                    out=pre[:, oc * 520: oc * 520 + 3], in_=tls[:, ti * 24 + oc * 3: ti * 24 + oc * 3 + 3]))
            mk.op("dve", lambda e: e.drain())
            for oc in range(NCH):
                for k in range(NCH):
                    mk.op("pe", lambda e, oc=oc, k=k, w_cur=w_cur: e.matmul(
                        out=ps_pp[:, 0:512], lhsT=w_cur[:, k * 1024 + oc * 128: k * 1024 + oc * 128 + 128],
                        rhs=xlnb[:, k * 512:(k + 1) * 512],
                        start=(k == 0), stop=(k == NCH - 1)), "c_pe" if k == NCH - 1 else None)
                mk.wait("act", "c_pe", mk.cnt["c_pe"])
                tgt = mk.op("act", lambda e, oc=oc: e.activation(
                    out=pre[:, oc * 520 + 3: oc * 520 + 515], in_=ps_pp[:, 0:512], func=AF.Copy), "c_act")
                mk.wait("pe", "c_act", tgt)
                mk.wait("dve", "c_act", tgt)
                mk.op("dve", lambda e, oc=oc, ti=ti: e.tensor_scalar(
                    out=ascr[:, 0:512], in0=pre[:, oc * 520: oc * 520 + 512],
                    scalar1=cvw[:, ti * 32 + oc * 4: ti * 32 + oc * 4 + 1],
                    scalar2=bia[:, (ti + 1) * 8 + oc: (ti + 1) * 8 + oc + 1],
                    op0=ALU.mult, op1=ALU.add))
                for j in range(1, 4):
                    mk.op("dve", lambda e, oc=oc, ti=ti, j=j: e.scalar_tensor_tensor(
                        out=ascr[:, 0:512], in0=pre[:, oc * 520 + j: oc * 520 + j + 512],
                        scalar=cvw[:, ti * 32 + oc * 4 + j: ti * 32 + oc * 4 + j + 1],
                        in1=ascr[:, 0:512], op0=ALU.mult, op1=ALU.add))
                tgt = mk.op("dve", lambda e, oc=oc, ti=ti: e.tensor_copy(
                    out=tls[:, ti * 24 + oc * 3: ti * 24 + oc * 3 + 3],
                    in_=pre[:, oc * 520 + 512: oc * 520 + 515]), "c_dve")
                mk.wait("act", "c_dve", tgt)
                if ti < 2:
                    tgt = mk.op("act", lambda e, oc=oc, dst=dst: e.activation(
                        out=dst.rearrange("p (n g d) -> p n g d", g=8, d=16)[:, :, oc, :],
                        in_=ascr[:, 0:512].rearrange("p (n d) -> p n d", d=16),
                        func=AF.Silu), "c_act")
                else:
                    tgt = mk.op("act", lambda e, oc=oc, dst=dst: e.activation(
                        out=dst[:, oc * 512:(oc + 1) * 512], in_=ascr[:, 0:512],
                        func=AF.Silu), "c_act")
                mk.wait("dve", "c_act", tgt)
            mark_used(wi_cur)
            if has_scale:
                mk.op("dve", lambda e, dst=dst: e.tensor_scalar(
                    out=dst[:, :], in0=dst[:, :],
                    scalar1=0.08838834764831845, scalar2=None, op0=ALU.mult), "c_dve")
        w_cur, wt_cur, wi_cur = w_next, wt_next, wi_next
        mk.wait("pe", "q_gp", wt_cur)
        if b < NB - 1:
            w_next, wt_next, wi_next = load_w("waT", avoid=wi_cur)
        for oc in range(NCH):
            for k in range(NCH):
                mk.op("pe", lambda e, oc=oc, k=k, w_cur=w_cur: e.matmul(
                    out=ps_pp[:, 0:512], lhsT=w_cur[:, k * 1024 + oc * 128: k * 1024 + oc * 128 + 128],
                    rhs=xlnb[:, k * 512:(k + 1) * 512],
                    start=(k == 0), stop=(k == NCH - 1)), "c_pe" if k == NCH - 1 else None)
            mk.wait("act", "c_pe", mk.cnt["c_pe"])
            tgt = mk.op("act", lambda e, oc=oc: e.activation(
                out=gfm.rearrange("p (n g d) -> p n g d", g=8, d=16)[:, :, oc, :],
                in_=ps_pp[:, 0:512].rearrange("p (n d) -> p n d", d=16), func=AF.Sigmoid), "c_act")
            mk.wait("pe", "c_act", tgt)
        mark_used(wi_cur)
        mk.wait("act", "c_dve", mk.cnt["c_dve"])
        for oc in range(NCH):
            mk.op("act", lambda e, oc=oc: e.activation(
                out=ascr2[:, 0:512], in_=afm[:, oc * 512:(oc + 1) * 512], func=AF.Exp,
                bias=0.0, scale=-1.0), "c_act")
            mk.wait("dve", "c_act", mk.cnt["c_act"])
            mk.op("dve", lambda e, oc=oc: e.tensor_tensor(
                out=bt.rearrange("p (n g d) -> p n g d", g=8, d=16)[:, :, oc, :],
                in0=ascr2[:, 0:512].rearrange("p (n d) -> p n d", d=16),
                in1=bbc[:, oc * 512:(oc + 1) * 512].rearrange("p (n d) -> p n d", d=16), op=ALU.mult))
            mk.op("dve", lambda e, oc=oc: e.tensor_tensor(
                out=ascr2[:, 0:512], in0=vs[:, oc * 512:(oc + 1) * 512],
                in1=ivg[:, oc * 512:(oc + 1) * 512], op=ALU.mult))
            tgt = mk.op("dve", lambda e, oc=oc: e.tensor_tensor(
                out=cb[:, oc * 512:(oc + 1) * 512], in0=ascr2[:, 0:512],
                in1=acp[:, oc * 512:(oc + 1) * 512], op=ALU.mult), "c_dve")
            mk.wait("act", "c_dve", tgt)
        if "dbg_ks" in scr and b == NB - 1:
            tgt = mk.cnt["c_dve"]
            mk.wait("gp", "c_dve", tgt)
            mk.dma("gp", scr["dbg_ks"], ks[:, :], "q_gp")
        prevctx = None
        for n in range(32):
            prevctx = emit_attna_chunk(mk, nc, name, n, locals(), prevctx)
        t_lastog = emit_attna_partO(mk, name, locals(), prevctx)
        if "dbg0" in scr and b == 0:
            mk.wait("gp", "c_dve", mk.cnt["c_dve"])
            mk.wait("gp", "c_act", mk.cnt["c_act"])
            for dnm, dsrc in (("og", og), ("gam", gam), ("cb", cb), ("st", st), ("ks", ks), ("qs", qs)):
                mk.dma("gp", scr["dbg0"][dnm], dsrc[:, :], "q_gp")
        w_cur2, wt_cur2, wi_cur2 = load_w("woT", avoid=(wi_next if b < NB - 1 else None))
        mk.wait("pe", "q_gp", wt_cur2)
        mk.wait("pe", "c_act", t_lastog)
        for oc in range(NCH):
            for k in range(NCH):
                mk.op("pe", lambda e, oc=oc, k=k, w_cur2=w_cur2: e.matmul(
                    out=ps_pp[:, 0:512], lhsT=w_cur2[:, k * 1024 + oc * 128: k * 1024 + oc * 128 + 128],
                    rhs=og[:, k * 512:(k + 1) * 512],
                    start=(k == 0), stop=(k == NCH - 1)), "c_pe" if k == NCH - 1 else None)
            mk.wait("act", "c_pe", mk.cnt["c_pe"])
            if oc > 0:
                mk.wait("act", "q_gp", mk.cnt["q_gp"])
            tgt = mk.op("act", lambda e, oc=oc: e.activation(
                out=ascr2[:, 0:512], in_=ps_pp[:, 0:512], func=AF.Copy), "c_act")
            mk.wait("gp", "c_act", tgt)
            mk.dma("gp", scr["aout"][oc * 128:(oc + 1) * 128, tb:tb + 512], ascr2[:, 0:512], "q_gp")
            mk.wait("pe", "c_act", tgt)
        mark_used(wi_cur2)
        mk.wait("gp", "q_gp", mk.cnt["q_gp"])
        mk.wait("act", "q_gp", mk.cnt["q_gp"])
        mk.wait("dve", "q_gp", mk.cnt["q_gp"])
    return

def emit_attna_chunk(mk, nc, name, n, L, prev):
    S = mk.sb
    qs, ks, vs = L["qs"], L["ks"], L["vs"]
    gam, cb, bt, gfm, og, st = L["gam"], L["cb"], L["bt"], L["gfm"], L["og"], L["st"]
    gbd, kqb, pkt2 = L["gbd"], L["kqb"], L["pkt"]
    wht, kth2, wth = L["wht"], L["kth"], L["wth"]

    osb, osq, ored, r2f, ogp, wfm = L["osb"], L["osq"], L["ored"], L["r2f"], L["ogp"], L["wfm"]
    ascr, ascr2 = L["ascr"], L["ascr2"]
    gc8, erow = L["gc8"], L["erow"]
    mdiag, mdiag2, lgb, lbb = L["mdiag"], L["mdiag2"], L["lgb"], L["lbb"]
    ident = L["ident"]
    ps_pp, ps_g, ps_tr, ps_tr2, ps_acc, ps_o = L["ps_pp"], L["ps_g"], L["ps_tr"], L["ps_tr2"], L["ps_acc"], L["ps_o"]
    ps_s = L["ps_s"]
    c0 = n * 16
    nb = n * 128
    par = n % 2
    kth = kth2[:, par * 1024:(par + 1) * 1024]
    pkt = pkt2[:, par * 512:(par + 1) * 512]
    p0f = ascr2[:, 512 + par * 128: 640 + par * 128]
    p0t = ps_tr[:, 512 + par * 128: 640 + par * 128]
    kc = ks[:, nb:nb + 128]; qc = qs[:, nb:nb + 128]
    btc = bt[:, nb:nb + 128]; gmc = gam[:, nb:nb + 128]; gfc = gfm[:, nb:nb + 128]

    def pv(buf, w=16, off=0):
        return buf.rearrange("p (a b) -> p a b", b=512)[:, :, c0 + off: c0 + off + w]

    mk.wait("pe", "c_dve", mk.cnt["c_dve"])
    mk.wait("pe", "c_act", mk.cnt["c_act"])
    for h in range(NCH):
        mk.op("pe", lambda e, h=h: e.matmul(
            out=ps_pp[:, h * 32: h * 32 + 16], lhsT=st[:, h * 128:(h + 1) * 128],
            rhs=kc[:, h * 16:(h + 1) * 16], start=True, stop=True))
        mk.op("pe", lambda e, h=h: e.matmul(
            out=ps_pp[:, h * 32 + 16: h * 32 + 32], lhsT=st[:, h * 128:(h + 1) * 128],
            rhs=qc[:, h * 16:(h + 1) * 16], start=True, stop=True))
    mk.op("pe", lambda e: e.matmul(out=ps_g[:, 0:128], lhsT=kc, rhs=kc,
          start=True, stop=True))
    mk.op("pe", lambda e: e.matmul(out=ps_g[:, 128:256], lhsT=kc, rhs=qc,
          start=True, stop=True), "c_pe")
    tgt_pg = mk.cnt["c_pe"]
    mk.wait("dve", "c_pe", tgt_pg)
    ppv = ps_pp[:, 0:256].rearrange("p (a b) -> p a b", b=32)
    mk.op("dve", lambda e: e.tensor_tensor(
        out=r2f.rearrange("p (a b) -> p a b", b=16), in0=pv(cb),
        in1=ppv[:, :, 0:16], op=ALU.subtract))
    mk.op("dve", lambda e: e.tensor_tensor(out=gbd[:, :], in0=ps_g[:, 0:128], in1=mdiag[:, :], op=ALU.mult))
    mk.op("dve", lambda e: e.tensor_tensor(
        out=p0f.rearrange("p (a b) -> p a b", b=16),
        in0=gmc.rearrange("p (a b) -> p a b", b=16),
        in1=ppv[:, :, 16:32], op=ALU.mult))
    tgt_gb = mk.op("dve", lambda e: e.tensor_tensor(out=kqb[:, :], in0=ps_g[:, 128:256], in1=mdiag2[:, :],
          op=ALU.mult), "c_dve")
    mk.wait("pe", "c_dve", tgt_gb)
    mk.op("pe", lambda e: e.transpose(out=ps_tr[:, 0:128], in_=r2f[:, :], identity=ident[:, :]))
    mk.op("pe", lambda e: e.transpose(out=ps_tr[:, 128:256], in_=btc, identity=ident[:, :]))
    mk.op("pe", lambda e: e.transpose(out=ps_tr[:, 256:384], in_=gmc, identity=ident[:, :]))
    mk.op("pe", lambda e: e.transpose(out=ps_tr[:, 384:512], in_=gfc, identity=ident[:, :]))
    tgt_p0t = mk.op("pe", lambda e: e.transpose(out=p0t, in_=p0f, identity=ident[:, :]), "c_pe")
    if n == 0 or KTH_PRECHAIN:
        for h in range(NCH):
            mk.op("pe", lambda e, h=h: e.transpose(
                out=ps_tr2[0:16, h * 128:(h + 1) * 128], in_=kc[:, h * 16:(h + 1) * 16],
                identity=ident[:, :]), "c_pe" if h == NCH - 1 else None)
    tgt_tr2 = mk.cnt["c_pe"]
    mk.wait("act", "c_pe", tgt_tr2)
    tgt_pk = mk.op("act", lambda e: e.activation(out=pkt[:, :], in_=ps_tr[:, 0:512], func=AF.Copy), "c_act")
    if n == 0 or KTH_PRECHAIN:
        tgt_pk = mk.op("act", lambda e: e.activation(out=kth[0:16, :], in_=ps_tr2[0:16, 0:1024], func=AF.Copy), "c_act")
    r2t = pkt[:, 0:128]; btt = pkt[:, 128:256]; gmt = pkt[:, 256:384]; gpk = pkt[:, 384:512]
    if prev is not None:
        ppar = prev["par"]
        gmt_p = pkt2[:, ppar * 512 + 256: ppar * 512 + 384]
        gpk_p = pkt2[:, ppar * 512 + 384: ppar * 512 + 512]
        p0t_p = ps_tr[:, 512 + ppar * 128: 640 + ppar * 128]

    def filler(k):
        if prev is None:
            return
        if k == 0:
            mk.op("dve", lambda e: e.tensor_tensor(out=osb[:, :], in0=ps_o[:, 0:128], in1=gmt_p, op=ALU.mult))
        elif k == 1:
            mk.wait("act", "c_pe", prev["p0t_tag"])
            L["t_o3"] = mk.op("act", lambda e: e.activation(
                out=ascr2[:, 256:384], in_=p0t_p, func=AF.Copy), "c_act")
        elif k == 2:
            mk.wait("dve", "c_act", L["t_o3"])
            mk.op("dve", lambda e: e.tensor_tensor(out=osb[:, :], in0=osb[:, :], in1=ascr2[:, 256:384], op=ALU.add))
        elif k == 3:
            mk.op("dve", lambda e: e.reduce_sum(out=ored[:, 0:1], in_=osb[:, :], axis=AX.X))
            mk.op("dve", lambda e: e.drain())
        elif k == 4:
            mk.op("dve", lambda e: e.tensor_scalar(out=ored[:, 1:2], in0=ored[:, 0:1],
                  scalar1=1.0 / 128, scalar2=None, op0=ALU.mult))
            mk.op("dve", lambda e: e.drain())
        elif k == 5:
            L["t_m"] = mk.op("dve", lambda e: e.tensor_scalar(out=osb[:, :], in0=osb[:, :],
                  scalar1=ored[:, 1:2], scalar2=None, op0=ALU.subtract), "c_dve")
        elif k == 6:
            mk.wait("act", "c_dve", L["t_m"])
            L["t_sq"] = mk.op("act", lambda e: e.activation(out=osq[:, :], in_=osb[:, :], func=AF.Square), "c_act")
        elif k == 8:
            mk.wait("dve", "c_act", L["t_sq"])
            mk.op("dve", lambda e: e.reduce_sum(out=ored[:, 2:3], in_=osq[:, :], axis=AX.X))
            mk.op("dve", lambda e: e.drain())
        elif k == 9:
            L["t_v"] = mk.op("dve", lambda e: e.tensor_scalar(out=ored[:, 5:6], in0=ored[:, 2:3],
                  scalar1=1.0 / 128, scalar2=None, op0=ALU.mult), "c_dve")
        elif k == 10:
            mk.wait("act", "c_dve", L["t_v"])
            mk.op("act", lambda e: e.activation(out=ored[:, 6:7], in_=ored[:, 5:6], func=AF.Ln, bias=EPS, scale=1.0))
            L["t_r"] = mk.op("act", lambda e: e.activation(out=ored[:, 6:7], in_=ored[:, 6:7], func=AF.Exp,
                  bias=0.0, scale=-0.5), "c_act")
        elif k == 12:
            mk.wait("dve", "c_act", L["t_r"])
            mk.op("dve", lambda e: e.tensor_scalar(out=osb[:, :], in0=osb[:, :], scalar1=ored[:, 6:7],
                  scalar2=None, op0=ALU.mult))
        elif k == 13:
            mk.op("dve", lambda e: e.tensor_tensor(out=osb[:, :], in0=osb[:, :], in1=lgb[:, :], op=ALU.mult))
            mk.op("dve", lambda e: e.tensor_tensor(out=osb[:, :], in0=osb[:, :], in1=lbb[:, :], op=ALU.add))
        elif k == 14:
            L["t_og"] = mk.op("dve", lambda e, gpk_p=gpk_p: e.tensor_tensor(
                out=ogp[:, :], in0=osb[:, :], in1=gpk_p, op=ALU.mult), "c_dve")

    mk.wait("pe", "c_act", tgt_pk)
    for k in range(16):
        pc = (k % 2) * 128
        mk.op("pe", lambda e, pc=pc: e.matmul(
            out=ps_acc[:, pc:pc + 128], lhsT=ident[:, :], rhs=r2t,
            start=True, stop=(k == 0)), None if k > 0 else "c_pe")
        if k > 0:
            mk.op("pe", lambda e, pc=pc: e.matmul(
                out=ps_acc[:, pc:pc + 128], lhsT=gbd[:, :], rhs=wht[:, :],
                start=False, stop=True), "c_pe")
        tgt_mm = mk.cnt["c_pe"]
        mk.wait("dve", "c_pe", tgt_mm)
        tgt_st = mk.op("dve", lambda e, pc=pc: e.tensor_tensor(
            out=wht[:, :], in0=btt, in1=ps_acc[:, pc:pc + 128], op=ALU.mult), "c_dve")
        filler(k)
        if (not KTH_PRECHAIN) and n < 31 and 2 <= k <= 9:
            h = k - 2
            kcn = ks[:, (nb + 128) + h * 16: (nb + 128) + (h + 1) * 16]
            L["t_ktr"] = mk.op("pe", lambda e, h=h, kcn=kcn: e.transpose(
                out=ps_tr2[0:16, h * 128:(h + 1) * 128], in_=kcn,
                identity=ident[:, :]), "c_pe")
        if (not KTH_PRECHAIN) and n < 31 and k == 10:
            mk.wait("act", "c_pe", L["t_ktr"])
            L["t_kev"] = mk.op("act", lambda e: e.activation(
                out=kth2[0:16, (1 - par) * 1024:(2 - par) * 1024],
                in_=ps_tr2[0:16, 0:1024], func=AF.Copy), "c_act")
        mk.wait("pe", "c_dve", tgt_st)
    if prev is not None:
        mk.wait("pe", "c_dve", L["t_og"])
        if (not KTH_PRECHAIN) and n < 31:
            mk.wait("pe", "c_act", L["t_kev"])
        t_ogt = mk.op("pe", lambda e: e.transpose(out=ps_tr2[:, 896:1024],
              in_=ogp[:, :], identity=ident[:, :]), "c_pe")
        mk.wait("act", "c_pe", t_ogt)
        pc0 = prev["c0"]
        mk.op("act", lambda e, pc0=pc0: e.activation(
            out=og.rearrange("p (a b) -> p a b", b=512)[:, :, pc0:pc0 + 16],
            in_=ps_tr2[:, 896:1024].rearrange("p (a b) -> p a b", b=16),
            func=AF.Copy), "c_act")
    mk.op("pe", lambda e: e.transpose(out=ps_tr[:, 768:896], in_=wht[:, :], identity=ident[:, :]), "c_pe")
    mk.wait("act", "c_pe", mk.cnt["c_pe"])
    tgt_wf = mk.op("act", lambda e: e.activation(out=wfm[:, :], in_=ps_tr[:, 768:896], func=AF.Copy), "c_act")
    mk.wait("pe", "c_act", tgt_wf)
    for h in range(NCH):
        mk.op("pe", lambda e, h=h: e.transpose(
            out=ps_tr2[0:16, h * 128:(h + 1) * 128], in_=wfm[:, h * 16:(h + 1) * 16],
            identity=ident[:, :]), "c_pe" if h == NCH - 1 else None)
    mk.wait("act", "c_pe", mk.cnt["c_pe"])
    tgt_wt = mk.op("act", lambda e: e.activation(out=wth[0:16, :], in_=ps_tr2[0:16, 0:1024], func=AF.Copy), "c_act")
    mk.wait("pe", "c_act", tgt_wt)
    mk.op("pe", lambda e: e.transpose(
        out=ps_tr[0:8, 896:1024], in_=gam[:, nb + 15:nb + 128:16],
        identity=ident[:, :]), "c_pe")
    mk.wait("act", "c_pe", mk.cnt["c_pe"])
    tgt_g8 = mk.op("act", lambda e: e.activation(out=gc8[0:8, :], in_=ps_tr[0:8, 896:1024], func=AF.Copy), "c_act")
    mk.wait("pe", "c_act", tgt_g8)
    for h in range(NCH):
        mk.op("pe", lambda e, h=h: e.matmul(
            out=ps_s[h // 4][:, (h % 4) * 128:(h % 4 + 1) * 128], lhsT=kth[0:16, h * 128:(h + 1) * 128],
            rhs=wth[0:16, h * 128:(h + 1) * 128], start=True, stop=True))
    for h in range(NCH):
        psg = ps_acc if h < 4 else ps_o
        mk.op("pe", lambda e, h=h, psg=psg: e.matmul(
            out=psg[:, (h % 4) * 128:(h % 4 + 1) * 128], lhsT=erow[0:8, h * 128:(h + 1) * 128],
            rhs=gc8[0:8, :], start=True, stop=True), "c_pe" if h == NCH - 1 else None)
    tgt_su = mk.cnt["c_pe"]
    mk.wait("dve", "c_pe", tgt_su)
    for g2 in range(2):
        psg = ps_acc if g2 == 0 else ps_o
        mk.op("dve", lambda e, g2=g2: e.tensor_tensor(
            out=ascr[:, 0:512], in0=st[:, g2 * 512:(g2 + 1) * 512], in1=ps_s[g2][:, 0:512], op=ALU.add))
        mk.op("dve", lambda e, g2=g2, psg=psg: e.tensor_tensor(
            out=st[:, g2 * 512:(g2 + 1) * 512], in0=ascr[:, 0:512], in1=psg[:, 0:512], op=ALU.mult),
            "c_dve" if g2 == 1 else None)
    tgt_stup = mk.cnt["c_dve"]
    mk.wait("pe", "c_dve", tgt_stup)
    tgt_omm = mk.op("pe", lambda e: e.matmul(out=ps_o[:, 0:128], lhsT=kqb[:, :], rhs=wht[:, :],
          start=True, stop=True), "c_pe")
    return {"par": par, "c0": c0, "p0t_tag": tgt_p0t, "omm_tag": tgt_omm}


def emit_attna_partO(mk, name, L, prev):
    S = mk.sb
    og = L["og"]
    osb, osq, ored, ogp = L["osb"], L["osq"], L["ored"], L["ogp"]
    ascr2, lgb, lbb = L["ascr2"], L["lgb"], L["lbb"]
    ident = L["ident"]
    ps_tr, ps_acc, ps_o = L["ps_tr"], L["ps_acc"], L["ps_o"]
    pkt2 = L["pkt"]
    ppar = prev["par"]
    gmt_p = pkt2[:, ppar * 512 + 256: ppar * 512 + 384]
    gpk_p = pkt2[:, ppar * 512 + 384: ppar * 512 + 512]
    p0t_p = ps_tr[:, 512 + ppar * 128: 640 + ppar * 128]
    mk.wait("dve", "c_pe", prev["omm_tag"])
    mk.op("dve", lambda e: e.tensor_tensor(out=osb[:, :], in0=ps_o[:, 0:128], in1=gmt_p, op=ALU.mult))
    mk.wait("act", "c_pe", prev["p0t_tag"])
    t_o3 = mk.op("act", lambda e: e.activation(out=ascr2[:, 256:384], in_=p0t_p, func=AF.Copy), "c_act")
    mk.wait("dve", "c_act", t_o3)
    mk.op("dve", lambda e: e.tensor_tensor(out=osb[:, :], in0=osb[:, :], in1=ascr2[:, 256:384], op=ALU.add))
    mk.op("dve", lambda e: e.reduce_sum(out=ored[:, 0:1], in_=osb[:, :], axis=AX.X))
    mk.op("dve", lambda e: e.drain())
    mk.op("dve", lambda e: e.tensor_scalar(out=ored[:, 1:2], in0=ored[:, 0:1],
          scalar1=1.0 / 128, scalar2=None, op0=ALU.mult))
    mk.op("dve", lambda e: e.drain())
    t_m = mk.op("dve", lambda e: e.tensor_scalar(out=osb[:, :], in0=osb[:, :],
          scalar1=ored[:, 1:2], scalar2=None, op0=ALU.subtract), "c_dve")
    mk.wait("act", "c_dve", t_m)
    t_sq = mk.op("act", lambda e: e.activation(out=osq[:, :], in_=osb[:, :], func=AF.Square), "c_act")
    mk.wait("dve", "c_act", t_sq)
    mk.op("dve", lambda e: e.reduce_sum(out=ored[:, 2:3], in_=osq[:, :], axis=AX.X))
    mk.op("dve", lambda e: e.drain())
    t_v = mk.op("dve", lambda e: e.tensor_scalar(out=ored[:, 5:6], in0=ored[:, 2:3], scalar1=1.0 / 128,
          scalar2=None, op0=ALU.mult), "c_dve")
    mk.wait("act", "c_dve", t_v)
    mk.op("act", lambda e: e.activation(out=ored[:, 6:7], in_=ored[:, 5:6], func=AF.Ln, bias=EPS, scale=1.0))
    t_r = mk.op("act", lambda e: e.activation(out=ored[:, 6:7], in_=ored[:, 6:7], func=AF.Exp,
          bias=0.0, scale=-0.5), "c_act")
    mk.wait("dve", "c_act", t_r)
    mk.op("dve", lambda e: e.tensor_scalar(out=osb[:, :], in0=osb[:, :], scalar1=ored[:, 6:7],
          scalar2=None, op0=ALU.mult))
    mk.op("dve", lambda e: e.tensor_tensor(out=osb[:, :], in0=osb[:, :], in1=lgb[:, :], op=ALU.mult))
    mk.op("dve", lambda e: e.tensor_tensor(out=osb[:, :], in0=osb[:, :], in1=lbb[:, :], op=ALU.add))
    t_og = mk.op("dve", lambda e: e.tensor_tensor(out=ogp[:, :], in0=osb[:, :], in1=gpk_p, op=ALU.mult), "c_dve")
    mk.wait("pe", "c_dve", t_og)
    ps_tr2 = L["ps_tr2"]
    t_ogt = mk.op("pe", lambda e: e.transpose(out=ps_tr2[:, 896:1024],
          in_=ogp[:, :], identity=ident[:, :]), "c_pe")
    mk.wait("act", "c_pe", t_ogt)
    pc0 = prev["c0"]
    t_oge = mk.op("act", lambda e, pc0=pc0: e.activation(
        out=og.rearrange("p (a b) -> p a b", b=512)[:, :, pc0:pc0 + 16],
        in_=ps_tr2[:, 896:1024].rearrange("p (a b) -> p a b", b=16),
        func=AF.Copy), "c_act")
    return t_oge


from concourse.bass_utils import run_bass_kernel_spmd
import ml_dtypes

_BF = ml_dtypes.bfloat16
_NC_CACHE = []


def _build_full():
    nc = bass.Bass("TRN2", target_bir_lowering=False, detect_race_conditions=False)
    es = ExitStack()
    with es:
        mk = MK(nc, es)
        xT = mk.din("xT", [1024, TFULL], F32)
        xe = mk.din("xe", [1024, TEXT], F32)
        selc = mk.din("selc", [128, 2], F32)
        for w in ("waT", "wqT", "wkT", "wvT", "wgT", "woT"):
            mk.din(w, [1024, 1024], F32)
        mk.din("wb_h", [128, 64], F32)
        mk.din("cvw_h", [128, 96], F32)
        mk.din("bia_h", [128, 32], F32)
        mk.din("bb1_h", [8, 1], F32)
        mk.din("lnr_h", [1, 256], F32)
        mk.din("mdiag_h", [128, 128], BF16)
        mk.din("mdiag2_h", [128, 128], BF16)
        mk.din("erow_h", [8, 1024], BF16)
        identbf = mk.din("identbf_h", [128, 128], BF16)
        mk.din("wupT", [1024, 4096], F32)
        mk.din("wdnT", [4096, 1024], F32)
        mk.din("pqT", [1024, 512], F32)
        mk.din("keysT", [64, 512], F32)
        mk.din("deT", [1024, 4096], F32)
        mk.din("ue", [4096, 1024], F32)
        for w in ("abqwT", "abkwT", "abvwT", "abowT"):
            mk.din(w, [1024, 1024], F32)
        for b in ("qb8", "kb8", "ob8"):
            mk.din(b, [128, 8], F32)
        mk.din("vb_row", [1, 1024], F32)
        maskc_h = mk.din("maskc", [128, 3 * 384], BF16)
        outT = mk.dout("outT", [1024, 2048], F32)
        aoutp = mk.dscratch("aout_s", [1024, 4352], BF16)
        beta_s = mk.dscratch("beta_s", [8, TFULL], F32)
        wupb = mk.dscratch("wupb", [1024, 4096], BF16)
        wdnb = mk.dscratch("wdnb", [4096, 1024], BF16)
        deb = mk.dscratch("deb", [1024, 4096], BF16)
        ueb = mk.dscratch("ueb", [4096, 1024], BF16)
        for s in ("q_sp", "q_gp", "c_pe", "c_act", "c_dve", "c_gp"):
            mk.new_sem(s)
        mk.sbuf("scr512a", [128, 512], F32)
        mk.sbuf("scr512b", [128, 512], F32)
        mk.sbuf("scr512c", [128, 512], F32)
        mk.sbuf("srow", [1, 2048], F32)
        oc_ = mk.sbuf("ones_col", [1, 128], F32)
        o1_ = mk.sbuf("ones128", [128, 1], F32)
        idb = mk.sbuf("identbf", [128, 128], BF16)
        sel = mk.sbuf("sel_s", [128, 2], F32)
        mk.init_consts([1e-5, -0.5])
        mk.op("dve", lambda e: e.memset(oc_[:, :], 1.0))
        mk.op("dve", lambda e: e.memset(o1_[:, :], 1.0))
        mk.dma("sp", idb[:, :], identbf, "q_sp")
        mk.dma("sp", sel[:, :], mk.dram["selc"], "q_sp")

        with ExitStack() as esA:
            def asb(nm, shape, dt):
                return esA.enter_context(nc.sbuf_tensor(nm, shape, dt))
            N = "aa"
            for nm, shape, dt in (
                ("_xab", [128, 8 * 256], F32), ("_xln", [128, 8 * 512], BF16),
                ("_pre", [128, 8 * 520], BF16),
                ("_qs", [128, 8 * 512], BF16), ("_ks", [128, 8 * 512], BF16),
                ("_vs", [128, 8 * 512], BF16), ("_gam", [128, 8 * 512], BF16),
                ("_ivg", [128, 8 * 512], BF16), ("_cb", [128, 8 * 512], BF16),
                ("_bt", [128, 8 * 512], BF16), ("_bbc", [128, 8 * 512], BF16),
                ("_gfm", [128, 8 * 512], BF16), ("_og", [128, 8 * 512], BF16),
                ("_acp", [128, 8 * 512], BF16),
                ("_afm", [128, 8 * 512], F32), ("_lam", [128, 8 * 256], F32),
                ("_beta8", [8, 512], F32), ("_st", [128, 8 * 128], BF16),
                ("_wbuf", [128, 2 * 8192], BF16), ("_wb8", [128, 64], BF16),
                ("_cvw", [128, 96], F32), ("_bia", [128, 32], F32),
                ("_bb1", [8, 1], F32), ("_lgb", [128, 128], F32),
                ("_lbb", [128, 128], F32), ("_lgr", [1, 256], F32),
                ("_gbd", [128, 128], BF16), ("_kqb", [128, 128], BF16),
                ("_wht", [128, 128], BF16),
                ("_r2f", [128, 128], BF16), ("_ogp", [128, 128], BF16),
                ("_wfm", [128, 128], BF16), ("_mdiag", [128, 128], BF16),
                ("_mdiag2", [128, 128], BF16), ("_pkt", [128, 1024], BF16),
                ("_kth", [16, 2048], BF16), ("_wth", [16, 1024], BF16),
                ("_osb", [128, 128], F32), ("_osq", [128, 128], F32),
                ("_ored", [128, 8], F32), ("_ascr", [128, 512], F32),
                ("_ascr2", [128, 768], BF16), ("_zbuf", [128, 256], F32),
                ("_gc8", [8, 128], BF16), ("_erow", [8, 1024], BF16),
                ("_lamc", [128, 8], F32), ("_tls", [128, 72], BF16),
            ):
                mk.sb[N + nm] = asb(N + nm, shape, dt)
            mk.barrier()
            az = mk.sb[N + "_ascr2"]
            tz = mk.op("dve", lambda e: e.memset(az[:, 0:256], 0.0), "c_dve")
            mk.wait("gp", "c_dve", tz)
            for c in range(NCH):
                mk.dma("gp", aoutp[c * 128:(c + 1) * 128, 0:256], az[:, 0:256], "q_gp")
            mk.wait("dve", "q_gp", mk.cnt["q_gp"])
            hbm_a = {"waT": mk.dram["waT"], "wqT": mk.dram["wqT"], "wkT": mk.dram["wkT"],
                     "wvT": mk.dram["wvT"], "wgT": mk.dram["wgT"], "woT": mk.dram["woT"],
                     "wb": mk.dram["wb_h"], "cvw": mk.dram["cvw_h"], "bia": mk.dram["bia_h"],
                     "bb1": mk.dram["bb1_h"], "lnr": mk.dram["lnr_h"], "mdiag": mk.dram["mdiag_h"],
                     "mdiag2": mk.dram["mdiag2_h"], "erow": mk.dram["erow_h"]}
            emit_attna(mk, TFULL, xT, hbm_a, {"aout": aoutp[:, 256:4352], "beta": beta_s}, esA, N)
            mk.barrier()

        with ExitStack() as esB:
            resid = esB.enter_context(nc.sbuf_tensor("resid", [128, NCH * TEXT], F32))
            xln2 = esB.enter_context(nc.sbuf_tensor("xln2", [128, NCH * TEXT], BF16))
            arena = esB.enter_context(nc.sbuf_tensor("arena", [128, 21248], F32))
            mk.sb["resid"] = resid
            mk.sb["xln2"] = xln2
            mk.sb["arena"] = arena
            pkeys = esB.enter_context(nc.sbuf_tensor("peer_keys", [128, 512], BF16))
            mk.sb["peer_keys"] = pkeys
            mk.view("ffn_wup", arena, 0, NCH * 2048, BF16)
            mk.view("ffn_wdn", arena, 8192, 16 * 1024, BF16)
            mk.view("ffn_mid", arena, 16384, 16 * 256, BF16)
            o = [0]
            def V(nm, cols, dt):
                mk.view(nm, arena, o[0], cols, dt)
                o[0] += (cols if dt == F32 else (cols + 1) // 2)
                o[0] = (o[0] + 1) & ~1
            nt = (TEXT + 127) // 128
            V("peer_qfm", 8 * 512, BF16)
            V("peer_sims", 512, F32)
            V("peer_exa", nt * 256, BF16)
            V("peer_eya", nt * 256, BF16)
            V("peer_tau", nt * 4, F32)
            V("peer_srow", 64, F32)
            V("peer_s8", 64, F32)
            V("peer_det", 8 * 1024, BF16)
            V("peer_ue", 2 * 1024, BF16)
            V("peer_dsb", 1024, BF16)
            V("peer_dsb2", 1024, BF16)
            V("peer_pds", 1024, F32)
            V("peer_pd2", 1024, F32)
            V("peer_wtmp", 1024, BF16)
            V("peer_wacc", 1024, BF16)
            wtw_off = o[0]
            V("peer_wtw", 32 * 128 * 2, BF16)
            mk.view("peer_wq", arena, wtw_off, 8 * 512, BF16)
            assert o[0] <= 21248, o[0]
            o[0] = 0
            V("ab_wbuf", 8 * 1024, BF16)
            V("ab_kroll", 8 * 896, BF16)
            V("ab_vroll", 7 * 1024, BF16)
            V("ab_qbuf", 8 * 512, BF16)
            V("ab_att", 8 * 512, BF16)
            exps_off = o[0]
            V("ab_exps", 4 * 384, F32)
            V("ab_pbf", 4 * 384, BF16)
            V("ab_pbft", 2 * 384, BF16)
            V("ab_red", 16, F32)
            V("ab_bq", 8, F32); V("ab_bk", 8, F32); V("ab_bo", 8, F32)
            V("ab_vbb", 1024, BF16)
            V("ab_mkc", 3 * 384, BF16)
            mk.sb["ab_vbr"] = arena[0:1, exps_off:exps_off + 1024]
            assert o[0] <= 21248, o[0]
            for k in list(mk.sb):
                if k.startswith("peer_"):
                    mk.sb[k.replace("peer_", "peer0_", 1)] = mk.sb[k]
                    mk.sb[k.replace("peer_", "peer1_", 1)] = mk.sb[k]
                if k.startswith("ffn_"):
                    mk.sb[k.replace("ffn_", "ffn0_", 1)] = mk.sb[k]
                    mk.sb[k.replace("ffn_", "ffn1_", 1)] = mk.sb[k]

            stg = [arena[:, 0:2048].bitcast(BF16), arena[:, 2048:4096].bitcast(BF16)]
            outtag = [0, 0]
            for srcT, dstT, nch_, cols in ((mk.dram["wupT"], wupb, 8, 4096),
                                           (mk.dram["wdnT"], wdnb, 32, 1024),
                                           (mk.dram["deT"], deb, 8, 4096),
                                           (mk.dram["ue"], ueb, 32, 1024)):
                for ci in range(nch_):
                    sl = ci % 2
                    mk.wait("gp", "q_sp", outtag[sl])
                    t_in = mk.dma("gp", stg[sl][:, 0:cols], srcT[ci * 128:(ci + 1) * 128, :], "q_gp")
                    mk.wait("sp", "q_gp", t_in)
                    outtag[sl] = mk.dma("sp", dstT[ci * 128:(ci + 1) * 128, :], stg[sl][:, 0:cols], "q_sp")
            mk.wait("gp", "q_sp", mk.cnt["q_sp"])
            mk.wait("sp", "q_sp", mk.cnt["q_sp"])
            rc = lambda c: resid[:, c * TEXT:(c + 1) * TEXT]
            for c in range(NCH):
                mk.dma("sp", rc(c), xe[c * 128:(c + 1) * 128, :], "q_sp")
            xet = mk.cnt["q_sp"]
            for half, (c0_, c1_) in enumerate(((0, TEXT), (2048, 2048 + TEXT))):
                for c in range(NCH):
                    mk.dma("gp", xln2[:, c * TEXT:(c + 1) * TEXT],
                           aoutp[c * 128:(c + 1) * 128, c0_:c1_], "q_gp")
                agt = mk.cnt["q_gp"]
                if half == 0:
                    mk.wait("dve", "q_sp", xet)
                mk.wait("dve", "q_gp", agt)
                tsl = None
                for c in range(NCH):
                    tsl = mk.op("dve", lambda e, c=c, half=half: e.scalar_tensor_tensor(
                        out=rc(c), in0=xln2[:, c * TEXT:(c + 1) * TEXT],
                        scalar=sel[:, half:half + 1], in1=rc(c),
                        op0=ALU.mult, op1=ALU.add), "c_dve" if c == NCH - 1 else None)
                mk.wait("gp", "c_dve", tsl)
            mk.barrier()
            hbm_p = {"wqT": mk.dram["pqT"], "keysT": mk.dram["keysT"],
                     "deT": deb, "ue": ueb}
            hbm_b = {"qwT": mk.dram["abqwT"], "kwT": mk.dram["abkwT"], "vwT": mk.dram["abvwT"],
                     "owT": mk.dram["abowT"], "qb8": mk.dram["qb8"], "kb8": mk.dram["kb8"],
                     "ob8": mk.dram["ob8"], "vb_row": mk.dram["vb_row"]}
            for sub in range(2):
                with ExitStack() as esl:
                    emit_layernorm_fm(mk, TEXT, rc, xln2, esl, "lnf%d" % sub)
                mk.barrier()
                with ExitStack() as esf:
                    emit_ffn_dense(mk, TEXT, xln2, rc, wupb, wdnb, esf, "ffn%d" % sub)
                mk.barrier()
                with ExitStack() as esp:
                    emit_peer(mk, TEXT, xln2, rc, hbm_p, esp, "peer%d" % sub, GRP=2)
                mk.barrier()
                if sub == 0:
                    with ExitStack() as esl:
                        emit_layernorm_fm(mk, TEXT, rc, xln2, esl, "lnb")
                    mk.barrier()
                    with ExitStack() as esa:
                        emit_attnb(mk, TEXT, 256, xln2, rc, hbm_b, maskc_h, esa, "ab")
                    mk.barrier()
            for c in range(NCH):
                mk.dma("sp", outT[c * 128:(c + 1) * 128, :], resid[:, c * TEXT + 256:(c + 1) * TEXT], "q_sp")
            mk.wait("sp", "q_sp", mk.cnt["q_sp"])
            mk.barrier()
        mk.finalize()
    return nc


def _host_pack(inputs):
    i = {k: np.asarray(v) for k, v in inputs.items()}
    shared = {}
    shared["waT"] = np.ascontiguousarray(i["L0_attn_a_Wa"], np.float32)
    shared["wqT"] = np.ascontiguousarray(i["L0_attn_a_Wq"], np.float32)
    shared["wkT"] = np.ascontiguousarray(i["L0_attn_a_Wk"], np.float32)
    shared["wvT"] = np.ascontiguousarray(i["L0_attn_a_Wv"], np.float32)
    shared["wgT"] = np.ascontiguousarray(i["L0_attn_a_Wg"], np.float32)
    shared["woT"] = np.ascontiguousarray(i["L0_attn_a_Wo"], np.float32)
    wb_h = np.zeros((128, 64), np.float32)
    cvw_h = np.zeros((128, 96), np.float32)
    bia_h = np.zeros((128, 32), np.float32)
    Wb = np.asarray(i["L0_attn_a_Wb"], np.float32)
    for c in range(8):
        wb_h[:, c * 8:(c + 1) * 8] = Wb[c * 128:(c + 1) * 128, :]
        for ti, cw in enumerate(("cqw", "ckw", "cvw")):
            w = np.asarray(i["L0_attn_a_" + cw], np.float32)
            for j in range(4):
                cvw_h[:, ti * 32 + c * 4 + j] = w[c * 128:(c + 1) * 128, j]
        for g, bv in enumerate(("ba", "cqb", "ckb", "cvb")):
            b = np.asarray(i["L0_attn_a_" + bv], np.float32)
            bia_h[:, g * 8 + c] = b[c * 128:(c + 1) * 128]
    shared["wb_h"] = wb_h
    shared["cvw_h"] = cvw_h
    shared["bia_h"] = bia_h
    shared["bb1_h"] = np.ascontiguousarray(np.asarray(i["L0_attn_a_bb"], np.float32).reshape(8, 1))
    shared["lnr_h"] = np.concatenate([np.asarray(i["L0_attn_a_ln_g"], np.float32),
                                      np.asarray(i["L0_attn_a_ln_b"], np.float32)]).reshape(1, 256)
    md = np.zeros((128, 128), np.float32)
    md2 = np.zeros((128, 128), np.float32)
    for h in range(8):
        for a_ in range(16):
            for t in range(16):
                if a_ < t:
                    md[h * 16 + a_, h * 16 + t] = -1
                if a_ <= t:
                    md2[h * 16 + a_, h * 16 + t] = 1
    shared["mdiag_h"] = md.astype(_BF)
    shared["mdiag2_h"] = md2.astype(_BF)
    erow_h = np.zeros((8, 1024), np.float32)
    for h in range(8):
        erow_h[h, h * 128:(h + 1) * 128] = 1
    shared["erow_h"] = erow_h.astype(_BF)
    shared["identbf_h"] = np.eye(128, dtype=_BF)
    shared["wupT"] = np.ascontiguousarray(np.asarray(i["L0_ffn_W_up"], np.float32).T)
    shared["wdnT"] = np.ascontiguousarray(np.asarray(i["L0_ffn_W_down"], np.float32).T)
    shared["pqT"] = np.ascontiguousarray(np.asarray(i["L0_ffn_W_q"], np.float32).T)
    keys = np.asarray(i["L0_ffn_keys"], np.float32)
    shared["keysT"] = np.ascontiguousarray(keys.transpose(3, 2, 0, 1).reshape(64, 512))
    shared["deT"] = np.ascontiguousarray(np.asarray(i["L0_ffn_down_embed"], np.float32).T)
    shared["ue"] = np.ascontiguousarray(np.asarray(i["L0_ffn_up_embed"], np.float32))
    shared["abqwT"] = np.ascontiguousarray(np.asarray(i["L0_attn_b_q_w"], np.float32).T)
    shared["abkwT"] = np.ascontiguousarray(np.asarray(i["L0_attn_b_k_w"], np.float32).T)
    shared["abvwT"] = np.ascontiguousarray(np.asarray(i["L0_attn_b_v_w"], np.float32).T)
    shared["abowT"] = np.ascontiguousarray(np.asarray(i["L0_attn_b_o_w"], np.float32).T)
    shared["qb8"] = np.ascontiguousarray(np.asarray(i["L0_attn_b_q_b"], np.float32).reshape(8, 128).T)
    shared["kb8"] = np.ascontiguousarray(np.asarray(i["L0_attn_b_k_b"], np.float32).reshape(8, 128).T)
    shared["ob8"] = np.ascontiguousarray(np.asarray(i["L0_attn_b_o_b"], np.float32).reshape(8, 128).T)
    shared["vb_row"] = np.ascontiguousarray(np.asarray(i["L0_attn_b_v_b"], np.float32).reshape(1, 1024))
    base = np.full((128, 384), -1e30, np.float32)
    for q in range(128):
        base[q, q:q + 257] = 0.0
    mk0 = []
    for qt in range(3):
        m = base.copy()
        cut = max(0, 256 - qt * 128)
        m[:, 0:cut] = -1e30
        mk0.append(m)
    maskc0 = np.concatenate(mk0, 1).astype(_BF)
    maskc1 = np.concatenate([base, base, base], 1).astype(_BF)
    x = np.asarray(i["L0_attn_a_x"], np.float32)
    in_maps = []
    for core in range(8):
        b, hf = core // 2, core % 2
        xbT = np.ascontiguousarray(x[b].T)
        if hf == 0:
            xe = np.concatenate([np.zeros((1024, 256), np.float32), xbT[:, 0:2048]], 1)
        else:
            xe = np.ascontiguousarray(xbT[:, 1792:4096])
        m = dict(shared)
        m["xT"] = xbT
        m["xe"] = xe
        m["selc"] = np.tile(np.array([[1.0 - hf, float(hf)]], np.float32), (128, 1))
        m["maskc"] = maskc0 if hf == 0 else maskc1
        in_maps.append(m)
    return in_maps


def kernel(**inputs):
    if not _NC_CACHE:
        _NC_CACHE.append(_build_full())
    nc = _NC_CACHE[0]
    in_maps = _host_pack(inputs)
    res = run_bass_kernel_spmd(nc, in_maps, core_ids=list(range(8)))
    out = np.zeros((4, 4096, 1024), np.float32)
    for core in range(8):
        b, hf = core // 2, core % 2
        out[b, hf * 2048:(hf + 1) * 2048, :] = np.asarray(res.results[core]["outT"]).T
    return out


# version 09
# speedup vs baseline: 1.1415x; 1.0062x over previous
import sys
sys.path.insert(0, '/opt/trn_rl_repo')
import numpy as np
from contextlib import ExitStack
import concourse.bass as bass
import concourse.mybir as mybir

F32 = mybir.dt.float32
BF16 = mybir.dt.bfloat16
U32 = mybir.dt.uint32
AF = mybir.ActivationFunctionType
ALU = mybir.AluOpType
AX = mybir.AxisListType

KTH_PRECHAIN = False
D = 1024
NCH = 8
TEXT = 2304
TFULL = 4096
NE = 4096
NI = 4096
EPS = 1e-5

class MK:
    def __init__(self, nc, es, dbg=()):
        self.nc = nc
        self.es = es
        self.dbg = set(dbg)
        self.S = lambda g: es.enter_context(g)
        self.sem = {}
        self.cnt = {}
        self.streams = {k: [] for k in ("sp", "act", "dve", "pe", "gp")}
        self.dram = {}
        self.sb = {}

    def new_sem(self, name):
        self.sem[name] = self.S(self.nc.semaphore(name))
        self.cnt[name] = 0

    def init_consts(self, values):
        self.constpool = self.sbuf("constpool", [128, max(8, len(values))], F32)
        for j, v in enumerate(values):
            ap = self.constpool[:, j:j + 1]
            self.nc.const_aps.register(F32, float(v), ap)
            self.em("dve", lambda e, ap=ap, v=v: e.memset(ap, float(v)))

    def sbuf(self, name, shape, dtype):
        t = self.S(self.nc.sbuf_tensor(name, shape, dtype))
        self.sb[name] = t
        return t

    def view(self, name, base, off_f32, cols, dtype):
        if dtype == F32:
            ap = base[:, off_f32: off_f32 + cols]
        else:
            assert dtype == BF16
            ap = base[:, off_f32: off_f32 + (cols + 1) // 2].bitcast(BF16)
        self.sb[name] = ap
        return ap

    def din(self, name, shape, dtype=F32):
        ap = self.nc.dram_tensor(name, shape, dtype, kind="ExternalInput").ap()
        self.dram[name] = ap
        return ap

    def dout(self, name, shape, dtype=F32):
        ap = self.nc.dram_tensor(name, shape, dtype, kind="ExternalOutput").ap()
        self.dram[name] = ap
        return ap

    def dscratch(self, name, shape, dtype):
        ap = self.nc.dram_tensor(name, shape, dtype, kind="Internal").ap()
        self.dram[name] = ap
        return ap

    def em(self, eng, fn):
        self.streams[eng].append(fn)

    def inc(self, sname, by):
        self.cnt[sname] += by
        return self.cnt[sname]

    def dma(self, eng, out, in_, sname):
        tgt = self.inc(sname, 16)
        sem = self.sem[sname]
        self.em(eng, lambda e: e.dma_start(out=out, in_=in_).then_inc(sem, 16))
        return tgt

    def wait(self, eng, sname, tgt):
        sem = self.sem[sname]
        self.em(eng, lambda e: e.wait_ge(sem, tgt))

    def op(self, eng, fn, sname=None, by=1):
        if sname is None:
            self.em(eng, fn)
            return None
        tgt = self.inc(sname, by)
        sem = self.sem[sname]
        def g(e, fn=fn, sem=sem, by=by):
            r = fn(e)
            r.then_inc(sem, by)
        self.em(eng, g)
        return tgt

    def barrier(self, engs=("sp", "act", "dve", "pe", "gp")):
        if "q_sp" in self.sem and "sp" in engs:
            self.wait("sp", "q_sp", self.cnt["q_sp"])
        if "q_gp" in self.sem and "gp" in engs:
            self.wait("gp", "q_gp", self.cnt["q_gp"])
        if "bar" not in self.sem:
            self.new_sem("bar")
        sem = self.sem["bar"]
        n = len(engs)
        base = self.cnt["bar"]
        self.cnt["bar"] = base + n
        tgt = base + n
        for e in engs:
            self.em(e, lambda x: x.drain().then_inc(sem, 1))
        for e in engs:
            self.em(e, lambda x: x.wait_ge(sem, tgt))

    def finalize(self):
        nc = self.nc
        with nc.Block() as block:
            @block.sync
            def _(sync):
                for fn in self.streams["sp"]:
                    fn(sync)
            @block.scalar
            def _(scalar):
                for fn in self.streams["act"]:
                    fn(scalar)
            @block.vector
            def _(vector):
                for fn in self.streams["dve"]:
                    fn(vector)
            @block.tensor
            def _(tensor):
                for fn in self.streams["pe"]:
                    fn(tensor)
            @block.gpsimd
            def _(gpsimd):
                for fn in self.streams["gp"]:
                    fn(gpsimd)



def emit_layernorm_fm(mk, T, src_chunk, xln, es, name, dst_chunk=None, ps4=None):
    if dst_chunk is None:
        dst_chunk = lambda c: xln[:, c * T:(c + 1) * T]
    nc = mk.nc
    nt = (T + 511) // 512
    if ps4 is None:
        ps_sum = es.enter_context(nc.psum_tensor(name + "_pss", [1, 1024], F32))
        ps_bc = es.enter_context(nc.psum_tensor(name + "_psb", [128, 1024], F32))
        ps4 = (ps_sum[0:1, 0:512], ps_sum[0:1, 512:1024], ps_bc[:, 0:512], ps_bc[:, 512:1024])
    psum_a, psum_b, psbc_a, psbc_b = ps4
    xsq = mk.sb["scr512a"]
    xsq2 = mk.sb["scr512b"]
    atmp = mk.sb["scr512c"]
    srow = mk.sb["srow"]
    ones_col = mk.sb["ones_col"]
    ones128 = mk.sb["ones128"]
    for it in range(nt):
        t0 = it * 512
        tw = min(512, T - t0)
        pe0 = mk.cnt["c_pe"]
        for c in range(NCH):
            src = src_chunk(c)
            buf = xsq if c % 2 == 0 else xsq2
            if c >= 2:
                mk.wait("act", "c_pe", pe0 + c - 1)
            tgt_sq = mk.op("act", lambda e, src=src, t0=t0, tw=tw, buf=buf: e.activation(
                out=buf[:, 0:tw], in_=src[:, t0:t0 + tw], func=AF.Square), "c_act")
            mk.wait("pe", "c_act", tgt_sq)
            mk.op("pe", lambda e, src=src, t0=t0, tw=tw, c=c: e.matmul(
                out=psum_a[0:1, 0:tw], lhsT=ones128[:, 0:1], rhs=src[:, t0:t0 + tw],
                start=(c == 0), stop=(c == NCH - 1)))
            mk.op("pe", lambda e, tw=tw, c=c, buf=buf: e.matmul(
                out=psum_b[0:1, 0:tw], lhsT=ones128[:, 0:1], rhs=buf[:, 0:tw],
                start=(c == 0), stop=(c == NCH - 1)), "c_pe")
        tgt_sums = mk.cnt["c_pe"]
        mk.wait("dve", "c_pe", tgt_sums)
        mean = srow[0:1, 0:512]; var = srow[0:1, 512:1024]
        rstd = srow[0:1, 1024:1536]; nmr = srow[0:1, 1536:2048]
        mk.op("dve", lambda e, tw=tw, mean=mean: e.tensor_scalar(out=mean[0:1, 0:tw], in0=psum_a[0:1, 0:tw],
              scalar1=1.0 / D, scalar2=None, op0=ALU.mult))
        mk.op("dve", lambda e, tw=tw, mean=mean: e.tensor_tensor(out=atmp[0:1, 0:tw], in0=mean[0:1, 0:tw],
              in1=mean[0:1, 0:tw], op=ALU.mult))
        mk.op("dve", lambda e, tw=tw, var=var: e.tensor_scalar(out=var[0:1, 0:tw], in0=psum_b[0:1, 0:tw],
              scalar1=1.0 / D, scalar2=None, op0=ALU.mult))
        tgt_v = mk.op("dve", lambda e, tw=tw, var=var: e.tensor_tensor(out=var[0:1, 0:tw], in0=var[0:1, 0:tw],
              in1=atmp[0:1, 0:tw], op=ALU.subtract), "c_dve")
        mk.wait("act", "c_dve", tgt_v)
        mk.op("act", lambda e, tw=tw, var=var, rstd=rstd: e.activation(out=rstd[0:1, 0:tw], in_=var[0:1, 0:tw],
              func=AF.Ln, bias=EPS, scale=1.0))
        tgt_r = mk.op("act", lambda e, tw=tw, rstd=rstd: e.activation(out=rstd[0:1, 0:tw], in_=rstd[0:1, 0:tw],
              func=AF.Exp, bias=0.0, scale=-0.5), "c_act")
        mk.wait("dve", "c_act", tgt_r)
        tgt_n = mk.op("dve", lambda e, tw=tw, mean=mean, rstd=rstd, nmr=nmr: e.scalar_tensor_tensor(
            out=nmr[0:1, 0:tw], in0=mean[0:1, 0:tw], scalar=-1.0, in1=rstd[0:1, 0:tw],
            op0=ALU.mult, op1=ALU.mult), "c_dve")
        mk.wait("pe", "c_dve", tgt_n)
        mk.op("pe", lambda e, tw=tw, rstd=rstd: e.matmul(out=psbc_a[:, 0:tw], lhsT=ones_col[0:1, :],
              rhs=rstd[0:1, 0:tw], start=True, stop=True))
        tgt_bc = mk.op("pe", lambda e, tw=tw, nmr=nmr: e.matmul(out=psbc_b[:, 0:tw], lhsT=ones_col[0:1, :],
              rhs=nmr[0:1, 0:tw], start=True, stop=True), "c_pe")
        mk.wait("dve", "c_pe", tgt_bc)
        for c in range(NCH):
            src = src_chunk(c)
            mk.op("dve", lambda e, src=src, t0=t0, tw=tw: e.tensor_tensor(
                out=atmp[:, 0:tw], in0=src[:, t0:t0 + tw], in1=psbc_a[:, 0:tw], op=ALU.mult))
            dstc = dst_chunk(c)
            mk.op("dve", lambda e, dstc=dstc, t0=t0, tw=tw: e.tensor_tensor(
                out=dstc[:, t0:t0 + tw], in0=atmp[:, 0:tw],
                in1=psbc_b[:, 0:tw], op=ALU.add), "c_dve" if c == NCH - 1 else None)
        mk.wait("pe", "c_dve", mk.cnt["c_dve"])
    return mk.cnt["c_dve"]


def emit_ffn_dense(mk, T, xln, resid_chunk, wupT_hbm, wdownT_hbm, es, name):
    nc = mk.nc
    wup = mk.sb[name + "_wup"]
    wdn = mk.sb[name + "_wdn"]
    mid = mk.sb[name + "_mid"]
    ps = [es.enter_context(nc.psum_tensor(name + "_ps%d" % j, [128, 512], F32)) for j in range(8)]
    ntt = (T + 255) // 256
    for half in range(2):
        i0 = half * 2048
        if half == 1:
            mk.wait("gp", "c_dve", mk.cnt["c_dve"])
        for c in range(NCH):
            mk.dma("gp", wup[:, c * 2048:(c + 1) * 2048], wupT_hbm[c * 128:(c + 1) * 128, i0:i0 + 2048], "q_gp")
        for ic in range(16):
            mk.dma("gp", wdn[:, ic * 1024:(ic + 1) * 1024], wdownT_hbm[i0 + ic * 128: i0 + (ic + 1) * 128, :], "q_gp")
        wtgt = mk.cnt["q_gp"]
        mk.wait("pe", "q_gp", wtgt)
        for it in range(ntt):
            t0 = it * 256
            tw = min(256, T - t0)
            act0 = mk.cnt["c_act"]
            for ic in range(16):
                pm = ps[6 + ic % 2]
                if ic >= 2:
                    mk.wait("pe", "c_act", act0 + ic - 1)
                for k in range(NCH):
                    mk.op("pe", lambda e, k=k, ic=ic, t0=t0, tw=tw, pm=pm: e.matmul(
                        out=pm[:, 0:tw], lhsT=wup[:, k * 2048 + ic * 128: k * 2048 + ic * 128 + 128],
                        rhs=xln[:, k * T + t0: k * T + t0 + tw],
                        start=(k == 0), stop=(k == NCH - 1)), "c_pe" if k == NCH - 1 else None)
                mk.wait("act", "c_pe", mk.cnt["c_pe"])
                mk.op("act", lambda e, pm=pm, ic=ic, tw=tw: e.activation(
                    out=mid[:, ic * 256: ic * 256 + tw], in_=pm[:, 0:tw], func=AF.Silu), "c_act")
            mk.wait("pe", "c_act", mk.cnt["c_act"])
            for ic in range(16):
                for oc in range(NCH):
                    mk.op("pe", lambda e, oc=oc, ic=ic, t0=t0, tw=tw: e.matmul(
                        out=ps[oc][:, 0:tw], lhsT=wdn[:, ic * 1024 + oc * 128: ic * 1024 + oc * 128 + 128],
                        rhs=mid[:, ic * 256: ic * 256 + tw],
                        start=(ic == 0), stop=(ic == 15)),
                        "c_pe" if (ic == 15 and oc == NCH - 1) else None)
            tgt_out = mk.cnt["c_pe"]
            mk.wait("dve", "c_pe", tgt_out)
            for oc in range(NCH):
                dst = resid_chunk(oc)
                mk.op("dve", lambda e, oc=oc, dst=dst, t0=t0, tw=tw: e.tensor_tensor(
                    out=dst[:, t0:t0 + tw], in0=ps[oc][:, 0:tw], in1=dst[:, t0:t0 + tw], op=ALU.add),
                    "c_dve" if oc == NCH - 1 else None)
            mk.wait("pe", "c_dve", mk.cnt["c_dve"])
    return mk.cnt["c_dve"]


def emit_peer(mk, T, xln, resid_chunk, hbm, es, name, GRP=3, stages="123"):
    nc = mk.nc
    qfm = mk.sb[name + "_qfm"]
    sims = mk.sb[name + "_sims"]
    exa = mk.sb[name + "_exa"]
    eya = mk.sb[name + "_eya"]
    taua = mk.sb[name + "_tau"]
    srow = mk.sb[name + "_srow"]
    s8 = mk.sb[name + "_s8"]
    det = mk.sb[name + "_det"]
    ue2 = mk.sb[name + "_ue"]
    dsb = mk.sb[name + "_dsb"]
    dsb2 = mk.sb[name + "_dsb2"]
    pds = mk.sb[name + "_pds"]
    pd2 = mk.sb[name + "_pd2"]
    wtmp = mk.sb[name + "_wtmp"]
    wacc = mk.sb[name + "_wacc"]
    wtw = mk.sb[name + "_wtw"]
    keys = mk.sb[name + "_keys"]
    wq = mk.sb[name + "_wq"]
    ident = mk.sb["identbf"]
    nt = (T + 127) // 128
    ps_d = [es.enter_context(nc.psum_tensor(name + "_psd%d" % j, [128, 512], F32)) for j in range(4)]

    mk.dma("gp", keys[0:64, :], hbm["keysT"], "q_gp")
    mk.dma("gp", keys[64:128, :], hbm["keysT"], "q_gp")
    for c in range(NCH):
        mk.dma("gp", wq[:, c * 512:(c + 1) * 512], hbm["wqT"][c * 128:(c + 1) * 128, :], "q_gp")
    mk.wait("pe", "q_gp", mk.cnt["q_gp"])
    with ExitStack() as es1:
        ps_q = es1.enter_context(nc.psum_tensor(name + "_psq", [128, 512], F32))
        ps_s = es1.enter_context(nc.psum_tensor(name + "_pssim", [128, 512], F32))
        for it5 in range((T + 511) // 512):
            t0g = it5 * 512
            twg = min(512, T - t0g)
            for ph in range(8):
                p, h = ph // 4, ph % 4
                boff = p * 256 + h * 64
                for k in range(NCH):
                    mk.op("pe", lambda e, boff=boff, k=k, t0g=t0g, twg=twg: e.matmul(
                        out=ps_q[0:64, 0:twg], lhsT=wq[:, k * 512 + boff: k * 512 + boff + 64],
                        rhs=xln[:, k * T + t0g:k * T + t0g + twg],
                        start=(k == 0), stop=(k == NCH - 1)), "c_pe" if k == NCH - 1 else None)
                mk.wait("act", "c_pe", mk.cnt["c_pe"])
                tgt = mk.op("act", lambda e, ph=ph, twg=twg: e.activation(
                    out=qfm[0:64, ph * 512: ph * 512 + twg], in_=ps_q[0:64, 0:twg], func=AF.Copy), "c_act")
                mk.wait("pe", "c_act", tgt)
            if stages == "q":
                continue
            for itl in range((twg + 127) // 128):
                it = it5 * 4 + itl
                t0 = itl * 128
                tw = min(128, twg - t0)
                for ph in range(8):
                    mk.op("pe", lambda e, t0=t0, tw=tw, ph=ph: e.matmul(
                        out=ps_s[0:tw, ph * 64:(ph + 1) * 64], lhsT=qfm[0:64, ph * 512 + t0: ph * 512 + t0 + tw],
                        rhs=keys[0:64, ph * 64:(ph + 1) * 64], start=True, stop=True),
                        "c_pe" if ph == 7 else None)
                mk.wait("act", "c_pe", mk.cnt["c_pe"])
                mk.wait("act", "c_dve", mk.cnt["c_dve"])
                tgt = mk.op("act", lambda e, tw=tw: e.activation(out=sims[0:tw, :], in_=ps_s[0:tw, :], func=AF.Copy), "c_act")
                mk.wait("dve", "c_act", tgt)
                if stages == "s":
                    continue
                for h in range(4):
                    sx = sims[:, h * 64:(h + 1) * 64]
                    sy = sims[:, 256 + h * 64: 256 + (h + 1) * 64]
                    vx = s8[:, 0:8]; vy = s8[:, 8:16]; g8 = s8[:, 16:24]
                    negM = s8[:, 24:25]; z = s8[:, 25:26]; negK = s8[:, 26:27]
                    ex8 = s8[:, 32:40]; ey8 = s8[:, 40:48]; p8 = s8[:, 48:56]
                    mk.op("dve", lambda e, sx=sx, vx=vx: e.max(out=vx, in_=sx))
                    mk.op("dve", lambda e, sy=sy, vy=vy: e.max(out=vy, in_=sy))
                    mk.op("dve", lambda e: e.drain())
                    mk.op("dve", lambda e, vx=vx, vy=vy: e.scalar_tensor_tensor(
                        out=srow[:, 0:64], in0=vx.rearrange("p (a o) -> p a o", o=1).broadcast_to([128, 8, 8]),
                        scalar=1.0, in1=vy.rearrange("p (o a) -> p o a", o=1).broadcast_to([128, 8, 8]),
                        op0=ALU.mult, op1=ALU.add))
                    mk.op("dve", lambda e, g8=g8: e.max(out=g8, in_=srow[:, 0:64]))
                    mk.op("dve", lambda e: e.drain())
                    mk.op("dve", lambda e, g8=g8, negM=negM: e.tensor_scalar(
                        out=negM, in0=g8[:, 0:1], scalar1=-1.0, scalar2=None, op0=ALU.mult), "c_dve")
                    mk.wait("act", "c_dve", mk.cnt["c_dve"])
                    tgt0 = mk.op("act", lambda e, g8=g8, negM=negM: e.activation(
                        out=srow[:, 0:8], in_=g8, func=AF.Exp, bias=negM, scale=1.0), "c_act")
                    mk.wait("dve", "c_act", tgt0)
                    tgtz = mk.op("dve", lambda e, z=z: e.reduce_sum(
                        out=z, in_=srow[:, 0:8], axis=AX.X), "c_dve")
                    mk.wait("act", "c_dve", tgtz)
                    tgt1 = mk.op("act", lambda e, z=z: e.activation(out=srow[:, 8:9], in_=z, func=AF.Ln), "c_act")
                    mk.wait("dve", "c_act", tgt1)
                    tgt2 = mk.op("dve", lambda e, negM=negM, negK=negK: e.tensor_tensor(
                        out=negK, in0=negM, in1=srow[:, 8:9], op=ALU.subtract), "c_dve")
                    mk.wait("act", "c_dve", tgt2)
                    exs = exa[:, it * 256 + h * 64: it * 256 + (h + 1) * 64]
                    eys = eya[:, it * 256 + h * 64: it * 256 + (h + 1) * 64]
                    mk.op("act", lambda e, sx=sx, negK=negK, exs=exs: e.activation(
                        out=exs, in_=sx, func=AF.Exp, bias=negK, scale=1.0), "c_act")
                    tgt3 = mk.op("act", lambda e, sy=sy, eys=eys: e.activation(
                        out=eys, in_=sy, func=AF.Exp), "c_act")
                    mk.wait("dve", "c_act", tgt3)
                    mk.op("dve", lambda e, exs=exs, ex8=ex8: e.max(out=ex8, in_=exs))
                    mk.op("dve", lambda e, eys=eys, ey8=ey8: e.max(out=ey8, in_=eys))
                    mk.op("dve", lambda e: e.drain())
                    mk.op("dve", lambda e, ex8=ex8, ey8=ey8: e.scalar_tensor_tensor(
                        out=srow[:, 0:64], in0=ex8.rearrange("p (a o) -> p a o", o=1).broadcast_to([128, 8, 8]),
                        scalar=1.0, in1=ey8.rearrange("p (o a) -> p o a", o=1).broadcast_to([128, 8, 8]),
                        op0=ALU.mult, op1=ALU.mult))
                    mk.op("dve", lambda e, p8=p8: e.max(out=p8, in_=srow[:, 0:64]))
                    mk.op("dve", lambda e: e.drain())
                    mk.op("dve", lambda e, p8=p8, it=it, h=h: e.tensor_copy(
                        out=taua[:, it * 4 + h: it * 4 + h + 1], in_=p8[:, 7:8]), "c_dve")
                mk.wait("pe", "c_act", tgt)
    if stages in ("q", "s") or "2" not in stages:
        return mk.cnt["c_dve"]
    ngr = (nt + GRP - 1) // GRP
    pd2b = pd2.bitcast(BF16)
    sp2 = [pd2b[:, 0:1024], pd2b[:, 1024:2048]]
    evtag = {}
    mbtag = {}
    for g in range(ngr):
        j0 = g * GRP
        jn = min(GRP, nt - j0)
        gw = sum(min(128, T - (j0 + j) * 128) for j in range(jn))
        with ExitStack() as es2:
            ps_tr = es2.enter_context(nc.psum_tensor(name + "_ptr%d" % g, [128, 1024], BF16))
            wtv = wtw[:, 0:32 * gw].rearrange("p (a b) -> p a b", a=32)
            for eq in range(4):
                e0 = eq * 1024
                mk.wait("gp", "c_pe", mk.cnt["c_pe"])
                for k in range(NCH):
                    mk.dma("gp", det[:, k * 1024:(k + 1) * 1024],
                           hbm["deT"][k * 128:(k + 1) * 128, e0:e0 + 1024], "q_gp")
                mk.wait("pe", "q_gp", mk.cnt["q_gp"])
                for j in range(jn):
                    it = j0 + j
                    t0 = it * 128
                    tw = min(128, T - t0)
                    par = (g * 4 * jn + eq * jn + j) % 2
                    pda, pdb = ps_d[par * 2], ps_d[par * 2 + 1]
                    dsl = dsb if par == 0 else dsb2
                    if par in evtag:
                        mk.wait("pe", "c_act", evtag[par])
                    for ns in range(2):
                        pm = pda if ns == 0 else pdb
                        for k in range(NCH):
                            mk.op("pe", lambda e, k=k, pm=pm, t0=t0, tw=tw, ns=ns: e.matmul(
                                out=pm[0:tw, :], lhsT=xln[:, k * T + t0: k * T + t0 + tw],
                                rhs=det[:, k * 1024 + ns * 512: k * 1024 + (ns + 1) * 512],
                                start=(k == 0), stop=(k == NCH - 1)),
                                "c_pe" if k == NCH - 1 else None)
                    mk.wait("act", "c_pe", mk.cnt["c_pe"])
                    if par in mbtag:
                        mk.wait("act", "c_dve", mbtag[par])
                    for ns in range(2):
                        pm = pda if ns == 0 else pdb
                        mk.op("act", lambda e, pm=pm, ns=ns, tw=tw, dsl=dsl: e.activation(
                            out=dsl[0:tw, ns * 512:(ns + 1) * 512], in_=pm[0:tw, :], func=AF.Copy),
                            "c_act" if ns == 1 else None)
                    evtag[par] = mk.cnt["c_act"]
                    tgt_d = mk.cnt["c_act"]
                    t_sil = [None] * 4
                    t_add = [None] * 4
                    for h in range(4):
                        exs = exa[:, it * 256 + h * 64 + eq * 16: it * 256 + h * 64 + eq * 16 + 16]
                        eys = eya[:, it * 256 + h * 64: it * 256 + (h + 1) * 64]
                        tau = taua[:, it * 4 + h: it * 4 + h + 1]
                        mk.op("dve", lambda e, exs=exs, eys=eys, tw=tw: e.tensor_tensor(
                            out=pds[0:tw, :],
                            in0=exs[0:tw].rearrange("p (a o) -> p a o", o=1).broadcast_to([tw, 16, 64]),
                            in1=eys[0:tw].rearrange("p (o a) -> p o a", o=1).broadcast_to([tw, 16, 64]),
                            op=ALU.mult))
                        mk.op("dve", lambda e, tau=tau, tw=tw: e.scalar_tensor_tensor(
                            out=pds[0:tw, :], in0=pds[0:tw, :], scalar=tau, in1=pds[0:tw, :],
                            op0=ALU.is_ge, op1=ALU.mult))
                        if h == 0:
                            mk.wait("dve", "c_act", tgt_d)
                        else:
                            mk.wait("dve", "c_act", t_sil[h - 1])
                        tgt_mb = mk.op("dve", lambda e, tw=tw, dsl=dsl: e.tensor_tensor(
                            out=wtmp[0:tw, :], in0=pds[0:tw, :], in1=dsl[0:tw, :], op=ALU.mult), "c_dve")
                        if h >= 1:
                            mk.wait("dve", "c_act", t_sil[h - 1])
                            t_add[h - 1] = mk.op("dve", lambda e, h=h, tw=tw: e.tensor_tensor(
                                out=wacc[0:tw, :], in0=wacc[0:tw, :], in1=sp2[(h - 1) % 2][0:tw, :],
                                op=ALU.add), "c_dve")
                        mk.wait("act", "c_dve", tgt_mb)
                        if h >= 2 and t_add[h - 2] is not None:
                            mk.wait("act", "c_dve", t_add[h - 2])
                        dst = wacc if h == 0 else sp2[h % 2]
                        t_sil[h] = mk.op("act", lambda e, dst=dst, tw=tw: e.activation(
                            out=dst[0:tw, :], in_=wtmp[0:tw, :], func=AF.Silu), "c_act")
                        if h == 3:
                            mk.wait("dve", "c_act", t_sil[3])
                            t_add[3] = mk.op("dve", lambda e, tw=tw: e.tensor_tensor(
                                out=wacc[0:tw, :], in0=wacc[0:tw, :], in1=sp2[1][0:tw, :],
                                op=ALU.add), "c_dve")
                    mbtag[par] = t_add[3]
                    mk.wait("pe", "c_dve", mk.cnt["c_dve"])
                    mk.wait("pe", "c_act", mk.cnt["c_act"])
                    gcol0 = sum(min(128, T - (j0 + jj) * 128) for jj in range(j))
                    for sl in range(8):
                        mk.op("pe", lambda e, sl=sl, tw=tw: e.transpose(
                            out=ps_tr[:, sl * 128: sl * 128 + tw], in_=wacc[:, sl * 128:(sl + 1) * 128],
                            identity=ident[:, 0:tw]), "c_pe" if sl == 7 else None)
                    mk.wait("act", "c_pe", mk.cnt["c_pe"])
                    mk.op("act", lambda e, eq=eq, gcol0=gcol0, tw=tw, wtv=wtv: e.activation(
                        out=wtv[:, eq * 8: eq * 8 + 8, gcol0:gcol0 + tw],
                        in_=ps_tr[:, 0:1024].rearrange("p (a b) -> p a b", a=8)[:, :, 0:tw],
                        func=AF.Copy), "c_act")
                    mk.wait("pe", "c_act", mk.cnt["c_act"])
        if "3" not in stages:
            continue
        with ExitStack() as es3:
            ps_o = [es3.enter_context(nc.psum_tensor(name + "_pso%d_%d" % (g, j), [128, 512], F32)) for j in range(4)]
            mk.wait("pe", "c_act", mk.cnt["c_act"])
            pe_tgts = {}
            for ec in range(32):
                buf = ue2[:, (ec % 2) * 1024:(ec % 2) * 1024 + 1024]
                if ec >= 2:
                    mk.wait("gp", "c_pe", pe_tgts[ec - 2])
                mk.dma("gp", buf, hbm["ue"][ec * 128:(ec + 1) * 128, :], "q_gp")
                mk.wait("pe", "q_gp", mk.cnt["q_gp"])
                for oc in range(8):
                    psu = ps_d[oc] if oc < 4 else ps_o[oc - 4]
                    mk.op("pe", lambda e, ec=ec, oc=oc, gw=gw, buf=buf, psu=psu: e.matmul(
                        out=psu[:, 0:gw], lhsT=buf[:, oc * 128:(oc + 1) * 128],
                        rhs=wtw[:, ec * gw:(ec + 1) * gw],
                        start=(ec == 0), stop=(ec == 31)),
                        "c_pe" if oc == 7 else None)
                pe_tgts[ec] = mk.cnt["c_pe"]
            mk.wait("dve", "c_pe", mk.cnt["c_pe"])
            for oc in range(8):
                psu = ps_d[oc] if oc < 4 else ps_o[oc - 4]
                dst = resid_chunk(oc)
                col = 0
                for j in range(jn):
                    it = j0 + j
                    t0 = it * 128
                    tw = min(128, T - t0)
                    mk.op("dve", lambda e, dst=dst, t0=t0, tw=tw, col=col, psu=psu: e.tensor_tensor(
                        out=dst[:, t0:t0 + tw], in0=psu[:, col:col + tw],
                        in1=dst[:, t0:t0 + tw], op=ALU.add), "c_dve")
                    col += tw
            mk.wait("pe", "c_dve", mk.cnt["c_dve"])
            mk.wait("gp", "c_dve", mk.cnt["c_dve"])
    return mk.cnt["c_dve"]

def emit_attnb(mk, T, TQ0, xln, resid_chunk, hbm, maskc, es, name):
    nc = mk.nc
    wbuf = mk.sb[name + "_wbuf"]
    kw = vw = qw = wbuf
    ow = wbuf
    kroll = mk.sb[name + "_kroll"]
    vroll = mk.sb[name + "_vroll"]
    qbuf = mk.sb[name + "_qbuf"]
    att = mk.sb[name + "_att"]
    exps = mk.sb[name + "_exps"]
    pbf = mk.sb[name + "_pbf"]
    pbft = mk.sb[name + "_pbft"]
    red = mk.sb[name + "_red"]
    bq = mk.sb[name + "_bq"]; bk = mk.sb[name + "_bk"]; bo = mk.sb[name + "_bo"]
    vbb = mk.sb[name + "_vbb"]
    vbr = mk.sb[name + "_vbr"]
    mkc = mk.sb[name + "_mkc"]
    ident = mk.sb["identbf"]
    ones_col = mk.sb["ones_col"]
    NT = T // 128
    NQ = (T - TQ0) // 128
    L_tgte = [0]
    L_pbft = [None, None]
    ps_sc = es.enter_context(nc.psum_tensor(name + "_psc", [128, 2048], F32))
    ps_tr = [es.enter_context(nc.psum_tensor(name + "_ptr%d" % j, [128, 512], BF16)) for j in range(2)]
    ps_at = [es.enter_context(nc.psum_tensor(name + "_pat%d" % j, [128, 512], F32)) for j in range(2)]

    def load_w(which, half=None):
        mk.wait("gp", "c_pe", mk.cnt["c_pe"])
        if half is None:
            for c in range(NCH):
                mk.dma("gp", wbuf[:, c * 1024:(c + 1) * 1024], hbm[which][c * 128:(c + 1) * 128, :], "q_gp")
        else:
            for c in range(NCH):
                mk.dma("gp", wbuf[:, c * 512:(c + 1) * 512],
                       hbm[which][c * 128:(c + 1) * 128, half * 512:(half + 1) * 512], "q_gp")
        mk.wait("pe", "q_gp", mk.cnt["q_gp"])
    mk.dma("gp", bq[:, :], hbm["qb8"], "q_gp")
    mk.dma("gp", bk[:, :], hbm["kb8"], "q_gp")
    mk.dma("gp", bo[:, :], hbm["ob8"], "q_gp")
    mk.dma("gp", vbr[:, :], hbm["vb_row"], "q_gp")
    mk.dma("gp", mkc[:, :], maskc, "q_gp")
    mk.wait("pe", "q_gp", mk.cnt["q_gp"])
    mk.op("pe", lambda e: e.matmul(out=ps_at[0][:, 0:512], lhsT=ones_col[0:1, :], rhs=vbr[0:1, 0:512],
          start=True, stop=True))
    mk.op("pe", lambda e: e.matmul(out=ps_at[1][:, 0:512], lhsT=ones_col[0:1, :], rhs=vbr[0:1, 512:1024],
          start=True, stop=True), "c_pe")
    mk.wait("act", "c_pe", mk.cnt["c_pe"])
    mk.op("act", lambda e: e.activation(out=vbb[:, 0:512], in_=ps_at[0][:, 0:512], func=AF.Copy))
    tgtv = mk.op("act", lambda e: e.activation(out=vbb[:, 512:1024], in_=ps_at[1][:, 0:512], func=AF.Copy), "c_act")
    mk.wait("pe", "c_act", tgtv)

    def proj_k(tiles):
        t0 = tiles[0] * 128
        tw = 128 * len(tiles)
        load_w("kwT")
        for oc in range(NCH):
            for k in range(NCH):
                mk.op("pe", lambda e, oc=oc, k=k, t0=t0, tw=tw: e.matmul(
                    out=ps_sc[:, 0:tw], lhsT=kw[:, k * 1024 + oc * 128: k * 1024 + oc * 128 + 128],
                    rhs=xln[:, k * T + t0: k * T + t0 + tw],
                    start=(k == 0), stop=(k == NCH - 1)), "c_pe" if k == NCH - 1 else None)
            mk.wait("act", "c_pe", mk.cnt["c_pe"])
            tgt = None
            for j, tj in enumerate(tiles):
                sl = tj % 7
                tgt = mk.op("act", lambda e, oc=oc, j=j, sl=sl: e.activation(
                    out=kroll[:, oc * 896 + sl * 128: oc * 896 + sl * 128 + 128],
                    in_=ps_sc[:, j * 128:(j + 1) * 128], func=AF.Identity,
                    bias=bk[:, oc:oc + 1], scale=1.0), "c_act")
            mk.wait("pe", "c_act", tgt)

    def proj_v(tiles):
        load_w("vwT")
        for tj in tiles:
            sl = tj % 7
            for nh in range(2):
                for k in range(NCH):
                    mk.op("pe", lambda e, tj=tj, nh=nh, k=k: e.matmul(
                        out=ps_sc[:, nh * 512:(nh + 1) * 512], lhsT=xln[:, k * T + tj * 128: k * T + tj * 128 + 128],
                        rhs=vw[:, k * 1024 + nh * 512: k * 1024 + (nh + 1) * 512],
                        start=(k == 0), stop=(k == NCH - 1)), "c_pe" if k == NCH - 1 else None)
            mk.wait("dve", "c_pe", mk.cnt["c_pe"])
            tgt = mk.op("dve", lambda e, sl=sl: e.tensor_tensor(
                out=vroll[:, sl * 1024:(sl + 1) * 1024], in0=ps_sc[:, 0:1024], in1=vbb[:, :], op=ALU.add), "c_dve")
            mk.wait("pe", "c_dve", tgt)

    proj_k([0, 1, 2])
    proj_v([0, 1, 2])
    for g in range((NQ + 3) // 4):
        nq = min(4, NQ - g * 4)
        newt = [t for t in range(g * 4 + 3, min(g * 4 + 7, NT))]
        if newt:
            proj_k(newt)
            proj_v(newt)
        q0 = TQ0 + g * 512
        qn = 128 * nq
        load_w("qwT")
        for oc in range(NCH):
            for k in range(NCH):
                mk.op("pe", lambda e, oc=oc, k=k, q0=q0, qn=qn: e.matmul(
                    out=ps_sc[:, 0:qn], lhsT=qw[:, k * 1024 + oc * 128: k * 1024 + oc * 128 + 128],
                    rhs=xln[:, k * T + q0: k * T + q0 + qn],
                    start=(k == 0), stop=(k == NCH - 1)), "c_pe" if k == NCH - 1 else None)
            mk.wait("act", "c_pe", mk.cnt["c_pe"])
            tgt = mk.op("act", lambda e, oc=oc, qn=qn: e.activation(
                out=qbuf[:, oc * 512: oc * 512 + qn], in_=ps_sc[:, 0:qn], func=AF.Identity,
                bias=bq[:, oc:oc + 1], scale=1.0), "c_act")
            mk.wait("pe", "c_act", tgt)
        for lq in range(nq):
            qt = g * 4 + lq
            mi = min(qt, 2)
            for hg in range(4):
                mk.wait("pe", "c_act", L_tgte[0])
                for hh in range(4):
                    h = hg * 4 + hh
                    ch, ro = h // 2, (h % 2) * 64
                    for c3 in range(3):
                        sl = (qt + c3) % 7
                        mk.op("pe", lambda e, hh=hh, ch=ch, ro=ro, lq=lq, sl=sl, c3=c3: e.matmul(
                            out=ps_sc[0:128, hh * 512 + c3 * 128: hh * 512 + (c3 + 1) * 128],
                            lhsT=qbuf[ro:ro + 64, ch * 512 + lq * 128: ch * 512 + lq * 128 + 128],
                            rhs=kroll[ro:ro + 64, ch * 896 + sl * 128: ch * 896 + sl * 128 + 128],
                            start=(c3 == 0), stop=False))
                    mk.op("pe", lambda e, hh=hh, mi=mi: e.matmul(
                        out=ps_sc[0:128, hh * 512: hh * 512 + 384], lhsT=ident[:, :],
                        rhs=mkc[:, mi * 384:(mi + 1) * 384], start=False, stop=True),
                        "c_pe" if hh == 3 else None)
                mk.wait("dve", "c_pe", mk.cnt["c_pe"])
                scv = ps_sc[:, 0:2048].rearrange("p (a b) -> p a b", a=4)[:, :, 0:384]
                mk.op("dve", lambda e, scv=scv: e.reduce_max(out=red[:, 0:4], in_=scv, axis=AX.X))
                mk.op("dve", lambda e: e.drain())
                tgtn = mk.op("dve", lambda e: e.tensor_scalar(
                    out=red[:, 4:8], in0=red[:, 0:4], scalar1=-0.125, scalar2=None, op0=ALU.mult), "c_dve")
                mk.wait("act", "c_dve", tgtn)
                tgte = None
                for hh in range(4):
                    tgte = mk.op("act", lambda e, hh=hh: e.activation(
                        out=exps[:, hh * 384:(hh + 1) * 384], in_=ps_sc[:, hh * 512: hh * 512 + 384],
                        func=AF.Exp, bias=red[:, 4 + hh:5 + hh], scale=0.125,
                        accum_out=red[:, 8 + hh:9 + hh]), "c_act")
                L_tgte[0] = tgte
                mk.wait("dve", "c_act", tgte)
                mk.op("dve", lambda e: e.reciprocal(out=red[:, 12:16], in_=red[:, 8:12]))
                mk.op("dve", lambda e: e.drain())
                tgtp = None
                for hh in range(4):
                    tgtp = mk.op("dve", lambda e, hh=hh: e.tensor_scalar(
                        out=pbf[:, hh * 384:(hh + 1) * 384], in0=exps[:, hh * 384:(hh + 1) * 384],
                        scalar1=red[:, 12 + hh:13 + hh], scalar2=None, op0=ALU.mult), "c_dve")
                mk.wait("pe", "c_dve", tgtp)
                for hh in range(4):
                    h = hg * 4 + hh
                    ro = (h % 2) * 64
                    pp = hh % 2
                    if L_pbft[pp]:
                        mk.wait("pe", "c_act", L_pbft[pp])
                    for c3 in range(3):
                        mk.op("pe", lambda e, hh=hh, pp=pp, c3=c3: e.transpose(
                            out=ps_tr[pp][:, c3 * 128:(c3 + 1) * 128],
                            in_=pbf[:, hh * 384 + c3 * 128: hh * 384 + (c3 + 1) * 128],
                            identity=ident[:, :]), "c_pe" if c3 == 2 else None)
                    mk.wait("act", "c_pe", mk.cnt["c_pe"])
                    tgt = mk.op("act", lambda e, pp=pp: e.activation(
                        out=pbft[:, pp * 384:(pp + 1) * 384], in_=ps_tr[pp][:, 0:384], func=AF.Copy), "c_act")
                    L_pbft[pp] = tgt
                    mk.wait("pe", "c_act", tgt)
                    for c3 in range(3):
                        sl = (qt + c3) % 7
                        mk.op("pe", lambda e, h=h, hh=hh, pp=pp, sl=sl, c3=c3, ro=ro: e.matmul(
                            out=ps_at[(h // 2) % 2][ro:ro + 64, 0:128],
                            lhsT=vroll[:, sl * 1024 + h * 64: sl * 1024 + h * 64 + 64],
                            rhs=pbft[:, pp * 384 + c3 * 128: pp * 384 + (c3 + 1) * 128],
                            start=(c3 == 0), stop=(c3 == 2), tile_position=(0, ro)),
                            "c_pe" if c3 == 2 else None)
                    if h % 2 == 1:
                        mk.wait("act", "c_pe", mk.cnt["c_pe"])
                        tgt = mk.op("act", lambda e, h=h, lq=lq: e.activation(
                            out=att[:, (h // 2) * 512 + lq * 128: (h // 2) * 512 + lq * 128 + 128],
                            in_=ps_at[(h // 2) % 2][:, 0:128], func=AF.Copy), "c_act")
                        mk.wait("pe", "c_act", tgt)
        mk.wait("pe", "c_act", mk.cnt["c_act"])
        for half in range(2):
            load_w("owT", half=half)
            for ic in range(NCH):
                for oc in range(4):
                    occ = half * 4 + oc
                    mk.op("pe", lambda e, ic=ic, oc=oc, occ=occ, qn=qn: e.matmul(
                        out=ps_sc[:, oc * 512: oc * 512 + qn], lhsT=ow[:, ic * 512 + oc * 128: ic * 512 + oc * 128 + 128],
                        rhs=att[:, ic * 512: ic * 512 + qn],
                        start=(ic == 0), stop=(ic == NCH - 1)),
                        "c_pe" if (ic == NCH - 1 and oc == 3) else None)
            mk.wait("dve", "c_pe", mk.cnt["c_pe"])
            for oc in range(4):
                occ = half * 4 + oc
                dst = resid_chunk(occ)
                mk.op("dve", lambda e, oc=oc, occ=occ, dst=dst, q0=q0, qn=qn: e.scalar_tensor_tensor(
                    out=dst[:, q0:q0 + qn], in0=ps_sc[:, oc * 512: oc * 512 + qn],
                    scalar=bo[:, occ:occ + 1], in1=dst[:, q0:q0 + qn],
                    op0=ALU.add, op1=ALU.add), "c_dve" if oc == 3 else None)
            mk.wait("pe", "c_dve", mk.cnt["c_dve"])
    return mk.cnt["c_dve"]


def emit_attna(mk, TF, xT_hbm, hbm, scr, es, name):
    nc = mk.nc
    BL = 512
    NB = TF // BL
    S = mk.sb
    xab = S[name + "_xab"]; xlnb = S[name + "_xln"]; pre = S[name + "_pre"]
    qs = S[name + "_qs"]; ks = S[name + "_ks"]; vs = S[name + "_vs"]
    acp = S[name + "_acp"]
    afm = S[name + "_afm"]; lam = S[name + "_lam"]
    gam = S[name + "_gam"]; ivg = S[name + "_ivg"]
    cb = S[name + "_cb"]; bt = S[name + "_bt"]; bbc = S[name + "_bbc"]
    beta8 = S[name + "_beta8"]; gfm = S[name + "_gfm"]; og = S[name + "_og"]
    st = S[name + "_st"]; wbuf = S[name + "_wbuf"]; wb8 = S[name + "_wb8"]
    cvw = S[name + "_cvw"]; bia = S[name + "_bia"]; bb1 = S[name + "_bb1"]
    lgb = S[name + "_lgb"]; lbb = S[name + "_lbb"]; lgr = S[name + "_lgr"]
    gbd = S[name + "_gbd"]; kqb = S[name + "_kqb"]
    pkt = S[name + "_pkt"]
    wht = S[name + "_wht"]
    kth = S[name + "_kth"]; wth = S[name + "_wth"]
    osb = S[name + "_osb"]; osq = S[name + "_osq"]; ored = S[name + "_ored"]
    r2f = S[name + "_r2f"]; ogp = S[name + "_ogp"]; wfm = S[name + "_wfm"]
    ascr = S[name + "_ascr"]; ascr2 = S[name + "_ascr2"]; zbuf = S[name + "_zbuf"]
    gc8 = S[name + "_gc8"]; erow = S[name + "_erow"]
    mdiag = S[name + "_mdiag"]; mdiag2 = S[name + "_mdiag2"]
    lamc = S[name + "_lamc"]; tls = S[name + "_tls"]
    ident = S["identbf"]
    ones_col = S["ones_col"]
    ps_pp = es.enter_context(nc.psum_tensor(name + "_ppp", [128, 512], F32))
    ps_g = es.enter_context(nc.psum_tensor(name + "_pg", [128, 512], F32))
    ps_tr = es.enter_context(nc.psum_tensor(name + "_ptr", [128, 1024], BF16))
    ps_tr2 = es.enter_context(nc.psum_tensor(name + "_ptr2", [128, 1024], BF16))
    ps_acc = es.enter_context(nc.psum_tensor(name + "_pacc", [128, 512], F32))
    ps_o = es.enter_context(nc.psum_tensor(name + "_po", [128, 512], F32))
    ps_s = [es.enter_context(nc.psum_tensor(name + "_psu%d" % j, [128, 512], F32)) for j in range(2)]

    WSLOT = [wbuf[:, 0:8192], wbuf[:, 8192:16384]]
    wcnt = [0]
    wguard = [0, 0]

    def load_w(which, avoid=None):
        idx = (1 - avoid) if avoid is not None else wcnt[0] % 2
        s = WSLOT[idx]
        wcnt[0] += 1
        mk.wait("gp", "c_pe", wguard[idx])
        for c in range(NCH):
            mk.dma("gp", s[:, c * 1024:(c + 1) * 1024], hbm[which][c * 128:(c + 1) * 128, :], "q_gp")
        return s, mk.cnt["q_gp"], idx

    def mark_used(idx):
        wguard[idx] = mk.cnt["c_pe"]

    mk.dma("gp", wb8[:, :], hbm["wb"], "q_gp")
    mk.dma("gp", cvw[:, :], hbm["cvw"], "q_gp")
    mk.dma("gp", bia[:, :], hbm["bia"], "q_gp")
    mk.dma("gp", bb1[:, :], hbm["bb1"], "q_gp")
    mk.dma("gp", lgr[0:1, :], hbm["lnr"], "q_gp")
    mk.dma("gp", mdiag[:, :], hbm["mdiag"], "q_gp")
    mk.dma("gp", mdiag2[:, :], hbm["mdiag2"], "q_gp")
    mk.dma("gp", erow[:, :], hbm["erow"], "q_gp")
    mk.wait("pe", "q_gp", mk.cnt["q_gp"])
    mk.op("pe", lambda e: e.matmul(out=ps_o[:, 0:128], lhsT=ones_col[0:1, :],
          rhs=lgr[0:1, 0:128], start=True, stop=True))
    mk.op("pe", lambda e: e.matmul(out=ps_o[:, 128:256], lhsT=ones_col[0:1, :], rhs=lgr[0:1, 128:256],
          start=True, stop=True), "c_pe")
    mk.wait("act", "c_pe", mk.cnt["c_pe"])
    mk.op("act", lambda e: e.activation(out=lgb[:, :], in_=ps_o[:, 0:128], func=AF.Copy))
    t0_ = mk.op("act", lambda e: e.activation(out=lbb[:, :], in_=ps_o[:, 128:256], func=AF.Copy), "c_act")
    mk.wait("pe", "c_act", t0_)
    mk.op("dve", lambda e: e.memset(st[:, :], 0.0))
    mk.op("dve", lambda e: e.memset(pre[:, :], 0.0))
    mk.op("dve", lambda e: e.memset(tls[:, :], 0.0))
    mk.op("dve", lambda e: e.memset(wht[:, :], 0.0))
    mk.op("dve", lambda e: e.memset(zbuf[:, :], 0.0))
    tgt = mk.op("dve", lambda e: e.memset(lamc[:, :], 0.0), "c_dve")
    mk.wait("pe", "c_dve", tgt)
    w_next, wt_next, wi_next = load_w("waT")

    for b in range(NB):
        tb = b * BL
        for hb in range(2):
            t0 = tb + hb * 256
            mk.wait("sp", "c_dve", mk.cnt["c_dve"])
            for c in range(NCH):
                mk.dma("sp", xab[:, c * 256:(c + 1) * 256], xT_hbm[c * 128:(c + 1) * 128, t0:t0 + 256], "q_sp")
            xt = mk.cnt["q_sp"]
            mk.wait("act", "q_sp", xt)
            mk.wait("pe", "q_sp", xt)
            mk.wait("dve", "q_sp", xt)
            with ExitStack() as esl:
                emit_layernorm_fm(mk, 256, lambda c: xab[:, c * 256:(c + 1) * 256], None, esl,
                                  name + "_ln%d_%d" % (b, hb),
                                  dst_chunk=lambda c: xlnb[:, c * 512 + hb * 256: c * 512 + hb * 256 + 256],
                                  ps4=(ps_pp[0:1, 0:512], ps_g[0:1, 0:512], ps_s[0][:, 0:512], ps_s[1][:, 0:512]))
        w_a, wt_a, wi_a = w_next, wt_next, wi_next
        mk.wait("pe", "q_gp", wt_a)
        w_next, wt_next, wi_next = load_w("wqT", avoid=wi_a)
        ptag = [0, 0]
        for oc in range(NCH):
            pb = ps_pp if oc % 2 == 0 else ps_g
            mk.wait("pe", "c_act", ptag[oc % 2])
            for k in range(NCH):
                mk.op("pe", lambda e, oc=oc, k=k, w_a=w_a, pb=pb: e.matmul(
                    out=pb[:, 0:512],
                    lhsT=w_a[:, k * 1024 + oc * 128: k * 1024 + oc * 128 + 128],
                    rhs=xlnb[:, k * 512:(k + 1) * 512],
                    start=(k == 0), stop=(k == NCH - 1)), "c_pe" if k == NCH - 1 else None)
            mk.wait("act", "c_pe", mk.cnt["c_pe"])
            ptag[oc % 2] = mk.op("act", lambda e, oc=oc, pb=pb: e.activation(
                out=acp[:, oc * 512:(oc + 1) * 512], in_=pb[:, 0:512], func=AF.Sigmoid,
                bias=bia[:, oc:oc + 1], scale=1.0), "c_act")
        mk.wait("pe", "c_act", ptag[0])
        mark_used(wi_a)
        for k in range(NCH):
            mk.op("pe", lambda e, k=k: e.matmul(
                out=ps_pp[0:8, 0:512], lhsT=wb8[:, k * 8:(k + 1) * 8],
                rhs=xlnb[:, k * 512:(k + 1) * 512],
                start=(k == 0), stop=(k == NCH - 1)), "c_pe" if k == NCH - 1 else None)
        mk.wait("act", "c_pe", mk.cnt["c_pe"])
        tgt = mk.op("act", lambda e: e.activation(out=beta8[0:8, :], in_=ps_pp[0:8, 0:512],
              func=AF.Sigmoid, bias=bb1[0:8, 0:1], scale=1.0), "c_act")
        mk.wait("pe", "c_act", tgt)
        mk.wait("gp", "c_act", tgt)
        bw = mk.dma("gp", scr["beta"][0:8, tb:tb + 512], beta8[0:8, :], "q_gp")
        mk.wait("gp", "q_gp", bw)
        for h in range(NCH):
            mk.dma("gp", bbc[:, h * 512:(h + 1) * 512],
                   scr["beta"][h:h + 1, tb:tb + 512].broadcast_to([128, 512]), "q_gp")
        bbt = mk.cnt["q_gp"]
        mk.wait("dve", "q_gp", bbt)
        mk.wait("act", "q_gp", bbt)
        for oc in range(NCH):
            mk.op("act", lambda e, oc=oc: e.activation(
                out=afm[:, oc * 512:(oc + 1) * 512], in_=acp[:, oc * 512:(oc + 1) * 512],
                func=AF.Ln), "c_act" if oc == NCH - 1 else None)
        mk.wait("dve", "c_act", mk.cnt["c_act"])
        for hb in range(2):
            for oc in range(NCH):
                mk.op("dve", lambda e, oc=oc, hb=hb: e.tensor_tensor_scan(
                    out=lam[:, oc * 256:(oc + 1) * 256],
                    data0=afm[:, oc * 512 + hb * 256: oc * 512 + hb * 256 + 256],
                    data1=zbuf[:, 0:256], initial=lamc[:, oc:oc + 1],
                    op0=ALU.add, op1=ALU.add))
                mk.op("dve", lambda e: e.drain())
                mk.op("dve", lambda e, oc=oc: e.tensor_copy(out=ascr[:, 0:1], in_=lamc[:, oc:oc + 1]))
                mk.op("dve", lambda e, oc=oc: e.tensor_copy(
                    out=ascr[:, 1:16], in_=lam[:, oc * 256 + 15: oc * 256 + 255].rearrange(
                        "p (a b) -> p a b", b=16)[:, :, 0]))
                mk.op("dve", lambda e, oc=oc: e.tensor_copy(out=lamc[:, oc:oc + 1],
                      in_=lam[:, oc * 256 + 255: oc * 256 + 256]))
                mk.op("dve", lambda e: e.drain())
                tgt = mk.op("dve", lambda e, oc=oc: e.tensor_tensor(
                    out=ascr[:, 256:512], in0=lam[:, oc * 256:(oc + 1) * 256],
                    in1=ascr[:, 0:16].rearrange("p (a o) -> p a o", o=1).broadcast_to([128, 16, 16]),
                    op=ALU.subtract), "c_dve")
                mk.wait("act", "c_dve", tgt)
                mk.op("act", lambda e, oc=oc, hb=hb: e.activation(
                    out=gam.rearrange("p (n g d) -> p n g d", g=8, d=16)[:, hb * 16:hb * 16 + 16, oc, :],
                    in_=ascr[:, 256:512].rearrange("p (n d) -> p n d", d=16), func=AF.Exp))
                tgt = mk.op("act", lambda e, oc=oc, hb=hb: e.activation(
                    out=ivg[:, oc * 512 + hb * 256: oc * 512 + hb * 256 + 256],
                    in_=ascr[:, 256:512], func=AF.Exp, bias=0.0, scale=-1.0), "c_act")
                mk.wait("dve", "c_act", tgt)
        for ti, (tn, wname, has_scale) in enumerate((("q", "wkT", False), ("k", "wvT", True), ("v", "wgT", False))):
            w_cur, wt_cur, wi_cur = w_next, wt_next, wi_next
            mk.wait("pe", "q_gp", wt_cur)
            w_next, wt_next, wi_next = load_w(wname, avoid=wi_cur)
            dst = (qs, ks, vs)[ti]
            for oc in range(NCH):
                mk.op("dve", lambda e, oc=oc, ti=ti: e.tensor_copy(
                    out=pre[:, oc * 520: oc * 520 + 3], in_=tls[:, ti * 24 + oc * 3: ti * 24 + oc * 3 + 3]))
            mk.op("dve", lambda e: e.drain())
            for oc in range(NCH):
                pb = ps_pp if oc % 2 == 0 else ps_g
                mk.wait("pe", "c_act", ptag[oc % 2])
                for k in range(NCH):
                    mk.op("pe", lambda e, oc=oc, k=k, w_cur=w_cur, pb=pb: e.matmul(
                        out=pb[:, 0:512], lhsT=w_cur[:, k * 1024 + oc * 128: k * 1024 + oc * 128 + 128],
                        rhs=xlnb[:, k * 512:(k + 1) * 512],
                        start=(k == 0), stop=(k == NCH - 1)), "c_pe" if k == NCH - 1 else None)
                mk.wait("act", "c_pe", mk.cnt["c_pe"])
                tgt = mk.op("act", lambda e, oc=oc, pb=pb: e.activation(
                    out=pre[:, oc * 520 + 3: oc * 520 + 515], in_=pb[:, 0:512], func=AF.Copy), "c_act")
                ptag[oc % 2] = tgt
                mk.wait("dve", "c_act", tgt)
                mk.op("dve", lambda e, oc=oc, ti=ti: e.tensor_scalar(
                    out=ascr[:, 0:512], in0=pre[:, oc * 520: oc * 520 + 512],
                    scalar1=cvw[:, ti * 32 + oc * 4: ti * 32 + oc * 4 + 1],
                    scalar2=bia[:, (ti + 1) * 8 + oc: (ti + 1) * 8 + oc + 1],
                    op0=ALU.mult, op1=ALU.add))
                for j in range(1, 4):
                    mk.op("dve", lambda e, oc=oc, ti=ti, j=j: e.scalar_tensor_tensor(
                        out=ascr[:, 0:512], in0=pre[:, oc * 520 + j: oc * 520 + j + 512],
                        scalar=cvw[:, ti * 32 + oc * 4 + j: ti * 32 + oc * 4 + j + 1],
                        in1=ascr[:, 0:512], op0=ALU.mult, op1=ALU.add))
                tgt = mk.op("dve", lambda e, oc=oc, ti=ti: e.tensor_copy(
                    out=tls[:, ti * 24 + oc * 3: ti * 24 + oc * 3 + 3],
                    in_=pre[:, oc * 520 + 512: oc * 520 + 515]), "c_dve")
                mk.wait("act", "c_dve", tgt)
                if ti < 2:
                    tgt = mk.op("act", lambda e, oc=oc, dst=dst: e.activation(
                        out=dst.rearrange("p (n g d) -> p n g d", g=8, d=16)[:, :, oc, :],
                        in_=ascr[:, 0:512].rearrange("p (n d) -> p n d", d=16),
                        func=AF.Silu), "c_act")
                else:
                    tgt = mk.op("act", lambda e, oc=oc, dst=dst: e.activation(
                        out=dst[:, oc * 512:(oc + 1) * 512], in_=ascr[:, 0:512],
                        func=AF.Silu), "c_act")
                mk.wait("dve", "c_act", tgt)
            mark_used(wi_cur)
            if has_scale:
                mk.op("dve", lambda e, dst=dst: e.tensor_scalar(
                    out=dst[:, :], in0=dst[:, :],
                    scalar1=0.08838834764831845, scalar2=None, op0=ALU.mult), "c_dve")
        w_cur, wt_cur, wi_cur = w_next, wt_next, wi_next
        mk.wait("pe", "q_gp", wt_cur)
        if b < NB - 1:
            w_next, wt_next, wi_next = load_w("waT", avoid=wi_cur)
        for oc in range(NCH):
            pb = ps_pp if oc % 2 == 0 else ps_g
            mk.wait("pe", "c_act", ptag[oc % 2])
            for k in range(NCH):
                mk.op("pe", lambda e, oc=oc, k=k, w_cur=w_cur, pb=pb: e.matmul(
                    out=pb[:, 0:512], lhsT=w_cur[:, k * 1024 + oc * 128: k * 1024 + oc * 128 + 128],
                    rhs=xlnb[:, k * 512:(k + 1) * 512],
                    start=(k == 0), stop=(k == NCH - 1)), "c_pe" if k == NCH - 1 else None)
            mk.wait("act", "c_pe", mk.cnt["c_pe"])
            ptag[oc % 2] = mk.op("act", lambda e, oc=oc, pb=pb: e.activation(
                out=gfm.rearrange("p (n g d) -> p n g d", g=8, d=16)[:, :, oc, :],
                in_=pb[:, 0:512].rearrange("p (n d) -> p n d", d=16), func=AF.Sigmoid), "c_act")
        mk.wait("pe", "c_act", max(ptag))
        mark_used(wi_cur)
        mk.wait("act", "c_dve", mk.cnt["c_dve"])
        for oc in range(NCH):
            mk.op("act", lambda e, oc=oc: e.activation(
                out=ascr2[:, 0:512], in_=afm[:, oc * 512:(oc + 1) * 512], func=AF.Exp,
                bias=0.0, scale=-1.0), "c_act")
            mk.wait("dve", "c_act", mk.cnt["c_act"])
            mk.op("dve", lambda e, oc=oc: e.tensor_tensor(
                out=bt.rearrange("p (n g d) -> p n g d", g=8, d=16)[:, :, oc, :],
                in0=ascr2[:, 0:512].rearrange("p (n d) -> p n d", d=16),
                in1=bbc[:, oc * 512:(oc + 1) * 512].rearrange("p (n d) -> p n d", d=16), op=ALU.mult))
            mk.op("dve", lambda e, oc=oc: e.tensor_tensor(
                out=ascr2[:, 0:512], in0=vs[:, oc * 512:(oc + 1) * 512],
                in1=ivg[:, oc * 512:(oc + 1) * 512], op=ALU.mult))
            tgt = mk.op("dve", lambda e, oc=oc: e.tensor_tensor(
                out=cb[:, oc * 512:(oc + 1) * 512], in0=ascr2[:, 0:512],
                in1=acp[:, oc * 512:(oc + 1) * 512], op=ALU.mult), "c_dve")
            mk.wait("act", "c_dve", tgt)
        if "dbg_ks" in scr and b == NB - 1:
            tgt = mk.cnt["c_dve"]
            mk.wait("gp", "c_dve", tgt)
            mk.dma("gp", scr["dbg_ks"], ks[:, :], "q_gp")
        prevctx = None
        for n in range(32):
            prevctx = emit_attna_chunk(mk, nc, name, n, locals(), prevctx)
        t_lastog = emit_attna_partO(mk, name, locals(), prevctx)
        if "dbg0" in scr and b == 0:
            mk.wait("gp", "c_dve", mk.cnt["c_dve"])
            mk.wait("gp", "c_act", mk.cnt["c_act"])
            for dnm, dsrc in (("og", og), ("gam", gam), ("cb", cb), ("st", st), ("ks", ks), ("qs", qs)):
                mk.dma("gp", scr["dbg0"][dnm], dsrc[:, :], "q_gp")
        w_cur2, wt_cur2, wi_cur2 = load_w("woT", avoid=(wi_next if b < NB - 1 else None))
        mk.wait("pe", "q_gp", wt_cur2)
        mk.wait("pe", "c_act", t_lastog)
        for oc in range(NCH):
            for k in range(NCH):
                mk.op("pe", lambda e, oc=oc, k=k, w_cur2=w_cur2: e.matmul(
                    out=ps_pp[:, 0:512], lhsT=w_cur2[:, k * 1024 + oc * 128: k * 1024 + oc * 128 + 128],
                    rhs=og[:, k * 512:(k + 1) * 512],
                    start=(k == 0), stop=(k == NCH - 1)), "c_pe" if k == NCH - 1 else None)
            mk.wait("act", "c_pe", mk.cnt["c_pe"])
            if oc > 0:
                mk.wait("act", "q_gp", mk.cnt["q_gp"])
            tgt = mk.op("act", lambda e, oc=oc: e.activation(
                out=ascr2[:, 0:512], in_=ps_pp[:, 0:512], func=AF.Copy), "c_act")
            mk.wait("gp", "c_act", tgt)
            mk.dma("gp", scr["aout"][oc * 128:(oc + 1) * 128, tb:tb + 512], ascr2[:, 0:512], "q_gp")
            mk.wait("pe", "c_act", tgt)
        mark_used(wi_cur2)
        mk.wait("gp", "q_gp", mk.cnt["q_gp"])
        mk.wait("act", "q_gp", mk.cnt["q_gp"])
        mk.wait("dve", "q_gp", mk.cnt["q_gp"])
    return

def emit_attna_chunk(mk, nc, name, n, L, prev):
    S = mk.sb
    qs, ks, vs = L["qs"], L["ks"], L["vs"]
    gam, cb, bt, gfm, og, st = L["gam"], L["cb"], L["bt"], L["gfm"], L["og"], L["st"]
    gbd, kqb, pkt2 = L["gbd"], L["kqb"], L["pkt"]
    wht, kth2, wth = L["wht"], L["kth"], L["wth"]

    osb, osq, ored, r2f, ogp, wfm = L["osb"], L["osq"], L["ored"], L["r2f"], L["ogp"], L["wfm"]
    ascr, ascr2 = L["ascr"], L["ascr2"]
    gc8, erow = L["gc8"], L["erow"]
    mdiag, mdiag2, lgb, lbb = L["mdiag"], L["mdiag2"], L["lgb"], L["lbb"]
    ident = L["ident"]
    ps_pp, ps_g, ps_tr, ps_tr2, ps_acc, ps_o = L["ps_pp"], L["ps_g"], L["ps_tr"], L["ps_tr2"], L["ps_acc"], L["ps_o"]
    ps_s = L["ps_s"]
    c0 = n * 16
    nb = n * 128
    par = n % 2
    kth = kth2[:, par * 1024:(par + 1) * 1024]
    pkt = pkt2[:, par * 512:(par + 1) * 512]
    p0f = ascr2[:, 512 + par * 128: 640 + par * 128]
    p0t = ps_tr[:, 512 + par * 128: 640 + par * 128]
    kc = ks[:, nb:nb + 128]; qc = qs[:, nb:nb + 128]
    btc = bt[:, nb:nb + 128]; gmc = gam[:, nb:nb + 128]; gfc = gfm[:, nb:nb + 128]

    def pv(buf, w=16, off=0):
        return buf.rearrange("p (a b) -> p a b", b=512)[:, :, c0 + off: c0 + off + w]

    mk.wait("pe", "c_dve", mk.cnt["c_dve"])
    mk.wait("pe", "c_act", mk.cnt["c_act"])
    for h in range(NCH):
        mk.op("pe", lambda e, h=h: e.matmul(
            out=ps_pp[:, h * 32: h * 32 + 16], lhsT=st[:, h * 128:(h + 1) * 128],
            rhs=kc[:, h * 16:(h + 1) * 16], start=True, stop=True))
        mk.op("pe", lambda e, h=h: e.matmul(
            out=ps_pp[:, h * 32 + 16: h * 32 + 32], lhsT=st[:, h * 128:(h + 1) * 128],
            rhs=qc[:, h * 16:(h + 1) * 16], start=True, stop=True))
    mk.op("pe", lambda e: e.matmul(out=ps_g[:, 0:128], lhsT=kc, rhs=kc,
          start=True, stop=True))
    mk.op("pe", lambda e: e.matmul(out=ps_g[:, 128:256], lhsT=kc, rhs=qc,
          start=True, stop=True), "c_pe")
    tgt_pg = mk.cnt["c_pe"]
    mk.wait("dve", "c_pe", tgt_pg)
    ppv = ps_pp[:, 0:256].rearrange("p (a b) -> p a b", b=32)
    mk.op("dve", lambda e: e.tensor_tensor(
        out=r2f.rearrange("p (a b) -> p a b", b=16), in0=pv(cb),
        in1=ppv[:, :, 0:16], op=ALU.subtract))
    mk.op("dve", lambda e: e.tensor_tensor(out=gbd[:, :], in0=ps_g[:, 0:128], in1=mdiag[:, :], op=ALU.mult))
    mk.op("dve", lambda e: e.tensor_tensor(
        out=p0f.rearrange("p (a b) -> p a b", b=16),
        in0=gmc.rearrange("p (a b) -> p a b", b=16),
        in1=ppv[:, :, 16:32], op=ALU.mult))
    tgt_gb = mk.op("dve", lambda e: e.tensor_tensor(out=kqb[:, :], in0=ps_g[:, 128:256], in1=mdiag2[:, :],
          op=ALU.mult), "c_dve")
    mk.wait("pe", "c_dve", tgt_gb)
    mk.op("pe", lambda e: e.transpose(out=ps_tr[:, 0:128], in_=r2f[:, :], identity=ident[:, :]))
    mk.op("pe", lambda e: e.transpose(out=ps_tr[:, 128:256], in_=btc, identity=ident[:, :]))
    mk.op("pe", lambda e: e.transpose(out=ps_tr[:, 256:384], in_=gmc, identity=ident[:, :]))
    mk.op("pe", lambda e: e.transpose(out=ps_tr[:, 384:512], in_=gfc, identity=ident[:, :]))
    tgt_p0t = mk.op("pe", lambda e: e.transpose(out=p0t, in_=p0f, identity=ident[:, :]), "c_pe")
    if n == 0 or KTH_PRECHAIN:
        for h in range(NCH):
            mk.op("pe", lambda e, h=h: e.transpose(
                out=ps_tr2[0:16, h * 128:(h + 1) * 128], in_=kc[:, h * 16:(h + 1) * 16],
                identity=ident[:, :]), "c_pe" if h == NCH - 1 else None)
    tgt_tr2 = mk.cnt["c_pe"]
    mk.wait("act", "c_pe", tgt_tr2)
    tgt_pk = mk.op("act", lambda e: e.activation(out=pkt[:, :], in_=ps_tr[:, 0:512], func=AF.Copy), "c_act")
    if n == 0 or KTH_PRECHAIN:
        tgt_pk = mk.op("act", lambda e: e.activation(out=kth[0:16, :], in_=ps_tr2[0:16, 0:1024], func=AF.Copy), "c_act")
    r2t = pkt[:, 0:128]; btt = pkt[:, 128:256]; gmt = pkt[:, 256:384]; gpk = pkt[:, 384:512]
    if prev is not None:
        ppar = prev["par"]
        gmt_p = pkt2[:, ppar * 512 + 256: ppar * 512 + 384]
        gpk_p = pkt2[:, ppar * 512 + 384: ppar * 512 + 512]
        p0t_p = ps_tr[:, 512 + ppar * 128: 640 + ppar * 128]

    def filler(k):
        if prev is None:
            return
        if k == 0:
            mk.op("dve", lambda e: e.tensor_tensor(out=osb[:, :], in0=ps_o[:, 0:128], in1=gmt_p, op=ALU.mult))
        elif k == 1:
            mk.wait("act", "c_pe", prev["p0t_tag"])
            L["t_o3"] = mk.op("act", lambda e: e.activation(
                out=ascr2[:, 256:384], in_=p0t_p, func=AF.Copy), "c_act")
        elif k == 2:
            mk.wait("dve", "c_act", L["t_o3"])
            mk.op("dve", lambda e: e.tensor_tensor(out=osb[:, :], in0=osb[:, :], in1=ascr2[:, 256:384], op=ALU.add))
        elif k == 3:
            mk.op("dve", lambda e: e.reduce_sum(out=ored[:, 0:1], in_=osb[:, :], axis=AX.X))
            mk.op("dve", lambda e: e.drain())
        elif k == 4:
            mk.op("dve", lambda e: e.tensor_scalar(out=ored[:, 1:2], in0=ored[:, 0:1],
                  scalar1=1.0 / 128, scalar2=None, op0=ALU.mult))
            mk.op("dve", lambda e: e.drain())
        elif k == 5:
            L["t_m"] = mk.op("dve", lambda e: e.tensor_scalar(out=osb[:, :], in0=osb[:, :],
                  scalar1=ored[:, 1:2], scalar2=None, op0=ALU.subtract), "c_dve")
        elif k == 6:
            mk.wait("act", "c_dve", L["t_m"])
            L["t_sq"] = mk.op("act", lambda e: e.activation(out=osq[:, :], in_=osb[:, :], func=AF.Square), "c_act")
        elif k == 8:
            mk.wait("dve", "c_act", L["t_sq"])
            mk.op("dve", lambda e: e.reduce_sum(out=ored[:, 2:3], in_=osq[:, :], axis=AX.X))
            mk.op("dve", lambda e: e.drain())
        elif k == 9:
            L["t_v"] = mk.op("dve", lambda e: e.tensor_scalar(out=ored[:, 5:6], in0=ored[:, 2:3],
                  scalar1=1.0 / 128, scalar2=None, op0=ALU.mult), "c_dve")
        elif k == 10:
            mk.wait("act", "c_dve", L["t_v"])
            mk.op("act", lambda e: e.activation(out=ored[:, 6:7], in_=ored[:, 5:6], func=AF.Ln, bias=EPS, scale=1.0))
            L["t_r"] = mk.op("act", lambda e: e.activation(out=ored[:, 6:7], in_=ored[:, 6:7], func=AF.Exp,
                  bias=0.0, scale=-0.5), "c_act")
        elif k == 12:
            mk.wait("dve", "c_act", L["t_r"])
            mk.op("dve", lambda e: e.tensor_scalar(out=osb[:, :], in0=osb[:, :], scalar1=ored[:, 6:7],
                  scalar2=None, op0=ALU.mult))
        elif k == 13:
            mk.op("dve", lambda e: e.tensor_tensor(out=osb[:, :], in0=osb[:, :], in1=lgb[:, :], op=ALU.mult))
            mk.op("dve", lambda e: e.tensor_tensor(out=osb[:, :], in0=osb[:, :], in1=lbb[:, :], op=ALU.add))
        elif k == 14:
            L["t_og"] = mk.op("dve", lambda e, gpk_p=gpk_p: e.tensor_tensor(
                out=ogp[:, :], in0=osb[:, :], in1=gpk_p, op=ALU.mult), "c_dve")

    mk.wait("pe", "c_act", tgt_pk)
    for k in range(16):
        pc = (k % 2) * 128
        mk.op("pe", lambda e, pc=pc: e.matmul(
            out=ps_acc[:, pc:pc + 128], lhsT=ident[:, :], rhs=r2t,
            start=True, stop=(k == 0)), None if k > 0 else "c_pe")
        if k > 0:
            mk.op("pe", lambda e, pc=pc: e.matmul(
                out=ps_acc[:, pc:pc + 128], lhsT=gbd[:, :], rhs=wht[:, :],
                start=False, stop=True), "c_pe")
        tgt_mm = mk.cnt["c_pe"]
        mk.wait("dve", "c_pe", tgt_mm)
        tgt_st = mk.op("dve", lambda e, pc=pc: e.tensor_tensor(
            out=wht[:, :], in0=btt, in1=ps_acc[:, pc:pc + 128], op=ALU.mult), "c_dve")
        filler(k)
        if (not KTH_PRECHAIN) and n < 31 and 2 <= k <= 9:
            h = k - 2
            kcn = ks[:, (nb + 128) + h * 16: (nb + 128) + (h + 1) * 16]
            L["t_ktr"] = mk.op("pe", lambda e, h=h, kcn=kcn: e.transpose(
                out=ps_tr2[0:16, h * 128:(h + 1) * 128], in_=kcn,
                identity=ident[:, :]), "c_pe")
        if (not KTH_PRECHAIN) and n < 31 and k == 10:
            mk.wait("act", "c_pe", L["t_ktr"])
            L["t_kev"] = mk.op("act", lambda e: e.activation(
                out=kth2[0:16, (1 - par) * 1024:(2 - par) * 1024],
                in_=ps_tr2[0:16, 0:1024], func=AF.Copy), "c_act")
        mk.wait("pe", "c_dve", tgt_st)
    if prev is not None:
        mk.wait("pe", "c_dve", L["t_og"])
        if (not KTH_PRECHAIN) and n < 31:
            mk.wait("pe", "c_act", L["t_kev"])
        t_ogt = mk.op("pe", lambda e: e.transpose(out=ps_tr2[:, 896:1024],
              in_=ogp[:, :], identity=ident[:, :]), "c_pe")
        mk.wait("act", "c_pe", t_ogt)
        pc0 = prev["c0"]
        mk.op("act", lambda e, pc0=pc0: e.activation(
            out=og.rearrange("p (a b) -> p a b", b=512)[:, :, pc0:pc0 + 16],
            in_=ps_tr2[:, 896:1024].rearrange("p (a b) -> p a b", b=16),
            func=AF.Copy), "c_act")
    mk.op("pe", lambda e: e.transpose(out=ps_tr[:, 768:896], in_=wht[:, :], identity=ident[:, :]), "c_pe")
    mk.wait("act", "c_pe", mk.cnt["c_pe"])
    tgt_wf = mk.op("act", lambda e: e.activation(out=wfm[:, :], in_=ps_tr[:, 768:896], func=AF.Copy), "c_act")
    mk.wait("pe", "c_act", tgt_wf)
    for h in range(NCH):
        mk.op("pe", lambda e, h=h: e.transpose(
            out=ps_tr2[0:16, h * 128:(h + 1) * 128], in_=wfm[:, h * 16:(h + 1) * 16],
            identity=ident[:, :]), "c_pe" if h == NCH - 1 else None)
    mk.wait("act", "c_pe", mk.cnt["c_pe"])
    tgt_wt = mk.op("act", lambda e: e.activation(out=wth[0:16, :], in_=ps_tr2[0:16, 0:1024], func=AF.Copy), "c_act")
    mk.wait("pe", "c_act", tgt_wt)
    mk.op("pe", lambda e: e.transpose(
        out=ps_tr[0:8, 896:1024], in_=gam[:, nb + 15:nb + 128:16],
        identity=ident[:, :]), "c_pe")
    mk.wait("act", "c_pe", mk.cnt["c_pe"])
    tgt_g8 = mk.op("act", lambda e: e.activation(out=gc8[0:8, :], in_=ps_tr[0:8, 896:1024], func=AF.Copy), "c_act")
    mk.wait("pe", "c_act", tgt_g8)
    for h in range(NCH):
        mk.op("pe", lambda e, h=h: e.matmul(
            out=ps_s[h // 4][:, (h % 4) * 128:(h % 4 + 1) * 128], lhsT=kth[0:16, h * 128:(h + 1) * 128],
            rhs=wth[0:16, h * 128:(h + 1) * 128], start=True, stop=True))
    for h in range(NCH):
        psg = ps_acc if h < 4 else ps_o
        mk.op("pe", lambda e, h=h, psg=psg: e.matmul(
            out=psg[:, (h % 4) * 128:(h % 4 + 1) * 128], lhsT=erow[0:8, h * 128:(h + 1) * 128],
            rhs=gc8[0:8, :], start=True, stop=True), "c_pe" if h == NCH - 1 else None)
    tgt_su = mk.cnt["c_pe"]
    mk.wait("dve", "c_pe", tgt_su)
    for g2 in range(2):
        psg = ps_acc if g2 == 0 else ps_o
        mk.op("dve", lambda e, g2=g2: e.tensor_tensor(
            out=ascr[:, 0:512], in0=st[:, g2 * 512:(g2 + 1) * 512], in1=ps_s[g2][:, 0:512], op=ALU.add))
        mk.op("dve", lambda e, g2=g2, psg=psg: e.tensor_tensor(
            out=st[:, g2 * 512:(g2 + 1) * 512], in0=ascr[:, 0:512], in1=psg[:, 0:512], op=ALU.mult),
            "c_dve" if g2 == 1 else None)
    tgt_stup = mk.cnt["c_dve"]
    mk.wait("pe", "c_dve", tgt_stup)
    tgt_omm = mk.op("pe", lambda e: e.matmul(out=ps_o[:, 0:128], lhsT=kqb[:, :], rhs=wht[:, :],
          start=True, stop=True), "c_pe")
    return {"par": par, "c0": c0, "p0t_tag": tgt_p0t, "omm_tag": tgt_omm}


def emit_attna_partO(mk, name, L, prev):
    S = mk.sb
    og = L["og"]
    osb, osq, ored, ogp = L["osb"], L["osq"], L["ored"], L["ogp"]
    ascr2, lgb, lbb = L["ascr2"], L["lgb"], L["lbb"]
    ident = L["ident"]
    ps_tr, ps_acc, ps_o = L["ps_tr"], L["ps_acc"], L["ps_o"]
    pkt2 = L["pkt"]
    ppar = prev["par"]
    gmt_p = pkt2[:, ppar * 512 + 256: ppar * 512 + 384]
    gpk_p = pkt2[:, ppar * 512 + 384: ppar * 512 + 512]
    p0t_p = ps_tr[:, 512 + ppar * 128: 640 + ppar * 128]
    mk.wait("dve", "c_pe", prev["omm_tag"])
    mk.op("dve", lambda e: e.tensor_tensor(out=osb[:, :], in0=ps_o[:, 0:128], in1=gmt_p, op=ALU.mult))
    mk.wait("act", "c_pe", prev["p0t_tag"])
    t_o3 = mk.op("act", lambda e: e.activation(out=ascr2[:, 256:384], in_=p0t_p, func=AF.Copy), "c_act")
    mk.wait("dve", "c_act", t_o3)
    mk.op("dve", lambda e: e.tensor_tensor(out=osb[:, :], in0=osb[:, :], in1=ascr2[:, 256:384], op=ALU.add))
    mk.op("dve", lambda e: e.reduce_sum(out=ored[:, 0:1], in_=osb[:, :], axis=AX.X))
    mk.op("dve", lambda e: e.drain())
    mk.op("dve", lambda e: e.tensor_scalar(out=ored[:, 1:2], in0=ored[:, 0:1],
          scalar1=1.0 / 128, scalar2=None, op0=ALU.mult))
    mk.op("dve", lambda e: e.drain())
    t_m = mk.op("dve", lambda e: e.tensor_scalar(out=osb[:, :], in0=osb[:, :],
          scalar1=ored[:, 1:2], scalar2=None, op0=ALU.subtract), "c_dve")
    mk.wait("act", "c_dve", t_m)
    t_sq = mk.op("act", lambda e: e.activation(out=osq[:, :], in_=osb[:, :], func=AF.Square), "c_act")
    mk.wait("dve", "c_act", t_sq)
    mk.op("dve", lambda e: e.reduce_sum(out=ored[:, 2:3], in_=osq[:, :], axis=AX.X))
    mk.op("dve", lambda e: e.drain())
    t_v = mk.op("dve", lambda e: e.tensor_scalar(out=ored[:, 5:6], in0=ored[:, 2:3], scalar1=1.0 / 128,
          scalar2=None, op0=ALU.mult), "c_dve")
    mk.wait("act", "c_dve", t_v)
    mk.op("act", lambda e: e.activation(out=ored[:, 6:7], in_=ored[:, 5:6], func=AF.Ln, bias=EPS, scale=1.0))
    t_r = mk.op("act", lambda e: e.activation(out=ored[:, 6:7], in_=ored[:, 6:7], func=AF.Exp,
          bias=0.0, scale=-0.5), "c_act")
    mk.wait("dve", "c_act", t_r)
    mk.op("dve", lambda e: e.tensor_scalar(out=osb[:, :], in0=osb[:, :], scalar1=ored[:, 6:7],
          scalar2=None, op0=ALU.mult))
    mk.op("dve", lambda e: e.tensor_tensor(out=osb[:, :], in0=osb[:, :], in1=lgb[:, :], op=ALU.mult))
    mk.op("dve", lambda e: e.tensor_tensor(out=osb[:, :], in0=osb[:, :], in1=lbb[:, :], op=ALU.add))
    t_og = mk.op("dve", lambda e: e.tensor_tensor(out=ogp[:, :], in0=osb[:, :], in1=gpk_p, op=ALU.mult), "c_dve")
    mk.wait("pe", "c_dve", t_og)
    ps_tr2 = L["ps_tr2"]
    t_ogt = mk.op("pe", lambda e: e.transpose(out=ps_tr2[:, 896:1024],
          in_=ogp[:, :], identity=ident[:, :]), "c_pe")
    mk.wait("act", "c_pe", t_ogt)
    pc0 = prev["c0"]
    t_oge = mk.op("act", lambda e, pc0=pc0: e.activation(
        out=og.rearrange("p (a b) -> p a b", b=512)[:, :, pc0:pc0 + 16],
        in_=ps_tr2[:, 896:1024].rearrange("p (a b) -> p a b", b=16),
        func=AF.Copy), "c_act")
    return t_oge


from concourse.bass_utils import run_bass_kernel_spmd
import ml_dtypes

_BF = ml_dtypes.bfloat16
_NC_CACHE = []


def _build_full():
    nc = bass.Bass("TRN2", target_bir_lowering=False, detect_race_conditions=False)
    es = ExitStack()
    with es:
        mk = MK(nc, es)
        xT = mk.din("xT", [1024, TFULL], F32)
        xe = mk.din("xe", [1024, TEXT], F32)
        selc = mk.din("selc", [128, 2], F32)
        for w in ("waT", "wqT", "wkT", "wvT", "wgT", "woT"):
            mk.din(w, [1024, 1024], F32)
        mk.din("wb_h", [128, 64], F32)
        mk.din("cvw_h", [128, 96], F32)
        mk.din("bia_h", [128, 32], F32)
        mk.din("bb1_h", [8, 1], F32)
        mk.din("lnr_h", [1, 256], F32)
        mk.din("mdiag_h", [128, 128], BF16)
        mk.din("mdiag2_h", [128, 128], BF16)
        mk.din("erow_h", [8, 1024], BF16)
        identbf = mk.din("identbf_h", [128, 128], BF16)
        mk.din("wupT", [1024, 4096], F32)
        mk.din("wdnT", [4096, 1024], F32)
        mk.din("pqT", [1024, 512], F32)
        mk.din("keysT", [64, 512], F32)
        mk.din("deT", [1024, 4096], F32)
        mk.din("ue", [4096, 1024], F32)
        for w in ("abqwT", "abkwT", "abvwT", "abowT"):
            mk.din(w, [1024, 1024], F32)
        for b in ("qb8", "kb8", "ob8"):
            mk.din(b, [128, 8], F32)
        mk.din("vb_row", [1, 1024], F32)
        maskc_h = mk.din("maskc", [128, 3 * 384], BF16)
        outT = mk.dout("outT", [1024, 2048], F32)
        aoutp = mk.dscratch("aout_s", [1024, 4352], BF16)
        beta_s = mk.dscratch("beta_s", [8, TFULL], F32)
        wupb = mk.dscratch("wupb", [1024, 4096], BF16)
        wdnb = mk.dscratch("wdnb", [4096, 1024], BF16)
        deb = mk.dscratch("deb", [1024, 4096], BF16)
        ueb = mk.dscratch("ueb", [4096, 1024], BF16)
        for s in ("q_sp", "q_gp", "c_pe", "c_act", "c_dve", "c_gp"):
            mk.new_sem(s)
        mk.sbuf("scr512a", [128, 512], F32)
        mk.sbuf("scr512b", [128, 512], F32)
        mk.sbuf("scr512c", [128, 512], F32)
        mk.sbuf("srow", [1, 2048], F32)
        oc_ = mk.sbuf("ones_col", [1, 128], F32)
        o1_ = mk.sbuf("ones128", [128, 1], F32)
        idb = mk.sbuf("identbf", [128, 128], BF16)
        sel = mk.sbuf("sel_s", [128, 2], F32)
        mk.init_consts([1e-5, -0.5])
        mk.op("dve", lambda e: e.memset(oc_[:, :], 1.0))
        mk.op("dve", lambda e: e.memset(o1_[:, :], 1.0))
        mk.dma("sp", idb[:, :], identbf, "q_sp")
        mk.dma("sp", sel[:, :], mk.dram["selc"], "q_sp")

        with ExitStack() as esA:
            def asb(nm, shape, dt):
                return esA.enter_context(nc.sbuf_tensor(nm, shape, dt))
            N = "aa"
            for nm, shape, dt in (
                ("_xab", [128, 8 * 256], F32), ("_xln", [128, 8 * 512], BF16),
                ("_pre", [128, 8 * 520], BF16),
                ("_qs", [128, 8 * 512], BF16), ("_ks", [128, 8 * 512], BF16),
                ("_vs", [128, 8 * 512], BF16), ("_gam", [128, 8 * 512], BF16),
                ("_ivg", [128, 8 * 512], BF16), ("_cb", [128, 8 * 512], BF16),
                ("_bt", [128, 8 * 512], BF16), ("_bbc", [128, 8 * 512], BF16),
                ("_gfm", [128, 8 * 512], BF16), ("_og", [128, 8 * 512], BF16),
                ("_acp", [128, 8 * 512], BF16),
                ("_afm", [128, 8 * 512], F32), ("_lam", [128, 8 * 256], F32),
                ("_beta8", [8, 512], F32), ("_st", [128, 8 * 128], BF16),
                ("_wbuf", [128, 2 * 8192], BF16), ("_wb8", [128, 64], BF16),
                ("_cvw", [128, 96], F32), ("_bia", [128, 32], F32),
                ("_bb1", [8, 1], F32), ("_lgb", [128, 128], F32),
                ("_lbb", [128, 128], F32), ("_lgr", [1, 256], F32),
                ("_gbd", [128, 128], BF16), ("_kqb", [128, 128], BF16),
                ("_wht", [128, 128], BF16),
                ("_r2f", [128, 128], BF16), ("_ogp", [128, 128], BF16),
                ("_wfm", [128, 128], BF16), ("_mdiag", [128, 128], BF16),
                ("_mdiag2", [128, 128], BF16), ("_pkt", [128, 1024], BF16),
                ("_kth", [16, 2048], BF16), ("_wth", [16, 1024], BF16),
                ("_osb", [128, 128], F32), ("_osq", [128, 128], F32),
                ("_ored", [128, 8], F32), ("_ascr", [128, 512], F32),
                ("_ascr2", [128, 768], BF16), ("_zbuf", [128, 256], F32),
                ("_gc8", [8, 128], BF16), ("_erow", [8, 1024], BF16),
                ("_lamc", [128, 8], F32), ("_tls", [128, 72], BF16),
            ):
                mk.sb[N + nm] = asb(N + nm, shape, dt)
            mk.barrier()
            az = mk.sb[N + "_ascr2"]
            tz = mk.op("dve", lambda e: e.memset(az[:, 0:256], 0.0), "c_dve")
            mk.wait("gp", "c_dve", tz)
            for c in range(NCH):
                mk.dma("gp", aoutp[c * 128:(c + 1) * 128, 0:256], az[:, 0:256], "q_gp")
            mk.wait("dve", "q_gp", mk.cnt["q_gp"])
            hbm_a = {"waT": mk.dram["waT"], "wqT": mk.dram["wqT"], "wkT": mk.dram["wkT"],
                     "wvT": mk.dram["wvT"], "wgT": mk.dram["wgT"], "woT": mk.dram["woT"],
                     "wb": mk.dram["wb_h"], "cvw": mk.dram["cvw_h"], "bia": mk.dram["bia_h"],
                     "bb1": mk.dram["bb1_h"], "lnr": mk.dram["lnr_h"], "mdiag": mk.dram["mdiag_h"],
                     "mdiag2": mk.dram["mdiag2_h"], "erow": mk.dram["erow_h"]}
            emit_attna(mk, TFULL, xT, hbm_a, {"aout": aoutp[:, 256:4352], "beta": beta_s}, esA, N)
            mk.barrier()

        with ExitStack() as esB:
            resid = esB.enter_context(nc.sbuf_tensor("resid", [128, NCH * TEXT], F32))
            xln2 = esB.enter_context(nc.sbuf_tensor("xln2", [128, NCH * TEXT], BF16))
            arena = esB.enter_context(nc.sbuf_tensor("arena", [128, 21248], F32))
            mk.sb["resid"] = resid
            mk.sb["xln2"] = xln2
            mk.sb["arena"] = arena
            pkeys = esB.enter_context(nc.sbuf_tensor("peer_keys", [128, 512], BF16))
            mk.sb["peer_keys"] = pkeys
            mk.view("ffn_wup", arena, 0, NCH * 2048, BF16)
            mk.view("ffn_wdn", arena, 8192, 16 * 1024, BF16)
            mk.view("ffn_mid", arena, 16384, 16 * 256, BF16)
            o = [0]
            def V(nm, cols, dt):
                mk.view(nm, arena, o[0], cols, dt)
                o[0] += (cols if dt == F32 else (cols + 1) // 2)
                o[0] = (o[0] + 1) & ~1
            nt = (TEXT + 127) // 128
            V("peer_qfm", 8 * 512, BF16)
            V("peer_sims", 512, F32)
            V("peer_exa", nt * 256, BF16)
            V("peer_eya", nt * 256, BF16)
            V("peer_tau", nt * 4, F32)
            V("peer_srow", 64, F32)
            V("peer_s8", 64, F32)
            V("peer_det", 8 * 1024, BF16)
            V("peer_ue", 2 * 1024, BF16)
            V("peer_dsb", 1024, BF16)
            V("peer_dsb2", 1024, BF16)
            V("peer_pds", 1024, F32)
            V("peer_pd2", 1024, F32)
            V("peer_wtmp", 1024, BF16)
            V("peer_wacc", 1024, BF16)
            wtw_off = o[0]
            V("peer_wtw", 32 * 128 * 2, BF16)
            mk.view("peer_wq", arena, wtw_off, 8 * 512, BF16)
            assert o[0] <= 21248, o[0]
            o[0] = 0
            V("ab_wbuf", 8 * 1024, BF16)
            V("ab_kroll", 8 * 896, BF16)
            V("ab_vroll", 7 * 1024, BF16)
            V("ab_qbuf", 8 * 512, BF16)
            V("ab_att", 8 * 512, BF16)
            exps_off = o[0]
            V("ab_exps", 4 * 384, F32)
            V("ab_pbf", 4 * 384, BF16)
            V("ab_pbft", 2 * 384, BF16)
            V("ab_red", 16, F32)
            V("ab_bq", 8, F32); V("ab_bk", 8, F32); V("ab_bo", 8, F32)
            V("ab_vbb", 1024, BF16)
            V("ab_mkc", 3 * 384, BF16)
            mk.sb["ab_vbr"] = arena[0:1, exps_off:exps_off + 1024]
            assert o[0] <= 21248, o[0]
            for k in list(mk.sb):
                if k.startswith("peer_"):
                    mk.sb[k.replace("peer_", "peer0_", 1)] = mk.sb[k]
                    mk.sb[k.replace("peer_", "peer1_", 1)] = mk.sb[k]
                if k.startswith("ffn_"):
                    mk.sb[k.replace("ffn_", "ffn0_", 1)] = mk.sb[k]
                    mk.sb[k.replace("ffn_", "ffn1_", 1)] = mk.sb[k]

            stg = [arena[:, 0:2048].bitcast(BF16), arena[:, 2048:4096].bitcast(BF16)]
            outtag = [0, 0]
            for srcT, dstT, nch_, cols in ((mk.dram["wupT"], wupb, 8, 4096),
                                           (mk.dram["wdnT"], wdnb, 32, 1024),
                                           (mk.dram["deT"], deb, 8, 4096),
                                           (mk.dram["ue"], ueb, 32, 1024)):
                for ci in range(nch_):
                    sl = ci % 2
                    mk.wait("gp", "q_sp", outtag[sl])
                    t_in = mk.dma("gp", stg[sl][:, 0:cols], srcT[ci * 128:(ci + 1) * 128, :], "q_gp")
                    mk.wait("sp", "q_gp", t_in)
                    outtag[sl] = mk.dma("sp", dstT[ci * 128:(ci + 1) * 128, :], stg[sl][:, 0:cols], "q_sp")
            mk.wait("gp", "q_sp", mk.cnt["q_sp"])
            mk.wait("sp", "q_sp", mk.cnt["q_sp"])
            rc = lambda c: resid[:, c * TEXT:(c + 1) * TEXT]
            for c in range(NCH):
                mk.dma("sp", rc(c), xe[c * 128:(c + 1) * 128, :], "q_sp")
            xet = mk.cnt["q_sp"]
            for half, (c0_, c1_) in enumerate(((0, TEXT), (2048, 2048 + TEXT))):
                for c in range(NCH):
                    mk.dma("gp", xln2[:, c * TEXT:(c + 1) * TEXT],
                           aoutp[c * 128:(c + 1) * 128, c0_:c1_], "q_gp")
                agt = mk.cnt["q_gp"]
                if half == 0:
                    mk.wait("dve", "q_sp", xet)
                mk.wait("dve", "q_gp", agt)
                tsl = None
                for c in range(NCH):
                    tsl = mk.op("dve", lambda e, c=c, half=half: e.scalar_tensor_tensor(
                        out=rc(c), in0=xln2[:, c * TEXT:(c + 1) * TEXT],
                        scalar=sel[:, half:half + 1], in1=rc(c),
                        op0=ALU.mult, op1=ALU.add), "c_dve" if c == NCH - 1 else None)
                mk.wait("gp", "c_dve", tsl)
            mk.barrier()
            hbm_p = {"wqT": mk.dram["pqT"], "keysT": mk.dram["keysT"],
                     "deT": deb, "ue": ueb}
            hbm_b = {"qwT": mk.dram["abqwT"], "kwT": mk.dram["abkwT"], "vwT": mk.dram["abvwT"],
                     "owT": mk.dram["abowT"], "qb8": mk.dram["qb8"], "kb8": mk.dram["kb8"],
                     "ob8": mk.dram["ob8"], "vb_row": mk.dram["vb_row"]}
            for sub in range(2):
                with ExitStack() as esl:
                    emit_layernorm_fm(mk, TEXT, rc, xln2, esl, "lnf%d" % sub)
                mk.barrier()
                with ExitStack() as esf:
                    emit_ffn_dense(mk, TEXT, xln2, rc, wupb, wdnb, esf, "ffn%d" % sub)
                mk.barrier()
                with ExitStack() as esp:
                    emit_peer(mk, TEXT, xln2, rc, hbm_p, esp, "peer%d" % sub, GRP=2)
                mk.barrier()
                if sub == 0:
                    with ExitStack() as esl:
                        emit_layernorm_fm(mk, TEXT, rc, xln2, esl, "lnb")
                    mk.barrier()
                    with ExitStack() as esa:
                        emit_attnb(mk, TEXT, 256, xln2, rc, hbm_b, maskc_h, esa, "ab")
                    mk.barrier()
            for c in range(NCH):
                mk.dma("sp", outT[c * 128:(c + 1) * 128, :], resid[:, c * TEXT + 256:(c + 1) * TEXT], "q_sp")
            mk.wait("sp", "q_sp", mk.cnt["q_sp"])
            mk.barrier()
        mk.finalize()
    return nc


def _host_pack(inputs):
    i = {k: np.asarray(v) for k, v in inputs.items()}
    shared = {}
    shared["waT"] = np.ascontiguousarray(i["L0_attn_a_Wa"], np.float32)
    shared["wqT"] = np.ascontiguousarray(i["L0_attn_a_Wq"], np.float32)
    shared["wkT"] = np.ascontiguousarray(i["L0_attn_a_Wk"], np.float32)
    shared["wvT"] = np.ascontiguousarray(i["L0_attn_a_Wv"], np.float32)
    shared["wgT"] = np.ascontiguousarray(i["L0_attn_a_Wg"], np.float32)
    shared["woT"] = np.ascontiguousarray(i["L0_attn_a_Wo"], np.float32)
    wb_h = np.zeros((128, 64), np.float32)
    cvw_h = np.zeros((128, 96), np.float32)
    bia_h = np.zeros((128, 32), np.float32)
    Wb = np.asarray(i["L0_attn_a_Wb"], np.float32)
    for c in range(8):
        wb_h[:, c * 8:(c + 1) * 8] = Wb[c * 128:(c + 1) * 128, :]
        for ti, cw in enumerate(("cqw", "ckw", "cvw")):
            w = np.asarray(i["L0_attn_a_" + cw], np.float32)
            for j in range(4):
                cvw_h[:, ti * 32 + c * 4 + j] = w[c * 128:(c + 1) * 128, j]
        for g, bv in enumerate(("ba", "cqb", "ckb", "cvb")):
            b = np.asarray(i["L0_attn_a_" + bv], np.float32)
            bia_h[:, g * 8 + c] = b[c * 128:(c + 1) * 128]
    shared["wb_h"] = wb_h
    shared["cvw_h"] = cvw_h
    shared["bia_h"] = bia_h
    shared["bb1_h"] = np.ascontiguousarray(np.asarray(i["L0_attn_a_bb"], np.float32).reshape(8, 1))
    shared["lnr_h"] = np.concatenate([np.asarray(i["L0_attn_a_ln_g"], np.float32),
                                      np.asarray(i["L0_attn_a_ln_b"], np.float32)]).reshape(1, 256)
    md = np.zeros((128, 128), np.float32)
    md2 = np.zeros((128, 128), np.float32)
    for h in range(8):
        for a_ in range(16):
            for t in range(16):
                if a_ < t:
                    md[h * 16 + a_, h * 16 + t] = -1
                if a_ <= t:
                    md2[h * 16 + a_, h * 16 + t] = 1
    shared["mdiag_h"] = md.astype(_BF)
    shared["mdiag2_h"] = md2.astype(_BF)
    erow_h = np.zeros((8, 1024), np.float32)
    for h in range(8):
        erow_h[h, h * 128:(h + 1) * 128] = 1
    shared["erow_h"] = erow_h.astype(_BF)
    shared["identbf_h"] = np.eye(128, dtype=_BF)
    shared["wupT"] = np.ascontiguousarray(np.asarray(i["L0_ffn_W_up"], np.float32).T)
    shared["wdnT"] = np.ascontiguousarray(np.asarray(i["L0_ffn_W_down"], np.float32).T)
    shared["pqT"] = np.ascontiguousarray(np.asarray(i["L0_ffn_W_q"], np.float32).T)
    keys = np.asarray(i["L0_ffn_keys"], np.float32)
    shared["keysT"] = np.ascontiguousarray(keys.transpose(3, 2, 0, 1).reshape(64, 512))
    shared["deT"] = np.ascontiguousarray(np.asarray(i["L0_ffn_down_embed"], np.float32).T)
    shared["ue"] = np.ascontiguousarray(np.asarray(i["L0_ffn_up_embed"], np.float32))
    shared["abqwT"] = np.ascontiguousarray(np.asarray(i["L0_attn_b_q_w"], np.float32).T)
    shared["abkwT"] = np.ascontiguousarray(np.asarray(i["L0_attn_b_k_w"], np.float32).T)
    shared["abvwT"] = np.ascontiguousarray(np.asarray(i["L0_attn_b_v_w"], np.float32).T)
    shared["abowT"] = np.ascontiguousarray(np.asarray(i["L0_attn_b_o_w"], np.float32).T)
    shared["qb8"] = np.ascontiguousarray(np.asarray(i["L0_attn_b_q_b"], np.float32).reshape(8, 128).T)
    shared["kb8"] = np.ascontiguousarray(np.asarray(i["L0_attn_b_k_b"], np.float32).reshape(8, 128).T)
    shared["ob8"] = np.ascontiguousarray(np.asarray(i["L0_attn_b_o_b"], np.float32).reshape(8, 128).T)
    shared["vb_row"] = np.ascontiguousarray(np.asarray(i["L0_attn_b_v_b"], np.float32).reshape(1, 1024))
    base = np.full((128, 384), -1e30, np.float32)
    for q in range(128):
        base[q, q:q + 257] = 0.0
    mk0 = []
    for qt in range(3):
        m = base.copy()
        cut = max(0, 256 - qt * 128)
        m[:, 0:cut] = -1e30
        mk0.append(m)
    maskc0 = np.concatenate(mk0, 1).astype(_BF)
    maskc1 = np.concatenate([base, base, base], 1).astype(_BF)
    x = np.asarray(i["L0_attn_a_x"], np.float32)
    in_maps = []
    for core in range(8):
        b, hf = core // 2, core % 2
        xbT = np.ascontiguousarray(x[b].T)
        if hf == 0:
            xe = np.concatenate([np.zeros((1024, 256), np.float32), xbT[:, 0:2048]], 1)
        else:
            xe = np.ascontiguousarray(xbT[:, 1792:4096])
        m = dict(shared)
        m["xT"] = xbT
        m["xe"] = xe
        m["selc"] = np.tile(np.array([[1.0 - hf, float(hf)]], np.float32), (128, 1))
        m["maskc"] = maskc0 if hf == 0 else maskc1
        in_maps.append(m)
    return in_maps


def kernel(**inputs):
    if not _NC_CACHE:
        _NC_CACHE.append(_build_full())
    nc = _NC_CACHE[0]
    in_maps = _host_pack(inputs)
    res = run_bass_kernel_spmd(nc, in_maps, core_ids=list(range(8)))
    out = np.zeros((4, 4096, 1024), np.float32)
    for core in range(8):
        b, hf = core // 2, core % 2
        out[b, hf * 2048:(hf + 1) * 2048, :] = np.asarray(res.results[core]["outT"]).T
    return out
